# version 21
# baseline (speedup 1.0000x reference)
import numpy as np
import ml_dtypes
import concourse.bass as bass
import concourse.mybir as mybir
from concourse.bass_utils import run_bass_kernel_spmd
from contextlib import ExitStack

F32 = mybir.dt.float32
BF16 = mybir.dt.bfloat16
U32 = mybir.dt.uint32
AF = mybir.ActivationFunctionType
ALU = mybir.AluOpType
AX = mybir.AxisListType
IOA = bass.IndirectOffsetOnAxis

NCORE = 8
T = 2048
E = 1024
H = 16
DH = 64
DR = 32
DQK = DH + DR
DKV = 256
NEXP = 16
CAP = 1024
FF = 4096
NT = T // 128
SCALE = 1.0 / float(np.sqrt(np.float64(DQK)))
EPS = 1e-5
BIGSLOT = 1.0e9


def M(*ds):
    out = {}
    for d in ds:
        for k, v in d.items():
            if v > out.get(k, 0):
                out[k] = v
    return out


class P:
    ENG = ("t", "v", "s", "g", "y")

    def __init__(self, nc, stack):
        self.nc = nc
        self.stack = stack
        self.sem = {e: stack.enter_context(nc.semaphore("sq_" + e))
                    for e in self.ENG}
        self.cnt = {e: 0 for e in self.ENG}
        self.ops = {e: [] for e in self.ENG}
        self.lastw = {e: {} for e in self.ENG}

    def chan(self, name):
        if name not in self.sem:
            self.sem[name] = self.stack.enter_context(
                self.nc.semaphore("ch_" + name))
            self.cnt[name] = 0
        return name

    def mark(self, *names):
        if names:
            return {n: self.cnt[n] for n in names}
        return dict(self.cnt)

    def _waits(self, eng, dep):
        if dep == "all":
            want = dict(self.cnt)
        elif dep is None:
            want = {}
        elif isinstance(dep, dict):
            want = {d: v for d, v in dep.items() if v > 0}
        else:
            want = {d: self.cnt[d] for d in dep}
        waits = []
        lw = self.lastw[eng]
        for d, v in want.items():
            if v > lw.get(d, 0):
                lw[d] = v
                waits.append((self.sem[d], v))
        return waits

    def op(self, eng, fn, dep="all", sem=None):
        name = eng if sem is None else self.chan_c(sem)
        waits = self._waits(eng, dep)
        self.cnt[name] += 1
        self.ops[eng].append((fn, waits, self.sem[name], 1))

    def chan_c(self, name):
        if name not in self.sem:
            self.sem[name] = self.stack.enter_context(
                self.nc.semaphore("cs_" + name))
            self.cnt[name] = 0
        return name

    def idma(self, eng, fn, dep="all", ch=None):
        ch = self.chan(ch if ch is not None else "qi_" + eng)
        waits = self._waits(eng, dep)
        v = self.cnt[ch]
        if v > self.lastw[eng].get(ch, 0):
            self.lastw[eng][ch] = v
            waits.append((self.sem[ch], v))
        self.cnt[ch] += 16
        self.ops[eng].append((fn, waits, self.sem[ch], 16))

    def dma(self, eng, out, in_, dep="all", ch=None, **kw):
        ch = self.chan(ch if ch is not None else "q_" + eng)
        waits = self._waits(eng, dep)
        v = self.cnt[ch]
        if v > self.lastw[eng].get(ch, 0):
            self.lastw[eng][ch] = v
            waits.append((self.sem[ch], v))
        self.cnt[ch] += 16
        self.ops[eng].append(
            (lambda e, o=out, i=in_, k=kw: e.dma_start(out=o, in_=i, **k),
             waits, self.sem[ch], 16))

    def build(self, block):
        deco = {"t": block.tensor, "v": block.vector, "s": block.scalar,
                "g": block.gpsimd, "y": block.sync}
        for eng in self.ENG:
            ops = self.ops[eng]
            if not ops:
                continue

            def body(e, ops=ops):
                for fn, waits, sem, inc in ops:
                    for h, v in waits:
                        e.wait_ge(h, v)
                    fn(e).then_inc(sem, inc)
            deco[eng](body)
        final = [(self.sem[e], c) for e, c in self.cnt.items() if c]

        def tail(e, fin=final):
            for h, v in fin:
                e.wait_ge(h, v)
        block.sync(tail)


def ln_pass(p, xt, scratch, vec, out_bf, dep, dep_last=None):
    ssq = vec[:, 0:1]
    s1 = vec[:, 1:2]
    mu = vec[:, 2:3]
    var = vec[:, 3:4]
    sd = vec[:, 4:5]
    rstd = vec[:, 5:6]
    nmr = vec[:, 6:7]
    p.op("s", lambda e: e.activation(scratch[:, :], xt, AF.Square,
                                     accum_out=ssq), dep=dep)
    sqm = p.cnt["s"]
    p.op("v", lambda e: e.reduce_sum(s1, xt, axis=AX.X), dep=dep)
    p.op("v", lambda e: e.tensor_scalar_mul(mu, s1, 1.0 / E),
         dep={"v": p.cnt["v"]})
    p.op("v", lambda e: e.tensor_scalar_mul(var, ssq, 1.0 / E),
         dep={"v": p.cnt["v"], "s": sqm})
    p.op("v", lambda e: e.tensor_mul(sd, mu, mu), dep={"v": p.cnt["v"]})
    p.op("v", lambda e: e.tensor_sub(var, var, sd), dep={"v": p.cnt["v"]})
    p.op("v", lambda e: e.tensor_scalar_add(var, var, float(EPS)),
         dep={"v": p.cnt["v"]})
    p.op("s", lambda e: e.activation(sd, var, AF.Ln),
         dep={"v": p.cnt["v"], "s": p.cnt["s"]})
    p.op("s", lambda e: e.activation(rstd, sd, AF.Exp, scale=-0.5),
         dep={"s": p.cnt["s"]})
    p.op("v", lambda e: e.scalar_tensor_tensor(nmr, mu, -1.0, rstd,
                                               ALU.mult, ALU.mult),
         dep={"v": p.cnt["v"], "s": p.cnt["s"]})
    fin = {"v": p.cnt["v"], "s": p.cnt["s"]}
    if dep_last:
        fin = M(dep_last, fin)
    p.op("v", lambda e: e.tensor_scalar(out_bf, xt, rstd, nmr,
                                        ALU.mult, ALU.add), dep=fin)
    return {"s": sqm, "v": p.cnt["v"]}


def ln_pair(p, xts, scrs, vecs, outs, deps, dep_lasts):
    st1 = [0, 0]
    st2 = [0, 0]
    ex = [None, None]
    for k in range(2):
        p.op("s", lambda e, k=k: e.activation(
            scrs[k][:, :], xts[k], AF.Square, accum_out=vecs[k][:, 0:1]),
            dep=deps[k])
        st1[k] = p.cnt["s"]
    for k in range(2):
        vec = vecs[k]
        ssq, s1, mu, var, sd, rstd, nmr = (vec[:, j:j + 1] for j in range(7))
        p.op("v", lambda e, k=k, s1=s1: e.reduce_sum(s1, xts[k], axis=AX.X),
             dep=M(deps[k], {"v": p.cnt["v"]}))
        p.op("v", lambda e, mu=mu, s1=s1: e.tensor_scalar_mul(
            mu, s1, 1.0 / E), dep={"v": p.cnt["v"]})
        p.op("v", lambda e, var=var, ssq=ssq: e.tensor_scalar_mul(
            var, ssq, 1.0 / E), dep={"v": p.cnt["v"], "s": st1[k]})
        p.op("v", lambda e, sd=sd, mu=mu: e.tensor_mul(sd, mu, mu),
             dep={"v": p.cnt["v"]})
        p.op("v", lambda e, var=var, sd=sd: e.tensor_sub(var, var, sd),
             dep={"v": p.cnt["v"]})
        p.op("v", lambda e, var=var: e.tensor_scalar_add(
            var, var, float(EPS)), dep={"v": p.cnt["v"]})
        vm = p.cnt["v"]
        p.op("s", lambda e, sd=sd, var=var: e.activation(sd, var, AF.Ln),
             dep={"v": vm, "s": p.cnt["s"]})
        p.op("s", lambda e, rstd=rstd, sd=sd: e.activation(
            rstd, sd, AF.Exp, scale=-0.5), dep={"s": p.cnt["s"]})
        st2[k] = p.cnt["s"]
    for k in range(2):
        vec = vecs[k]
        mu, rstd, nmr = vec[:, 2:3], vec[:, 5:6], vec[:, 6:7]
        p.op("v", lambda e, nmr=nmr, mu=mu, rstd=rstd: e.scalar_tensor_tensor(
            nmr, mu, -1.0, rstd, ALU.mult, ALU.mult),
            dep={"v": p.cnt["v"], "s": st2[k]})
        fin = M(dep_lasts[k], {"v": p.cnt["v"], "s": st2[k]})
        p.op("v", lambda e, k=k, rstd=rstd, nmr=nmr: e.tensor_scalar(
            outs[k], xts[k], rstd, nmr, ALU.mult, ALU.add), dep=fin)
        ex[k] = {"s": st1[k], "v": p.cnt["v"]}
    return ex


def build_L1():
    nc = bass.Bass("TRN2", target_bir_lowering=False, debug=False, num_swdge_queues=4)
    x_in = nc.dram_tensor("x", [T, E], F32, kind="ExternalInput").ap()
    wq_in = nc.dram_tensor("wq", [E, H * DH], F32, kind="ExternalInput").ap()
    wqr_in = nc.dram_tensor("wqr", [E, H * DR], F32, kind="ExternalInput").ap()
    wkr_in = nc.dram_tensor("wkr", [E, DR], F32, kind="ExternalInput").ap()
    wkvd_in = nc.dram_tensor("wkvd", [E, DKV], F32, kind="ExternalInput").ap()
    wku_in = nc.dram_tensor("wku", [DKV, H * DH], F32, kind="ExternalInput").ap()
    wvu_in = nc.dram_tensor("wvu", [DKV, H * DH], F32, kind="ExternalInput").ap()
    wo_in = nc.dram_tensor("wo", [E, E], F32, kind="ExternalInput").ap()
    lobo_in = nc.dram_tensor("lobo", [1, H], F32, kind="ExternalInput").ap()
    wg_in = nc.dram_tensor("wg", [E, NEXP], F32, kind="ExternalInput").ap()
    cosr_in = nc.dram_tensor("cosr", [T, 256], F32, kind="ExternalInput").ap()
    sinr_in = nc.dram_tensor("sinr", [T, 256], F32, kind="ExternalInput").ap()

    xp_out = nc.dram_tensor("xprime", [T, E], F32, kind="ExternalOutput").ap()
    x2_out = nc.dram_tensor("xln2", [T, E], BF16, kind="ExternalOutput").ap()
    idx_out = nc.dram_tensor("idxo", [T, 1], F32, kind="ExternalOutput").ap()
    gs_out = nc.dram_tensor("gso", [T, 1], F32, kind="ExternalOutput").ap()
    lloc_out = nc.dram_tensor("lloco", [NT, 128, NEXP], F32, kind="ExternalOutput").ap()
    cnt_out = nc.dram_tensor("cnto", [1, NEXP], F32, kind="ExternalOutput").ap()

    qhat = nc.dram_tensor("qhat", [H, 97, T], BF16, kind="Internal").ap()
    khat = nc.dram_tensor("khat", [H, 97, T], BF16, kind="Internal").ap()

    idbf = nc.inline_tensor(np.eye(128, dtype=ml_dtypes.bfloat16), "idbf").ap()
    idf = nc.inline_tensor(np.eye(128, dtype=np.float32), "idf").ap()
    m0np = np.where(np.arange(128)[None, :] > np.arange(128)[:, None],
                    np.float32(-1e30), np.float32(0.0)).astype(np.float32)
    m0 = nc.inline_tensor(m0np, "m0").ap()
    trilS = nc.inline_tensor(
        np.where(np.arange(128)[:, None] < np.arange(128)[None, :],
                 np.float32(1.0), np.float32(0.0)).astype(np.float32), "trilS").ap()
    onesf = nc.inline_tensor(np.ones((128, 128), np.float32), "onesf").ap()
    onescol = nc.inline_tensor(np.ones((128, 1), np.float32), "onescol").ap()
    iota16 = nc.inline_tensor(np.tile(np.arange(16, dtype=np.float32), (128, 1)),
                              "iota16").ap()
    iom99 = nc.inline_tensor(np.tile(np.arange(16, dtype=np.float32) - 99.0,
                                     (128, 1)), "iom99").ap()
    ones16 = nc.inline_tensor(np.ones((128, 16), np.float32), "ones16").ap()
    onesrow128 = nc.inline_tensor(np.ones((1, 128), np.float32), "or128i").ap()

    with ExitStack() as st:
        EE = st.enter_context
        p = P(nc, st)
        wkvd = EE(nc.sbuf_tensor("wkvd_s", [128, 8 * DKV], BF16))
        wku = EE(nc.sbuf_tensor("wku_s", [128, 2 * 1024], BF16))
        wvu = EE(nc.sbuf_tensor("wvu_s", [128, 2 * 1024], BF16))
        wkr = EE(nc.sbuf_tensor("wkr_s", [128, 8 * DR], BF16))
        wq = EE(nc.sbuf_tensor("wq_s", [128, 8 * 1024], BF16))
        wqr = EE(nc.sbuf_tensor("wqr_s", [128, 8 * 512], BF16))
        wg = EE(nc.sbuf_tensor("wg_s", [128, 8 * NEXP], BF16))
        vhat = [EE(nc.sbuf_tensor(f"vhat{t}", [128, H * 65], BF16))
                for t in range(NT)]
        idbf_sb = EE(nc.sbuf_tensor("idbf_sb", [128, 128], BF16))
        idf_sb = EE(nc.sbuf_tensor("idf_sb", [128, 128], F32))
        m0_sb = EE(nc.sbuf_tensor("m0_sb", [128, 128], F32))
        trilS_sb = EE(nc.sbuf_tensor("trilS_sb", [128, 128], F32))
        onesf_sb = EE(nc.sbuf_tensor("onesf_sb", [128, 128], F32))
        onescol_sb = EE(nc.sbuf_tensor("onescol_sb", [128, 1], F32))
        iota16_sb = EE(nc.sbuf_tensor("iota16_sb", [128, 16], F32))
        iom99_sb = EE(nc.sbuf_tensor("iom99_sb", [128, 16], F32))
        ones16_sb = EE(nc.sbuf_tensor("ones16_sb", [128, 16], F32))
        c_all = EE(nc.sbuf_tensor("c_all", [128, 16], F32))
        lobo_sb = EE(nc.sbuf_tensor("lobo_sb", [1, 16], F32))
        vec2 = [EE(nc.sbuf_tensor(f"vec{i}", [128, 16], F32)) for i in range(2)]
        scr2 = [EE(nc.sbuf_tensor(f"scr{i}", [128, 1024], F32)) for i in range(2)]

        for dst, src in ((idbf_sb, idbf), (idf_sb, idf), (m0_sb, m0),
                         (trilS_sb, trilS), (onesf_sb, onesf),
                         (onescol_sb, onescol), (iota16_sb, iota16),
                         (iom99_sb, iom99), (ones16_sb, ones16),
                         (lobo_sb, lobo_in)):
            p.dma("y", dst[:, :], src, ch="c0", dep=None)
        cm = p.mark("c0")

        p.dma("g", wkvd[:, :].rearrange("p (c n) -> p c n", n=DKV),
              wkvd_in.rearrange("(c p) n -> p c n", p=128), ch="w0", dep=None)
        p.dma("g", wku[:, :].rearrange("p (c n) -> p c n", n=1024),
              wku_in.rearrange("(c p) n -> p c n", p=128), ch="w1", dep=None)
        p.dma("g", wvu[:, :].rearrange("p (c n) -> p c n", n=1024),
              wvu_in.rearrange("(c p) n -> p c n", p=128), ch="w2", dep=None)
        p.dma("g", wkr[:, :].rearrange("p (c n) -> p c n", n=DR),
              wkr_in.rearrange("(c p) n -> p c n", p=128), ch="w0", dep=None)
        p.dma("g", wq[:, :].rearrange("p (c n) -> p c n", n=1024),
              wq_in.rearrange("(c p) n -> p c n", p=128), ch="w1", dep=None)
        p.dma("g", wqr[:, :].rearrange("p (c n) -> p c n", n=512),
              wqr_in.rearrange("(c p) n -> p c n", p=128), ch="w2", dep=None)
        p.dma("g", wg[:, :].rearrange("p (c n) -> p c n", n=NEXP),
              wg_in.rearrange("(c p) n -> p c n", p=128), ch="w0", dep=None)
        wm = p.mark("w0", "w1", "w2")
        p.op("s", lambda e: e.activation(wq[:, :], wq[:, :], AF.Copy,
                                         scale=SCALE), dep=wm)
        p.op("s", lambda e: e.activation(wqr[:, :], wqr[:, :], AF.Copy,
                                         scale=SCALE), dep=wm)
        wscale = p.mark("s")

        with nc.psum_tensor("psc", [128, 16], F32) as psc:
            p.op("s", lambda e: e.activation(lobo_sb[:, :], lobo_sb[:, :],
                                             AF.Exp), dep=cm)
            p.op("t", lambda e: e.matmul(psc[:, :], onesf_sb[0:1, :],
                                         lobo_sb[:, :], start=True, stop=True),
                 dep={"s": p.cnt["s"], **cm})
            p.op("v", lambda e: e.tensor_copy(c_all[:, :], psc[:, :]),
                 dep={"t": p.cnt["t"]})

        import os as _os3
        if _os3.environ.get("K_PHASE", "full") == "p0":
            with nc.Block() as block:
                p.build(block)
            return nc

        krfm = EE(nc.sbuf_tensor("krfm", [32, T], BF16))
        qrfm = [EE(nc.sbuf_tensor(f"qrfm{i}", [128, T], BF16))
                for i in range(4)]
        with ExitStack() as stA:
            EA = stA.enter_context
            xfm = [EA(nc.sbuf_tensor(f"xfm{c}", [128, T], BF16)) for c in range(8)]
            lat = [EA(nc.sbuf_tensor(f"lat{c}", [128, T], BF16)) for c in range(2)]
            cosr = EA(nc.sbuf_tensor("cosr_s", [128, NT * 256], F32))
            sinr = EA(nc.sbuf_tensor("sinr_s", [128, NT * 256], F32))
            xt2b = [EA(nc.sbuf_tensor(f"xt{i}", [128, 1024], F32))
                    for i in range(2)]
            x1t2 = [EA(nc.sbuf_tensor(f"x1t{i}", [128, 1024], BF16))
                    for i in range(2)]
            stg2b = [EA(nc.sbuf_tensor(f"stg{i}", [128, 512], BF16))
                     for i in range(2)]
            rop = EA(nc.sbuf_tensor("rop", [128, 512], F32))
            ropo = EA(nc.sbuf_tensor("ropo", [128, 512], BF16))
            krbf2 = [EA(nc.sbuf_tensor(f"krbf{i}", [128, 65], BF16))
                     for i in range(2)]

            pp2 = [EA(nc.psum_tensor(f"pp{i}", [128, 512], F32))
                   for i in range(4)]
            pb2 = [EA(nc.psum_tensor(f"pb{i}", [128, 128], BF16))
                   for i in range(2)]
            pk2 = [EA(nc.psum_tensor(f"pk{i}", [128, 32], F32))
                   for i in range(2)]

            p.dma("g", cosr[:, :].rearrange("p (t n) -> p t n", n=256),
                  cosr_in.rearrange("(t p) n -> p t n", p=128), ch="cs", dep=None)
            p.dma("g", sinr[:, :].rearrange("p (t n) -> p t n", n=256),
                  sinr_in.rearrange("(t p) n -> p t n", p=128), ch="cs", dep=None)
            csm = p.mark("cs")

            lnfree = [{}, {}]
            trmark = [0, 0]
            pbfree = [0, 0]
            for tp in range(0, NT, 2):
                ents = []
                for t in (tp, tp + 1):
                    i = t % 2
                    xch = f"x{i}"
                    p.dma("g", xt2b[i][:, :], x_in[t * 128:(t + 1) * 128, :],
                          ch=xch, dep=lnfree[i])
                    ents.append({xch: p.cnt[xch], **cm})
                exs = ln_pair(p, [xt2b[0][:, :], xt2b[1][:, :]],
                              scr2, vec2,
                              [x1t2[0][:, :], x1t2[1][:, :]], ents,
                              [{"t": trmark[0]}, {"t": trmark[1]}])
                for t in (tp, tp + 1):
                    i = t % 2
                    lnfree[i] = exs[i]
                    for c in range(8):
                        p.op("t", lambda e, c=c, i=i: e.transpose(
                            pb2[c % 2][:, :],
                            x1t2[i][:, c * 128:(c + 1) * 128],
                            idbf_sb[:, :]),
                            dep={"v": max(exs[i]["v"], pbfree[c % 2]), **cm})
                        tm = p.cnt["t"]
                        p.op("v", lambda e, c=c, t=t: e.tensor_copy(
                            xfm[c][:, t * 128:(t + 1) * 128],
                            pb2[c % 2][:, :]),
                            dep={"t": tm})
                        pbfree[c % 2] = p.cnt["v"]
                    trmark[i] = p.cnt["t"]
            xfm_done = {"v": p.cnt["v"]}

            if _os3.environ.get("K_PHASE", "full") == "p1":
                with nc.Block() as block:
                    p.build(block)
                return nc

            ppfree = [{}, {}, {}, {}]
            gj = [0]

            def proj(slices, evac_fn, extra=None, reader="s"):
                j = gj[0]
                gj[0] += 1
                bank = pp2[j % 4]
                for i, (l, r) in enumerate(slices):
                    dep = None
                    if i == 0:
                        dep = M(ppfree[j % 4], extra or {})
                    p.op("t", lambda e, l=l, r=r, i=i, n=len(slices), bank=bank:
                         e.matmul(bank[:, :], l, r, start=(i == 0),
                                  stop=(i == n - 1)), dep=dep)
                tm = p.cnt["t"]
                p.op(reader, lambda e, bank=bank: evac_fn(e, bank),
                     dep={"t": tm})
                ppfree[j % 4] = {reader: p.cnt[reader]}
                return p.cnt[reader]

            for B in range(4):
                for d in range(2):
                    sl = [(wkvd[:, c * DKV + d * 128: c * DKV + d * 128 + 128],
                           xfm[c][:, B * 512:(B + 1) * 512]) for c in range(8)]
                    proj(sl, lambda e, bank, B=B, d=d: e.activation(
                        lat[d][:, B * 512:(B + 1) * 512], bank[:, :], AF.Copy),
                        extra={**xfm_done, **wm})
            lat_done = {"s": p.cnt["s"]}

            if _os3.environ.get("K_PHASE", "full") == "p2a":
                with nc.Block() as block:
                    p.build(block)
                return nc

            stgfree = [{}, {}]
            kwi = [0]

            def head_rows(dst, B, o, src_is_q):
                j = gj[0]
                si = j % 2
                stg = stg2b[si]
                if src_is_q:
                    sl = [(wq[:, c * 1024 + o * 128: c * 1024 + o * 128 + 128],
                           xfm[c][:, B * 512:(B + 1) * 512]) for c in range(8)]
                    extra = {**xfm_done, **wscale}
                else:
                    sl = [(wku[:, c * 1024 + o * 128: c * 1024 + o * 128 + 128],
                           lat[c][:, B * 512:(B + 1) * 512]) for c in range(2)]
                    extra = {**lat_done, **wm}
                em = proj(sl, lambda e, bank, stg=stg: e.activation(
                    stg[:, :], bank[:, :], AF.Copy),
                    extra=M(extra, stgfree[si]))
                ch = f"kw{kwi[0] % 3}"
                kwi[0] += 1
                p.dma("g", dst[2 * o, 0:64, B * 512:(B + 1) * 512],
                      stg[0:64, :], ch=ch, dep={"s": em})
                p.dma("g", dst[2 * o + 1, 0:64, B * 512:(B + 1) * 512],
                      stg[64:128, :], ch=ch, dep=None)
                stgfree[si] = {ch: p.cnt[ch]}

            for B in range(4):
                for o in range(8):
                    head_rows(khat, B, o, False)
            for B in range(4):
                for o in range(8):
                    head_rows(qhat, B, o, True)

            if _os3.environ.get("K_PHASE", "full") == "p2c":
                with nc.Block() as block:
                    p.build(block)
                return nc

            _vdep = _os3.environ.get("K_P2D", "prec")
            for t in range(NT):
                vm = []
                for nh in range(2):
                    sl = [(lat[c][:, t * 128:(t + 1) * 128],
                           wvu[:, c * 1024 + nh * 512: c * 1024 + (nh + 1) * 512])
                          for c in range(2)]
                    if _vdep == "allall":
                        j = gj[0]; gj[0] += 1
                        bank = pp2[j % 2]
                        for i, (l, r) in enumerate(sl):
                            p.op("t", lambda e, l=l, r=r, i=i, bank=bank:
                                 e.matmul(bank[:, :], l, r, start=(i == 0),
                                          stop=(i == 1)), dep="all")
                        p.op("s", lambda e, bank=bank, t=t, nh=nh: e.activation(
                            vhat[t][:, :].rearrange("p (h q) -> p h q", q=65)
                            [:, nh * 8:(nh + 1) * 8, 0:64],
                            bank[:, :].rearrange("p (h d) -> p h d", d=64),
                            AF.Copy), dep="all")
                        ppfree[j % 2] = {"s": p.cnt["s"]}
                    else:
                        vm.append(proj(sl, lambda e, bank, t=t, nh=nh:
                                       e.activation(
                            vhat[t][:, :].rearrange("p (h q) -> p h q", q=65)
                            [:, nh * 8:(nh + 1) * 8, 0:64],
                            bank[:, :].rearrange("p (h d) -> p h d", d=64),
                            AF.Copy), extra={**lat_done, **wm}))
                if _os3.environ.get("K_NOMEMSET", "0") == "1":
                    pass
                elif _vdep == "allall":
                    p.op("v", lambda e, t=t: e.memset(
                        vhat[t][:, :].rearrange("p (h q) -> p h q", q=65)
                        [:, :, 64:65], 1.0), dep="all")
                else:
                    p.op("v", lambda e, t=t: e.memset(
                        vhat[t][:, :].rearrange("p (h q) -> p h q", q=65)
                        [:, :, 64:65], 1.0), dep=None)
            vhat_done = {"s": p.cnt["s"], "v": p.cnt["v"]}

            if _os3.environ.get("K_PHASE", "full") == "p2d":
                with nc.Block() as block:
                    p.build(block)
                return nc

            for i in range(2):
                p.op("v", lambda e, i=i: e.memset(krbf2[i][:, :], 0.0),
                     dep=None)
            pkfree = [0, 0]
            krtm = [0, 0]
            for t in range(NT):
                ti = t % 2
                pk = pk2[ti]
                krbf = krbf2[ti]
                for c in range(8):
                    p.op("t", lambda e, t=t, c=c, pk=pk: e.matmul(
                        pk[:, :],
                        xfm[c][:, t * 128:(t + 1) * 128],
                        wkr[:, c * 32:(c + 1) * 32],
                        start=(c == 0), stop=(c == 7)),
                        dep={**xfm_done, **wm,
                             "v": pkfree[ti]} if c == 0 else None)
                tm = p.cnt["t"]
                cs = cosr[:, t * 256: t * 256 + 16]
                sn = sinr[:, t * 256: t * 256 + 16]
                p.op("v", lambda e, cs=cs, pk=pk: e.tensor_mul(
                    rop[:, 0:16], pk[:, 0:16], cs),
                    dep={"t": tm, **csm, "v": p.cnt["v"]})
                p.op("v", lambda e, sn=sn, pk=pk: e.tensor_mul(
                    rop[:, 16:32], pk[:, 16:32], sn), dep={"v": p.cnt["v"]})
                p.op("v", lambda e: e.tensor_sub(rop[:, 32:48],
                                                 rop[:, 0:16], rop[:, 16:32]),
                     dep={"v": p.cnt["v"]})
                p.op("v", lambda e, cs=cs, pk=pk: e.tensor_mul(
                    rop[:, 0:16], pk[:, 16:32], cs), dep={"v": p.cnt["v"]})
                p.op("v", lambda e, sn=sn, pk=pk: e.tensor_mul(
                    rop[:, 16:32], pk[:, 0:16], sn), dep={"v": p.cnt["v"]})
                pkfree[ti] = p.cnt["v"]
                p.op("v", lambda e: e.tensor_add(rop[:, 48:64],
                                                 rop[:, 0:16], rop[:, 16:32]),
                     dep={"v": p.cnt["v"]})
                p.op("v", lambda e, krbf=krbf: e.tensor_copy(
                    krbf[:, 0:32], rop[:, 32:64]),
                    dep={"v": p.cnt["v"], "t": krtm[ti]})
                vm_ = p.cnt["v"]
                p.op("t", lambda e, ti=ti, krbf=krbf: e.transpose(
                    pb2[ti][0:65, :], krbf[:, :], idbf_sb[:, :]),
                     dep={"v": max(vm_, pbfree[ti])})
                krtm[ti] = p.cnt["t"]
                p.op("v", lambda e, t=t, ti=ti: e.tensor_copy(
                    krfm[0:32, t * 128:(t + 1) * 128], pb2[ti][0:32, 0:128]),
                    dep={"t": p.cnt["t"]})
                pbfree[ti] = p.cnt["v"]
            krm = p.cnt["v"]

            for t in range(NT):
                j = gj[0]
                gj[0] += 1
                bank = pp2[j % 2]
                for c in range(8):
                    p.op("t", lambda e, t=t, c=c, bank=bank: e.matmul(
                        bank[:, :],
                        xfm[c][:, t * 128:(t + 1) * 128],
                        wqr[:, c * 512:(c + 1) * 512],
                        start=(c == 0), stop=(c == 7)),
                        dep=M(xfm_done, wscale,
                              ppfree[j % 2]) if c == 0 else None)
                tm = p.cnt["t"]
                x1 = bank[:, :].rearrange("p (h d) -> p h d", d=32)[:, :, 0:16]
                x2 = bank[:, :].rearrange("p (h d) -> p h d", d=32)[:, :, 16:32]
                cs = cosr[:, t * 256:(t + 1) * 256].rearrange(
                    "p (h j) -> p h j", j=16)
                sn = sinr[:, t * 256:(t + 1) * 256].rearrange(
                    "p (h j) -> p h j", j=16)
                o1 = ropo[:, :].rearrange("p (h d) -> p h d", d=32)[:, :, 0:16]
                o2 = ropo[:, :].rearrange("p (h d) -> p h d", d=32)[:, :, 16:32]
                r1 = rop[:, 0:256].rearrange("p (h j) -> p h j", j=16)
                r2 = rop[:, 256:512].rearrange("p (h j) -> p h j", j=16)
                p.op("v", lambda e, x1=x1, cs=cs, r1=r1: e.tensor_mul(r1, x1, cs),
                     dep={"t": tm, **csm, "v": p.cnt["v"]})
                p.op("v", lambda e, x2=x2, sn=sn, r2=r2: e.tensor_mul(r2, x2, sn),
                     dep={"v": p.cnt["v"]})
                p.op("v", lambda e, o1=o1, r1=r1, r2=r2: e.tensor_sub(o1, r1, r2),
                     dep={"v": p.cnt["v"]})
                p.op("v", lambda e, x2=x2, cs=cs, r1=r1: e.tensor_mul(r1, x2, cs),
                     dep={"v": p.cnt["v"]})
                p.op("v", lambda e, x1=x1, sn=sn, r2=r2: e.tensor_mul(r2, x1, sn),
                     dep={"v": p.cnt["v"]})
                ppfree[j % 4] = {"v": p.cnt["v"]}
                p.op("v", lambda e, o2=o2, r1=r1, r2=r2: e.tensor_add(o2, r1, r2),
                     dep={"v": p.cnt["v"]})
                ropm = p.cnt["v"]
                for q in range(4):
                    si = q % 2
                    p.op("t", lambda e, q=q: e.transpose(
                        pb2[q % 2][:, :], ropo[:, q * 128:(q + 1) * 128],
                        idbf_sb[:, :]),
                        dep={"v": max(ropm, pbfree[q % 2])})
                    tm2 = p.cnt["t"]
                    p.op("v", lambda e, si=si, q=q, t=t: e.tensor_copy(
                        qrfm[q][:, t * 128:(t + 1) * 128], pb2[si][:, :]),
                        dep={"t": tm2})
                    pbfree[q % 2] = p.cnt["v"]

        kw_done = M(p.mark("kw0", "kw1", "kw2"), p.mark("t", "v", "s"))

        import os as _os2
        if _os2.environ.get("K_PHASE", "full") == "p2":
            with nc.Block() as block:
                p.build(block)
            return nc

        with ExitStack() as stB:
            EB = stB.enter_context
            y_all = [EB(nc.sbuf_tensor(f"yall{c}", [128, T], BF16))
                     for c in range(8)]
            wo = EB(nc.sbuf_tensor("wo_s", [128, 8 * 1024], BF16))
            stB1 = EB(ExitStack())
            EB1 = stB1.enter_context
            qsb2 = [EB1(nc.sbuf_tensor(f"qsb{i}", [97, T], BF16))
                    for i in range(3)]
            ksb2 = [EB1(nc.sbuf_tensor(f"ksb{i}", [97, T], BF16))
                    for i in range(3)]
            pbuf = EB1(nc.sbuf_tensor("pbuf", [128, 8192], BF16))
            m_all2 = [EB1(nc.sbuf_tensor(f"m_all{i}", [128, 65], F32))
                      for i in range(2)]
            mpart = EB1(nc.sbuf_tensor("mpart", [128, 4], F32))
            mT2 = [EB1(nc.sbuf_tensor(f"mT{i}", [16, 128], BF16))
                   for i in range(2)]
            dn64 = EB1(nc.sbuf_tensor("dn64", [128, 512], F32))
            ysb = EB1(nc.sbuf_tensor("ysb", [64, 512], F32))
            bcw = EB1(nc.sbuf_tensor("bcw", [128, 65], F32))
            psA2 = [EB1(nc.psum_tensor(f"psA{i}", [128, 512], F32))
                    for i in range(3)]
            psB = EB1(nc.psum_tensor("psB", [128, 2048], F32))
            psY = EB1(nc.psum_tensor("psY", [65, 512], F32))

            p.dma("g", wo[:, :].rearrange("p (c n) -> p c n", n=1024),
                  wo_in.rearrange("(c p) n -> p c n", p=128), ch="w1",
                  dep=kw_done)

            p.op("v", lambda e: e.memset(bcw[:, :], 0.0), dep=kw_done)
            p.op("v", lambda e: e.memset(bcw[64:65, :], 1.0),
                 dep={"v": p.cnt["v"]})
            p.op("v", lambda e: e.memset(dn64[:, :], 0.0), dep=kw_done)
            for i in range(2):
                p.op("v", lambda e, i=i: e.memset(m_all2[i][:, :], 0.0),
                     dep=kw_done)

            pieces = []
            for qt in range(NT):
                span = (qt + 1) * 128
                off = 0
                while off < span:
                    w = min(512, span - off)
                    pieces.append((qt, off, w, off == 0,
                                   off + w >= (qt + 1) * 128))
                    off += w

            state = {
                "qkfree": [{}, {}, {}],
                "ldm": [{}, {}, {}],
                "mfree": [0, 0],
                "redm": {},
                "pc": 0,
                "gpar": 0,
                "psB_guard": [{}, {}],
                "negm": [0, 0],
                "row96": [0, 0],
                "dnm": 0,
                "ysbm": 0,
                "ttm": 0,
                "bcm": 0,
                "rcpv": 0,
            }

            for par in range(3):
                p.dma("g", ksb2[par][64:96, :], krfm[0:32, :],
                      ch=f"hk{par}", dep=kw_done)
                p.op("v", lambda e, par=par: e.memset(
                    ksb2[par][96:97, :], 1.0), dep=kw_done)
            qkinit = {"v": p.cnt["v"]}

            def emit_loads(h):
                par = h % 3
                dep = dict(state["qkfree"][par])
                if h < 2:
                    dep = M(dep, kw_done)
                p.dma("g", qsb2[par][0:64, :], qhat[h, 0:64, :],
                      ch=f"hq{par}", dep=dep)
                p.dma("g", qsb2[par][64:96, :],
                      qrfm[h // 4][32 * (h % 4):32 * (h % 4) + 32, :],
                      ch=f"hq{par}", dep=None)
                p.dma("g", ksb2[par][0:64, :], khat[h, 0:64, :],
                      ch=f"hk{par}", dep=dep)
                state["ldm"][par] = M({f"hq{par}": p.cnt[f"hq{par}"],
                                       f"hk{par}": p.cnt[f"hk{par}"], **cm},
                                      qkinit)

            def gen_A(h):
                par = h % 3
                mpar = h % 2
                qsb, ksb = qsb2[par], ksb2[par]
                m_all = m_all2[mpar]
                ldm = state["ldm"][par]
                for qt, off, w, first, diag in pieces:
                    pc = state["pc"]
                    state["pc"] += 1
                    bank = psA2[pc % 3]
                    dep = dict(ldm)
                    if pc >= 3:
                        dep["v"] = max(dep.get("v", 0),
                                       state["redm"][pc - 3])
                    p.op("t", lambda e, qt=qt, off=off, w=w, bank=bank:
                         e.matmul(bank[:, 0:w],
                                  qsb[0:96, qt * 128:(qt + 1) * 128],
                                  ksb[0:96, off:off + w],
                                  start=True, stop=True), dep=dep)
                    tm = p.cnt["t"]
                    vdep = {"t": tm, "v": p.cnt["v"]}
                    if diag:
                        dcol = qt * 128 - off
                        p.op("v", lambda e, bank=bank, dcol=dcol: e.tensor_add(
                            bank[:, dcol:dcol + 128],
                            bank[:, dcol:dcol + 128], m0_sb[:, :]), dep=vdep)
                        vdep = {"v": p.cnt["v"]}
                    if first:
                        dd = dict(vdep)
                        dd["t"] = max(dd.get("t", 0), state["mfree"][mpar])
                        p.op("v", lambda e, qt=qt, w=w, bank=bank: e.reduce_max(
                            m_all[:, qt:qt + 1], bank[:, 0:w], axis=AX.X),
                            dep=dd)
                    else:
                        p.op("v", lambda e, w=w, bank=bank, pc=pc: e.reduce_max(
                            mpart[:, pc % 4:pc % 4 + 1], bank[:, 0:w],
                            axis=AX.X), dep=vdep)
                        p.op("v", lambda e, qt=qt, pc=pc: e.tensor_max(
                            m_all[:, qt:qt + 1], m_all[:, qt:qt + 1],
                            mpart[:, pc % 4:pc % 4 + 1]),
                            dep={"v": p.cnt["v"]})
                    state["redm"][pc] = p.cnt["v"]
                    yield
                p.op("t", lambda e: e.transpose(psB[0:65, 0:128], m_all[:, :],
                                                idf_sb[:, :]),
                     dep=M({"v": state["redm"][state["pc"] - 1], **cm},
                           state["psB_guard"][0]))
                state["mfree"][mpar] = p.cnt["t"]
                p.op("s", lambda e: e.activation(mT2[mpar][:, :],
                                                 psB[0:16, 0:128],
                                                 AF.Copy, scale=-1.0),
                     dep={"t": p.cnt["t"], "mw": p.cnt.get("mw", 0)})
                state["negm"][mpar] = p.cnt["s"]
                state["psB_guard"][0] = M(state["psB_guard"][0],
                                          {"s": p.cnt["s"]})
                p.dma("y", qhat[h, 96:97, :].rearrange("o (a b) -> (o a) b",
                                                       b=128),
                      mT2[mpar][:, :], ch="mw", dep={"s": p.cnt["s"]})
                p.dma("y", qsb[96:97, :], qhat[h, 96:97, :], ch="r96",
                      dep={"mw": p.cnt["mw"]})
                state["row96"][par % 2] = p.cnt["r96"]

            def gen_B(h):
                par = h % 3
                mpar = h % 2
                qsb, ksb = qsb2[par], ksb2[par]
                firstmm = True
                for b in range(4):
                    nch = 4 * (b + 1)
                    for g0 in range(0, nch, 2):
                        ng = min(2, nch - g0)
                        gi = state["gpar"]
                        state["gpar"] = 1 - gi
                        for ci in range(ng):
                            ct = g0 + ci
                            dep = None
                            if ci == 0:
                                dep = dict(state["psB_guard"][gi])
                                if firstmm:
                                    dep = M(dep, {"s": state["negm"][mpar]})
                                    dep["r96"] = state["row96"][par % 2]
                                    firstmm = False
                            cut = max(0, 128 * ct - 512 * b)
                            p.op("t", lambda e, ct=ct, ci=ci, b=b, gi=gi,
                                 cut=cut: e.matmul(
                                     psB[:, gi * 1024 + ci * 512 + cut:
                                         gi * 1024 + (ci + 1) * 512],
                                     ksb[:, ct * 128:(ct + 1) * 128],
                                     qsb[:, b * 512 + cut:(b + 1) * 512],
                                     start=True, stop=True), dep=dep,
                                 sem="tb")
                        p.op("s", lambda e, g0=g0, ng=ng, gi=gi: e.activation(
                            pbuf[:, g0 * 512:(g0 + ng) * 512],
                            psB[:, gi * 1024:gi * 1024 + ng * 512], AF.Exp),
                            dep={"tb": p.cnt["tb"], "g": p.cnt["g"]},
                            sem="sb")
                        state["psB_guard"][gi] = {"sb": p.cnt["sb"]}
                        em = p.cnt["sb"]
                        for ci in range(ng):
                            ct = g0 + ci
                            if ct >= 4 * b:
                                p.op("g", lambda e, ct=ct, b=b: e.affine_select(
                                    pbuf[:, ct * 512:(ct + 1) * 512],
                                    pbuf[:, ct * 512:(ct + 1) * 512],
                                    pattern=[[1, 512]], compare_op=ALU.is_ge,
                                    fill=0.0, base=512 * b - 128 * ct,
                                    channel_multiplier=-1), dep={"sb": em})
                        yield
                    for ct in range(nch):
                        cut = max(0, 128 * ct - 512 * b)
                        dep = None
                        if ct == 0:
                            dep = M({"sb": max(p.cnt["sb"], state["ysbm"]),
                                     "g": p.cnt["g"],
                                     "vb": state["ttm"],
                                     "tb": state["bcm"]}, vhat_done)
                        p.op("t", lambda e, ct=ct, h=h, nch=nch, cut=cut:
                             e.matmul(
                                 psY[:, cut:512],
                                 vhat[ct][:, h * 65:(h + 1) * 65],
                                 pbuf[:, ct * 512 + cut:(ct + 1) * 512],
                                 start=(ct == 0), stop=(ct == nch - 1)),
                             dep=dep, sem="tb")
                    vml = p.cnt["tb"]
                    p.op("v", lambda e, h=h: e.tensor_scalar_add(
                        dn64[64:65, :], psY[64:65, :],
                        c_all[64:65, h:h + 1]),
                        dep={"tb": max(vml, state["bcm"]),
                             "vb": state["rcpv"]}, sem="vb")
                    p.op("s", lambda e: e.activation(ysb[:, :], psY[0:64, :],
                                                     AF.Copy),
                         dep={"tb": vml, "vb": state["ttm"]}, sem="sb")
                    state["ysbm"] = p.cnt["sb"]
                    p.op("v", lambda e: e.reciprocal(dn64[64:65, :],
                                                     dn64[64:65, :]),
                         dep={"vb": p.cnt["vb"]},
                         sem="vb")
                    state["rcpv"] = p.cnt["vb"]
                    rcm2 = p.cnt["vb"]
                    p.op("t", lambda e: e.matmul(psY[0:65, :],
                                                 bcw[:, :],
                                                 dn64[:, :],
                                                 start=True, stop=True),
                         dep={"vb": rcm2, "sb": state["ysbm"]},
                         sem="tb")
                    state["bcm"] = p.cnt["tb"]
                    p.op("v", lambda e, h=h, b=b: e.tensor_mul(
                        y_all[h // 2][64 * (h % 2):64 * (h % 2) + 64,
                                      b * 512:(b + 1) * 512],
                        ysb[:, :], psY[0:64, :]),
                        dep={"sb": p.cnt["sb"], "tb": p.cnt["tb"],
                             "vb": p.cnt.get("vb", 0)}, sem="vb")
                    state["ttm"] = p.cnt["vb"]
                    yield
                state["qkfree"][par] = {"tb": p.cnt["tb"]}

            def drain(gen, n):
                for _ in range(n):
                    try:
                        next(gen)
                    except StopIteration:
                        return False
                return True

            import os as _os
            ILEAVE = _os.environ.get("K_ILEAVE", "1") == "1"
            emit_loads(0)
            emit_loads(1)
            ga = gen_A(0)
            drain(ga, 10 ** 9)
            for h in range(H):
                if h + 2 < H:
                    emit_loads(h + 2)
                if h + 1 < H:
                    ga = gen_A(h + 1)
                else:
                    ga = None
                gb = gen_B(h)
                if not ILEAVE:
                    if ga is not None:
                        drain(ga, 10 ** 9)
                    drain(gb, 10 ** 9)
                else:
                    while True:
                        alive_a = drain(ga, 3) if ga is not None else False
                        alive_b = drain(gb, 1)
                        if not alive_b and not alive_a:
                            break

            yall_done = {"vb": p.cnt.get("vb", 0), "v": p.cnt["v"]}

            if _os2.environ.get("K_PHASE", "full") == "attn":
                for c in range(8):
                    for hf in range(2):
                        p.dma("g", xp_out.rearrange("(c p) n -> c p n", p=128)
                              [2 * c + hf, :, :],
                              y_all[c][:, hf * 1024:(hf + 1) * 1024],
                              ch="dbg", dep="all")
                with nc.Block() as block:
                    p.build(block)
                return nc

            stB1.close()
            pend = p.mark("t", "v", "s", "g")
            with ExitStack() as stC:
                EC = stC.enter_context
                aofm = EC(nc.sbuf_tensor("aofm", [128, 8 * 512], BF16))
                aot = EC(nc.sbuf_tensor("aot", [128, 1024], F32))
                xt2 = [EC(nc.sbuf_tensor(f"xt2_{i}", [128, 1024], F32))
                       for i in range(2)]
                xp22 = [EC(nc.sbuf_tensor(f"xp2_{i}", [128, 1024], F32))
                        for i in range(2)]
                x2bf2 = [EC(nc.sbuf_tensor(f"x2bf{i}", [128, 1024], BF16))
                         for i in range(2)]
                x2fm2 = [EC(nc.sbuf_tensor(f"x2fm{i}", [128, 8 * 128], BF16))
                         for i in range(2)]
                gexp = EC(nc.sbuf_tensor("gexp", [128, 16], F32))
                grt2 = [EC(nc.sbuf_tensor(f"grt{i}", [128, 16], F32))
                        for i in range(2)]
                oneh2 = [EC(nc.sbuf_tensor(f"oneh{i}", [128, 16], F32))
                         for i in range(2)]
                xacc = EC(nc.sbuf_tensor("xacc", [128, 16], F32))
                lloct2 = [EC(nc.sbuf_tensor(f"lloct{i}", [128, 16], F32))
                          for i in range(2)]
                gvec2 = [EC(nc.sbuf_tensor(f"gvec{i}", [128, 8], F32))
                         for i in range(2)]
                cnt_sb = EC(nc.sbuf_tensor("cnt_sb", [1, 16], F32))
                pw2 = [EC(nc.psum_tensor(f"pw{i}", [128, 512], F32))
                       for i in range(2)]
                pao2 = [EC(nc.psum_tensor(f"pao{i}", [128, 128], BF16))
                        for i in range(2)]
                psG2 = [EC(nc.psum_tensor(f"psG{i}", [128, 16], F32))
                        for i in range(2)]
                psL = EC(nc.psum_tensor("psL", [128, 16], F32))
                psC = EC(nc.psum_tensor("psC", [1, 16], F32))

                p.op("v", lambda e: e.memset(xacc[:, :], 0.0), dep=pend)
                pwfree = [0, 0]
                paofree = [0, 0]
                xfree = [dict(pend), dict(pend)]
                xpfree = [{}, {}]
                x2bfree = [0, 0]
                x2fmfree = [0, 0]
                psGfree = [{}, {}]
                gvfree = [{}, {}]
                wom = p.mark("w1")
                for B in range(4):
                    for o in range(8):
                        for c in range(8):
                            dep = None
                            if c == 0:
                                dep = {**yall_done, **wom, **cm,
                                       "s": pwfree[o % 2]}
                            p.op("t", lambda e, B=B, o=o, c=c: e.matmul(
                                pw2[o % 2][:, :],
                                wo[:, c * 1024 + o * 128:
                                   c * 1024 + o * 128 + 128],
                                y_all[c][:, B * 512:(B + 1) * 512],
                                start=(c == 0), stop=(c == 7)), dep=dep)
                        p.op("s", lambda e, o=o: e.activation(
                            aofm[:, o * 512:(o + 1) * 512],
                            pw2[o % 2][:, :], AF.Copy),
                            dep={"t": p.cnt["t"]})
                        pwfree[o % 2] = p.cnt["s"]
                    aom = p.cnt["s"]
                    for u in range(4):
                        t = 4 * B + u
                        i = t % 2
                        psG = psG2[i]
                        gvec = gvec2[i]
                        grt = grt2[i]
                        oneh = oneh2[i]
                        for o in range(8):
                            p.op("t", lambda e, o=o, u=u: e.transpose(
                                pao2[o % 2][:, :],
                                aofm[:, o * 512 + u * 128:
                                     o * 512 + u * 128 + 128],
                                idbf_sb[:, :]),
                                dep={"s": aom, "v": paofree[o % 2]})
                            tm = p.cnt["t"]
                            p.op("v", lambda e, o=o: e.tensor_copy(
                                aot[:, o * 128:(o + 1) * 128],
                                pao2[o % 2][:, :]),
                                dep={"t": tm, "v": p.cnt["v"]})
                            paofree[o % 2] = p.cnt["v"]
                        p.dma("g", xt2[i][:, :],
                              x_in[t * 128:(t + 1) * 128, :],
                              ch=f"x{i}", dep=xfree[i])
                        p.op("v", lambda e, i=i: e.tensor_add(
                            xp22[i][:, :], xt2[i][:, :], aot[:, :]),
                            dep=M({f"x{i}": p.cnt[f"x{i}"],
                                   "v": p.cnt["v"]}, xpfree[i]))
                        xfree[i] = {"v": p.cnt["v"]}
                        p.dma("g", xp_out[t * 128:(t + 1) * 128, :],
                              xp22[i][:, :], ch="ow0", dep={"v": p.cnt["v"]})
                        lnr = ln_pass(p, xp22[i][:, :], scr2[i], vec2[i],
                                      x2bf2[i][:, :],
                                      dep={"v": p.cnt["v"], "s": p.cnt["s"]},
                                      dep_last={"t": x2bfree[i],
                                                "ow1": p.cnt.get("ow1", 0)})
                        xpfree[i] = M(lnr, {"ow0": p.cnt["ow0"]})
                        lnm = p.cnt["v"]
                        p.dma("g", x2_out[t * 128:(t + 1) * 128, :],
                              x2bf2[i][:, :], ch="ow1", dep={"v": lnm})
                        for c in range(8):
                            p.op("t", lambda e, c=c, i=i: e.transpose(
                                pao2[c % 2][:, :],
                                x2bf2[i][:, c * 128:(c + 1) * 128],
                                idbf_sb[:, :]),
                                dep={"v": max(lnm, paofree[c % 2])})
                            tm = p.cnt["t"]
                            p.op("v", lambda e, c=c, i=i: e.tensor_copy(
                                x2fm2[i][:, c * 128:(c + 1) * 128],
                                pao2[c % 2][:, :]),
                                dep={"t": max(tm, x2fmfree[i]),
                                     "v": p.cnt["v"]})
                            paofree[c % 2] = p.cnt["v"]
                        x2bfree[i] = p.cnt["t"]
                        x2m = p.cnt["v"]
                        for c in range(8):
                            p.op("t", lambda e, c=c, i=i, psG=psG: e.matmul(
                                psG[:, :],
                                x2fm2[i][:, c * 128:(c + 1) * 128],
                                wg[:, c * 16:(c + 1) * 16],
                                start=(c == 0), stop=(c == 7)),
                                dep=M({"v": x2m, **wm},
                                      psGfree[i]) if c == 0 else None)
                        gmm = p.cnt["t"]
                        x2fmfree[i] = gmm
                        gm = gvec[:, 0:1]
                        gmn = gvec[:, 1:2]
                        gsum = gvec[:, 2:3]
                        gsr = gvec[:, 3:4]
                        idxv = gvec[:, 5:6]
                        p.op("v", lambda e, gm=gm, psG=psG: e.reduce_max(
                            gm, psG[:, :], axis=AX.X),
                            dep=M({"t": gmm, "v": p.cnt["v"]}, gvfree[i]))
                        p.op("v", lambda e, gm=gm, gmn=gmn:
                             e.tensor_scalar_mul(gmn, gm, -1.0),
                             dep={"v": p.cnt["v"]})
                        gmnm = p.cnt["v"]
                        p.op("s", lambda e, gmn=gmn, gsum=gsum, psG=psG:
                             e.activation(
                                 gexp[:, :], psG[:, :], AF.Exp, bias=gmn,
                                 accum_out=gsum),
                             dep={"v": gmnm, "s": p.cnt["s"], "t": gmm})
                        sm = p.cnt["s"]
                        p.op("v", lambda e, grt=grt, gm=gm, psG=psG:
                             e.tensor_scalar_sub(grt[:, :], psG[:, :], gm),
                             dep={"v": p.cnt["v"]})
                        p.op("v", lambda e, grt=grt: e.tensor_scalar(
                            grt[:, :], grt[:, :], -1e30, 1.0,
                            ALU.mult, ALU.min), dep={"v": p.cnt["v"]})
                        p.op("v", lambda e, grt=grt: e.scalar_tensor_tensor(
                            grt[:, :], grt[:, :], -1.0, iom99_sb[:, :],
                            ALU.mult, ALU.mult), dep={"v": p.cnt["v"]})
                        p.op("v", lambda e, grt=grt: e.tensor_add(
                            grt[:, :], grt[:, :], iota16_sb[:, :]),
                            dep={"v": p.cnt["v"]})
                        p.op("v", lambda e, grt=grt, idxv=idxv:
                             e.tensor_reduce(
                                 idxv, grt[:, :], axis=AX.X, op=ALU.min),
                             dep={"v": p.cnt["v"]})
                        ivm = p.cnt["v"]
                        p.dma("y", idx_out[t * 128:(t + 1) * 128, :], idxv,
                              ch="ow3", dep={"v": ivm})
                        p.op("v", lambda e, oneh=oneh, idxv=idxv:
                             e.tensor_scalar_mul(
                                 oneh[:, :], ones16_sb[:, :], idxv),
                             dep={"v": ivm, "t": p.cnt["t"]})
                        p.op("v", lambda e, oneh=oneh: e.tensor_sub(
                            oneh[:, :], oneh[:, :], iota16_sb[:, :]),
                            dep={"v": p.cnt["v"]})
                        p.op("v", lambda e, oneh=oneh: e.tensor_mul(
                            oneh[:, :], oneh[:, :], oneh[:, :]),
                            dep={"v": p.cnt["v"]})
                        p.op("v", lambda e, oneh=oneh: e.tensor_scalar(
                            oneh[:, :], oneh[:, :], 1.0, -1.0,
                            ALU.min, ALU.mult), dep={"v": p.cnt["v"]})
                        p.op("v", lambda e, oneh=oneh: e.tensor_scalar_add(
                            oneh[:, :], oneh[:, :], 1.0),
                            dep={"v": p.cnt["v"]})
                        ohm = p.cnt["v"]
                        p.op("t", lambda e, oneh=oneh: e.matmul(
                            psL[:, :], trilS_sb[:, :], oneh[:, :],
                            start=True, stop=False),
                            dep={"v": ohm})
                        p.op("t", lambda e: e.matmul(
                            psL[:, :], onesf_sb[:, :], xacc[:, :],
                            start=False, stop=True), dep=None)
                        tm = p.cnt["t"]
                        p.op("v", lambda e, i=i: e.tensor_copy(
                            lloct2[i][:, :], psL[:, :]),
                             dep={"t": tm, "ow3": p.cnt.get("ow3", 0),
                                  "v": p.cnt["v"]})
                        p.dma("y", lloc_out[t, :, :], lloct2[i][:, :],
                              ch="ow3", dep={"v": p.cnt["v"]})
                        p.op("v", lambda e, oneh=oneh: e.tensor_add(
                            xacc[:, :], xacc[:, :], oneh[:, :]),
                            dep={"t": tm, "v": p.cnt["v"]})
                        p.op("v", lambda e, gsr=gsr, gsum=gsum: e.reciprocal(
                            gsr, gsum), dep={"s": sm, "v": p.cnt["v"]})
                        p.dma("y", gs_out[t * 128:(t + 1) * 128, :], gsr,
                              ch="ow2", dep={"v": p.cnt["v"]})
                        psGfree[i] = {"v": p.cnt["v"], "s": sm}
                        gvfree[i] = {"s": sm, "ow2": p.cnt["ow2"],
                                     "ow3": p.cnt["ow3"]}
                p.op("t", lambda e: e.matmul(psC[:, :], onescol_sb[:, :],
                                             xacc[:, :], start=True,
                                             stop=True),
                     dep={"v": p.cnt["v"], **cm})
                p.op("v", lambda e: e.tensor_copy(cnt_sb[:, :], psC[:, :]),
                     dep={"t": p.cnt["t"]})
                p.dma("y", cnt_out, cnt_sb[:, :], ch="ow0",
                      dep={"v": p.cnt["v"]})

        with nc.Block() as block:
            p.build(block)
    return nc

def build_L2():
    nc = bass.Bass("TRN2", target_bir_lowering=False, debug=False, num_swdge_queues=4)
    cnts_in = nc.dram_tensor("cnts", [NCORE, NEXP], F32, kind="ExternalInput").ap()
    lloc_in = nc.dram_tensor("lloc", [NT, 128, NEXP], F32, kind="ExternalInput").ap()
    idx_in = nc.dram_tensor("idxT", [128, NT], F32, kind="ExternalInput").ap()
    toki_in = nc.dram_tensor("tokiT", [128, NT], F32, kind="ExternalInput").ap()
    sel_in = nc.dram_tensor("sel", [NCORE, 1], F32, kind="ExternalInput").ap()

    gpart_out = nc.dram_tensor("gpart", [NCORE * T, 1], F32, kind="ExternalOutput").ap()
    slotp_out = nc.dram_tensor("slotp", [T, 1], F32, kind="ExternalOutput").ap()

    iota16 = nc.inline_tensor(np.tile(np.arange(16, dtype=np.float32), (128, 1)),
                              "iota16").ap()
    ones16 = nc.inline_tensor(np.ones((128, 16), np.float32), "ones16").ap()
    onesrow128 = nc.inline_tensor(np.ones((1, 128), np.float32), "or128").ap()
    idf2 = nc.inline_tensor(np.eye(128, dtype=np.float32), "idf2").ap()

    with ExitStack() as st:
        EE = st.enter_context
        p = P(nc, st)
        cnts = EE(nc.sbuf_tensor("cnts_s", [NCORE, NEXP], F32))
        or128_sb = EE(nc.sbuf_tensor("or128_sb", [1, 128], F32))
        sel = EE(nc.sbuf_tensor("sel_s", [NCORE, 1], F32))
        offr = EE(nc.sbuf_tensor("offr", [1, 16], F32))
        offb = EE(nc.sbuf_tensor("offb", [128, 16], F32))
        idxT = EE(nc.sbuf_tensor("idxT_s", [128, NT], F32))
        tokiT = EE(nc.sbuf_tensor("tokiT_s", [128, NT], F32))
        toki_u = EE(nc.sbuf_tensor("toki_u", [128, NT], F32))
        lloct = EE(nc.sbuf_tensor("lloct", [128, 16], F32))
        oneh = EE(nc.sbuf_tensor("oneh", [128, 16], F32))
        big = EE(nc.sbuf_tensor("big", [128, 128], F32))
        slots = EE(nc.sbuf_tensor("slots", [128, NT], F32))
        slots_u = EE(nc.sbuf_tensor("slots_u", [128, NT], U32))
        v1 = EE(nc.sbuf_tensor("v1", [128, 8], F32))
        i16 = EE(nc.sbuf_tensor("i16", [128, 16], F32))
        o16 = EE(nc.sbuf_tensor("o16", [128, 16], F32))
        idf_sb2 = EE(nc.sbuf_tensor("idf_sb2", [128, 128], F32))
        slotsT = EE(nc.sbuf_tensor("slotsT", [16, 128], F32))
        psO = EE(nc.psum_tensor("psO", [1, 16], F32))
        psB = EE(nc.psum_tensor("psB", [128, 16], F32))
        psT = EE(nc.psum_tensor("psT", [16, 128], F32))

        p.dma("y", cnts[:, :], cnts_in)
        p.dma("y", sel[:, :], sel_in)
        p.dma("y", idxT[:, :], idx_in)
        p.dma("y", tokiT[:, :], toki_in)
        p.dma("y", i16[:, :], iota16)
        p.dma("y", o16[:, :], ones16)
        p.dma("y", or128_sb[:, :], onesrow128)
        p.dma("y", idf_sb2[:, :], idf2)
        p.op("t", lambda e: e.matmul(psO[:, :], sel[:, :], cnts[:, :],
                                     start=True, stop=True))
        p.op("v", lambda e: e.tensor_copy(offr[:, :], psO[:, :]))
        p.op("t", lambda e: e.matmul(psB[:, :], or128_sb[:, :], offr[:, :],
                                     start=True, stop=True))
        p.op("v", lambda e: e.tensor_copy(offb[:, :], psB[:, :]))
        p.op("v", lambda e: e.memset(big[:, :], BIGSLOT))
        for r in range(NCORE):
            p.dma("y", gpart_out[r * T:(r + 1) * T, :]
                  .rearrange("(a b) o -> a (b o)", b=128), big[0:16, :])

        for t in range(NT):
            idxv = idxT[:, t:t + 1]
            gl = v1[:, 0:1]
            ofs = v1[:, 1:2]
            pen = v1[:, 2:3]
            p.dma("y", lloct[:, :], lloc_in[t, :, :])
            p.op("v", lambda e, idxv=idxv: e.tensor_scalar_mul(oneh[:, :],
                                                               o16[:, :], idxv))
            p.op("v", lambda e: e.tensor_sub(oneh[:, :], oneh[:, :], i16[:, :]))
            p.op("v", lambda e: e.tensor_mul(oneh[:, :], oneh[:, :], oneh[:, :]))
            p.op("v", lambda e: e.tensor_scalar_min(oneh[:, :], oneh[:, :], 1.0))
            p.op("s", lambda e: e.activation(oneh[:, :], oneh[:, :], AF.Copy,
                                             scale=-1.0, bias=1.0))
            p.op("v", lambda e: e.tensor_mul(lloct[:, :], lloct[:, :], oneh[:, :]))
            p.op("v", lambda e, gl=gl: e.reduce_sum(gl, lloct[:, :], axis=AX.X))
            p.op("v", lambda e: e.tensor_mul(oneh[:, :], offb[:, :], oneh[:, :]))
            p.op("v", lambda e, ofs=ofs: e.reduce_sum(ofs, oneh[:, :], axis=AX.X))
            p.op("v", lambda e, gl=gl, ofs=ofs: e.tensor_add(gl, gl, ofs))
            p.op("v", lambda e, pen=pen, gl=gl: e.tensor_scalar_add(
                pen, gl, -(CAP - 1.0)))
            p.op("v", lambda e, pen=pen: e.tensor_scalar_max(pen, pen, 0.0))
            p.op("v", lambda e, pen=pen: e.tensor_scalar_min(pen, pen, 1.0))
            p.op("s", lambda e, pen=pen: e.activation(pen, pen, AF.Copy,
                                                      scale=BIGSLOT))
            sl = slots[:, t:t + 1]
            p.op("v", lambda e, sl=sl, idxv=idxv: e.tensor_scalar_mul(
                sl, idxv, float(CAP)))
            p.op("v", lambda e, sl=sl, gl=gl: e.tensor_add(sl, sl, gl))
            p.op("v", lambda e, sl=sl, pen=pen: e.tensor_add(sl, sl, pen))
        p.op("v", lambda e: e.tensor_copy(slots_u[:, :], slots[:, :]))
        p.op("t", lambda e: e.transpose(psT[:, :], slots[:, :], idf_sb2[:, :]))
        p.op("v", lambda e: e.tensor_copy(slotsT[:, :], psT[0:16, :]))
        p.dma("y", slotp_out.rearrange("(a b) o -> a (b o)", b=128),
              slotsT[:, :])
        for t in range(NT):
            p.idma("g", lambda e, t=t: e.indirect_dma_start(
                out=gpart_out[0:128, :],
                out_offset=IOA(slots_u[:, t:t + 1], 0),
                in_=tokiT[:, t:t + 1],
                in_offset=None,
                bounds_check=NCORE * T - 1, oob_is_err=False))

        with nc.Block() as block:
            p.build(block)
    return nc


def build_L3():
    nc = bass.Bass("TRN2", target_bir_lowering=False, debug=False, num_swdge_queues=4)
    xall_in = nc.dram_tensor("xall", [NCORE * T, E], BF16, kind="ExternalInput").ap()
    gm_in = nc.dram_tensor("gmy", [128, NCORE * NT], F32, kind="ExternalInput").ap()
    w1_in = nc.dram_tensor("w1own", [2, E, FF], F32, kind="ExternalInput").ap()
    w2_in = nc.dram_tensor("w2own", [2, FF, E], F32, kind="ExternalInput").ap()

    co_out = nc.dram_tensor("contrib", [T, E], BF16, kind="ExternalOutput").ap()

    idbf = nc.inline_tensor(np.eye(128, dtype=ml_dtypes.bfloat16), "idbf").ap()

    with ExitStack() as st:
        EE = st.enter_context
        p = P(nc, st)
        idbf_sb = EE(nc.sbuf_tensor("idbf_sb", [128, 128], BF16))
        gm = EE(nc.sbuf_tensor("gm_s", [128, NCORE * NT], F32))
        gmin_u = EE(nc.sbuf_tensor("gmin_u", [128, NT], U32))
        w2sb = EE(nc.sbuf_tensor("w2sb", [128, 32 * 1024], BF16))
        w1sb2 = [EE(nc.sbuf_tensor(f"w1sb{i}", [128, 8 * 2048], BF16))
                 for i in range(2)]
        hsb = [EE(nc.sbuf_tensor(f"hsb{i}", [128, 16 * 512], BF16))
               for i in range(2)]
        xd_tm2 = [EE(nc.sbuf_tensor(f"xd_tm{i}", [128, 1024], BF16))
                  for i in range(2)]
        xd_fm = [EE(nc.sbuf_tensor(f"xdfm{c}", [128, 512], BF16))
                 for c in range(8)]
        ost2 = [EE(nc.sbuf_tensor(f"ost{i}", [128, 512], BF16))
                for i in range(2)]
        po = [EE(nc.psum_tensor(f"po{i}", [128, 512], F32)) for i in range(6)]
        pb = EE(nc.psum_tensor("pbx", [128, 128], BF16))

        p.dma("y", idbf_sb[:, :], idbf, ch="c0", dep=None)
        p.dma("y", gm[:, :], gm_in, ch="c0", dep=None)
        cm = p.mark("c0")
        p.op("v", lambda e: e.tensor_tensor(
            gm[:, 0:NT], gm[:, 0:NT], gm[:, NT:2 * NT], ALU.min), dep=cm)
        for r in range(2, NCORE):
            p.op("v", lambda e, r=r: e.tensor_tensor(
                gm[:, 0:NT], gm[:, 0:NT],
                gm[:, r * NT:(r + 1) * NT], ALU.min), dep={"v": p.cnt["v"]})
        p.op("v", lambda e: e.tensor_copy(gmin_u[:, :], gm[:, 0:NT]),
             dep={"v": p.cnt["v"]})
        gum = {"v": p.cnt["v"]}

        p.dma("g", w2sb[:, :].rearrange("p (c n) -> p c n", n=1024),
              w2_in[0, :, :].rearrange("(c p) n -> p c n", p=128),
              ch="wa", dep=None)
        p.dma("g", w1sb2[0][:, :].rearrange("p (c n) -> p c n", n=2048),
              w1_in[0, :, 0:2048].rearrange("(c p) n -> p c n", p=128),
              ch="wb", dep=None)

        xdm = [0, 0]
        relum = [0, 0, 0, 0, 0, 0]
        outm_prev = 0
        ostm = [{}, {}]
        p.dma("g", w1sb2[1][:, :].rearrange("p (c n) -> p c n", n=2048),
              w1_in[0, :, 2048:4096].rearrange("(c p) n -> p c n", p=128),
              ch="wb1", dep=None)
        w1ld = [("wb", p.cnt["wb"]), ("wb1", p.cnt["wb1"])]
        for ee in range(2):
            if ee == 1:
                p.dma("g", w2sb[:, :].rearrange("p (c n) -> p c n", n=1024),
                      w2_in[1, :, :].rearrange("(c p) n -> p c n", p=128),
                      ch="wa", dep={"t": outm_prev})
            w2m = p.cnt["wa"]
            for B in range(2):
                blk = ee * 2 + B
                for u in range(4):
                    j = blk * 4 + u
                    bi = u % 2
                    p.op("v", lambda e, bi=bi: e.memset(xd_tm2[bi][:, :], 0.0),
                         dep={"t": xdm[bi], "v": p.cnt["v"]})
                    p.idma("g", lambda e, j=j, bi=bi: e.indirect_dma_start(
                        out=xd_tm2[bi][:, :], out_offset=None,
                        in_=xall_in[0:128, :],
                        in_offset=IOA(gmin_u[:, j:j + 1], 0),
                        bounds_check=NCORE * T - 1, oob_is_err=False),
                        ch="gi", dep=M(gum, {"v": p.cnt["v"]}))
                    gim = p.cnt["gi"]
                    for c in range(8):
                        p.op("t", lambda e, c=c, bi=bi: e.transpose(
                            pb[:, :], xd_tm2[bi][:, c * 128:(c + 1) * 128],
                            idbf_sb[:, :]),
                            dep={"gi": gim, "v": p.cnt["v"], **cm})
                        p.op("v", lambda e, c=c, u=u: e.tensor_copy(
                            xd_fm[c][:, u * 128:(u + 1) * 128], pb[:, :]),
                            dep={"t": p.cnt["t"]})
                    xdm[bi] = p.cnt["t"]
                xdc = p.cnt["v"]
                for fh in range(2):
                    lch, ldm = w1ld[fh]
                    for f in range(16):
                        pi = 4 + (f % 2)
                        for c in range(8):
                            dep = None
                            if c == 0:
                                dep = {"s": relum[pi], "v": xdc,
                                       lch: ldm}
                            p.op("t", lambda e, f=f, c=c, pi=pi, fh=fh:
                                 e.matmul(
                                     po[pi][:, :],
                                     w1sb2[fh][:, c * 2048 + f * 128:
                                               c * 2048 + f * 128 + 128],
                                     xd_fm[c][:, :],
                                     start=(c == 0), stop=(c == 7)), dep=dep)
                        p.op("s", lambda e, f=f, fh=fh, pi=pi: e.activation(
                            hsb[fh][:, f * 512:(f + 1) * 512],
                            po[pi][:, :], AF.Relu), dep={"t": p.cnt["t"]})
                        relum[pi] = p.cnt["s"]
                    if B == 1 and ee == 0:
                        ch = "wb" if fh == 0 else "wb1"
                        p.dma("g", w1sb2[fh][:, :]
                              .rearrange("p (c n) -> p c n", n=2048),
                              w1_in[1, :, fh * 2048:(fh + 1) * 2048]
                              .rearrange("(c p) n -> p c n", p=128),
                              ch=ch, dep={"t": p.cnt["t"]})
                        w1ld[fh] = (ch, p.cnt[ch])
                hrel = p.cnt["s"]
                for half in range(2):
                    pairs = [(u, dh) for u in range(4) for dh in range(2)][
                        half * 4:(half + 1) * 4]
                    for pi, (u, dh) in enumerate(pairs):
                        for fh in range(2):
                            for f in range(16):
                                dep = None
                                if fh == 0 and f == 0:
                                    dep = {"s": max(relum[pi], hrel),
                                           "wa": w2m}
                                p.op("t", lambda e, u=u, dh=dh, fh=fh, f=f,
                                     pi=pi: e.matmul(
                                         po[pi][:, :],
                                         hsb[fh][:, f * 512 + u * 128:
                                                 f * 512 + u * 128 + 128],
                                         w2sb[:, (fh * 16 + f) * 1024 +
                                              dh * 512: (fh * 16 + f) * 1024 +
                                              (dh + 1) * 512],
                                         start=(fh == 0 and f == 0),
                                         stop=(fh == 1 and f == 15)), dep=dep)
                        relum[pi] = 0
                    gend = p.cnt["t"]
                    for pi, (u, dh) in enumerate(pairs):
                        oi = pi % 2
                        p.op("s", lambda e, pi=pi, oi=oi: e.activation(
                            ost2[oi][:, :], po[pi][:, :], AF.Copy),
                            dep=M({"t": gend}, ostm[oi]))
                        relum[pi] = p.cnt["s"]
                        row0 = blk * 512 + u * 128
                        ch = f"co{oi}"
                        p.dma("g", co_out[row0:row0 + 128,
                                          dh * 512:(dh + 1) * 512],
                              ost2[oi][:, :], ch=ch, dep={"s": p.cnt["s"]})
                        ostm[oi] = {ch: p.cnt[ch]}
                outm_prev = p.cnt["t"]

        with nc.Block() as block:
            p.build(block)
    return nc


def build_L4():
    nc = bass.Bass("TRN2", target_bir_lowering=False, debug=False, num_swdge_queues=4)
    call_in = nc.dram_tensor("call", [NCORE * T, E], BF16, kind="ExternalInput").ap()
    slot_in = nc.dram_tensor("slotT", [128, NT], F32, kind="ExternalInput").ap()
    gs_in = nc.dram_tensor("gsT", [128, NT], F32, kind="ExternalInput").ap()
    xp_in = nc.dram_tensor("xp", [T, E], F32, kind="ExternalInput").ap()
    out = nc.dram_tensor("out", [T, E], F32, kind="ExternalOutput").ap()

    with ExitStack() as st:
        EE = st.enter_context
        p = P(nc, st)
        slotT = EE(nc.sbuf_tensor("slotT_s", [128, NT], F32))
        slot_u = EE(nc.sbuf_tensor("slot_u", [128, NT], U32))
        gsT = EE(nc.sbuf_tensor("gsT_s", [128, NT], F32))
        rt = EE(nc.sbuf_tensor("rt", [128, 1024], BF16))
        rtf = EE(nc.sbuf_tensor("rtf", [128, 1024], F32))
        xt = EE(nc.sbuf_tensor("xt", [128, 1024], F32))
        ot = EE(nc.sbuf_tensor("ot", [128, 1024], F32))

        p.dma("y", slotT[:, :], slot_in)
        p.dma("y", gsT[:, :], gs_in)
        p.op("v", lambda e: e.tensor_copy(slot_u[:, :], slotT[:, :]))
        for t in range(NT):
            p.op("v", lambda e: e.memset(rt[:, :], 0.0))
            p.idma("g", lambda e, t=t: e.indirect_dma_start(
                out=rt[:, :], out_offset=None,
                in_=call_in[0:128, :],
                in_offset=IOA(slot_u[:, t:t + 1], 0),
                bounds_check=NCORE * T - 1, oob_is_err=False))
            p.op("v", lambda e, t=t: e.tensor_scalar_mul(rtf[:, :], rt[:, :],
                                                         gsT[:, t:t + 1]))
            p.dma("g", xt[:, :], xp_in[t * 128:(t + 1) * 128, :], ch="xl")
            p.op("v", lambda e: e.tensor_add(ot[:, :], xt[:, :], rtf[:, :]))
            p.dma("g", out[t * 128:(t + 1) * 128, :], ot[:, :], ch="ol")

        with nc.Block() as block:
            p.build(block)
    return nc


_NC_CACHE = {}


def _get(name, builder):
    if name not in _NC_CACHE:
        _NC_CACHE[name] = builder()
    return _NC_CACHE[name]


def kernel(L0_attn_x, L0_attn_Wq, L0_attn_Wqr, L0_attn_Wkr, L0_attn_Wkvd,
           L0_attn_Wku, L0_attn_Wvu, L0_attn_Wo, L0_attn_lobo_log,
           L0_ffn_Wg, L0_ffn_w1, L0_ffn_w2):
    x = np.ascontiguousarray(np.asarray(L0_attn_x, dtype=np.float32))
    wq = np.ascontiguousarray(np.asarray(L0_attn_Wq, dtype=np.float32))
    wqr = np.ascontiguousarray(np.asarray(L0_attn_Wqr, dtype=np.float32))
    wkr = np.ascontiguousarray(np.asarray(L0_attn_Wkr, dtype=np.float32))
    wkvd = np.ascontiguousarray(np.asarray(L0_attn_Wkvd, dtype=np.float32))
    wku = np.ascontiguousarray(np.asarray(L0_attn_Wku, dtype=np.float32))
    wvu = np.ascontiguousarray(np.asarray(L0_attn_Wvu, dtype=np.float32))
    wo = np.ascontiguousarray(np.asarray(L0_attn_Wo, dtype=np.float32))
    lobo = np.ascontiguousarray(np.asarray(L0_attn_lobo_log,
                                           dtype=np.float32)).reshape(1, H)
    wg = np.ascontiguousarray(np.asarray(L0_ffn_Wg, dtype=np.float32))
    w1 = np.ascontiguousarray(np.asarray(L0_ffn_w1, dtype=np.float32))
    w2 = np.ascontiguousarray(np.asarray(L0_ffn_w2, dtype=np.float32))

    jj = np.arange(16, dtype=np.float64)
    inv = 1.0 / (10000.0 ** (jj / 16.0))
    ang = np.arange(T, dtype=np.float64)[:, None] * inv[None, :]
    cosr = np.tile(np.cos(ang).astype(np.float32), (1, 16))
    sinr = np.tile(np.sin(ang).astype(np.float32), (1, 16))

    nc1 = _get("L1", build_L1)
    maps1 = [{"x": x[k], "wq": wq, "wqr": wqr, "wkr": wkr, "wkvd": wkvd,
              "wku": wku, "wvu": wvu, "wo": wo, "lobo": lobo, "wg": wg,
              "cosr": cosr, "sinr": sinr} for k in range(NCORE)]
    r1 = run_bass_kernel_spmd(nc1, maps1, core_ids=list(range(NCORE))).results

    cnts = np.concatenate([r["cnto"] for r in r1], axis=0)
    nc2 = _get("L2", build_L2)
    toki = np.arange(NCORE * T, dtype=np.float32).reshape(NCORE, NT, 128)
    maps2 = []
    for k in range(NCORE):
        maps2.append({
            "cnts": cnts,
            "lloc": r1[k]["lloco"],
            "idxT": np.ascontiguousarray(
                r1[k]["idxo"].reshape(NT, 128).T),
            "tokiT": np.ascontiguousarray(toki[k].transpose(1, 0)),
            "sel": (np.arange(NCORE, dtype=np.float32) < k)
            .astype(np.float32).reshape(NCORE, 1),
        })
    r2 = run_bass_kernel_spmd(nc2, maps2, core_ids=list(range(NCORE))).results

    xall = np.concatenate([r["xln2"] for r in r1], axis=0)
    gparts = np.stack([r2[k]["gpart"][:, 0] for k in range(NCORE)])
    nc3 = _get("L3", build_L3)
    maps3 = []
    for k in range(NCORE):
        gmy = gparts[:, k * T:(k + 1) * T]
        gmy = gmy.reshape(NCORE, NT, 128).transpose(2, 0, 1).reshape(128,
                                                                     NCORE * NT)
        maps3.append({
            "xall": xall,
            "gmy": np.ascontiguousarray(gmy),
            "w1own": w1[2 * k:2 * k + 2],
            "w2own": w2[2 * k:2 * k + 2],
        })
    r3 = run_bass_kernel_spmd(nc3, maps3, core_ids=list(range(NCORE))).results

    call = np.concatenate([r["contrib"] for r in r3], axis=0)
    nc4 = _get("L4", build_L4)
    maps4 = []
    for k in range(NCORE):
        maps4.append({
            "call": call,
            "slotT": np.ascontiguousarray(
                r2[k]["slotp"].reshape(NT, 128).T),
            "gsT": np.ascontiguousarray(
                r1[k]["gso"].reshape(NT, 128).T),
            "xp": r1[k]["xprime"],
        })
    r4 = run_bass_kernel_spmd(nc4, maps4, core_ids=list(range(NCORE))).results

    out = np.stack([r4[k]["out"] for k in range(NCORE)], axis=0)
    return out.astype(np.float32)


# version 22
# speedup vs baseline: 1.0578x; 1.0578x over previous
import numpy as np
import ml_dtypes
import concourse.bass as bass
import concourse.mybir as mybir
from concourse.bass_utils import run_bass_kernel_spmd
from contextlib import ExitStack

F32 = mybir.dt.float32
BF16 = mybir.dt.bfloat16
U32 = mybir.dt.uint32
AF = mybir.ActivationFunctionType
ALU = mybir.AluOpType
AX = mybir.AxisListType
IOA = bass.IndirectOffsetOnAxis

NCORE = 8
T = 2048
E = 1024
H = 16
DH = 64
DR = 32
DQK = DH + DR
DKV = 256
NEXP = 16
CAP = 1024
FF = 4096
NT = T // 128
SCALE = 1.0 / float(np.sqrt(np.float64(DQK)))
EPS = 1e-5
BIGSLOT = 1.0e9


def M(*ds):
    out = {}
    for d in ds:
        for k, v in d.items():
            if v > out.get(k, 0):
                out[k] = v
    return out


class P:
    ENG = ("t", "v", "s", "g", "y")

    def __init__(self, nc, stack):
        self.nc = nc
        self.stack = stack
        self.sem = {e: stack.enter_context(nc.semaphore("sq_" + e))
                    for e in self.ENG}
        self.cnt = {e: 0 for e in self.ENG}
        self.ops = {e: [] for e in self.ENG}
        self.lastw = {e: {} for e in self.ENG}

    def chan(self, name):
        if name not in self.sem:
            self.sem[name] = self.stack.enter_context(
                self.nc.semaphore("ch_" + name))
            self.cnt[name] = 0
        return name

    def mark(self, *names):
        if names:
            return {n: self.cnt[n] for n in names}
        return dict(self.cnt)

    def _waits(self, eng, dep):
        if dep == "all":
            want = dict(self.cnt)
        elif dep is None:
            want = {}
        elif isinstance(dep, dict):
            want = {d: v for d, v in dep.items() if v > 0}
        else:
            want = {d: self.cnt[d] for d in dep}
        waits = []
        lw = self.lastw[eng]
        for d, v in want.items():
            if v > lw.get(d, 0):
                lw[d] = v
                waits.append((self.sem[d], v))
        return waits

    def op(self, eng, fn, dep="all", sem=None):
        name = eng if sem is None else self.chan_c(sem)
        waits = self._waits(eng, dep)
        self.cnt[name] += 1
        self.ops[eng].append((fn, waits, self.sem[name], 1))

    def chan_c(self, name):
        if name not in self.sem:
            self.sem[name] = self.stack.enter_context(
                self.nc.semaphore("cs_" + name))
            self.cnt[name] = 0
        return name

    def idma(self, eng, fn, dep="all", ch=None):
        ch = self.chan(ch if ch is not None else "qi_" + eng)
        waits = self._waits(eng, dep)
        v = self.cnt[ch]
        if v > self.lastw[eng].get(ch, 0):
            self.lastw[eng][ch] = v
            waits.append((self.sem[ch], v))
        self.cnt[ch] += 16
        self.ops[eng].append((fn, waits, self.sem[ch], 16))

    def dma(self, eng, out, in_, dep="all", ch=None, **kw):
        ch = self.chan(ch if ch is not None else "q_" + eng)
        waits = self._waits(eng, dep)
        v = self.cnt[ch]
        if v > self.lastw[eng].get(ch, 0):
            self.lastw[eng][ch] = v
            waits.append((self.sem[ch], v))
        self.cnt[ch] += 16
        self.ops[eng].append(
            (lambda e, o=out, i=in_, k=kw: e.dma_start(out=o, in_=i, **k),
             waits, self.sem[ch], 16))

    def build(self, block):
        deco = {"t": block.tensor, "v": block.vector, "s": block.scalar,
                "g": block.gpsimd, "y": block.sync}
        for eng in self.ENG:
            ops = self.ops[eng]
            if not ops:
                continue

            def body(e, ops=ops):
                for fn, waits, sem, inc in ops:
                    for h, v in waits:
                        e.wait_ge(h, v)
                    fn(e).then_inc(sem, inc)
            deco[eng](body)
        final = [(self.sem[e], c) for e, c in self.cnt.items() if c]

        def tail(e, fin=final):
            for h, v in fin:
                e.wait_ge(h, v)
        block.sync(tail)


def ln_pass(p, xt, scratch, vec, out_bf, dep, dep_last=None):
    ssq = vec[:, 0:1]
    s1 = vec[:, 1:2]
    mu = vec[:, 2:3]
    var = vec[:, 3:4]
    sd = vec[:, 4:5]
    rstd = vec[:, 5:6]
    nmr = vec[:, 6:7]
    p.op("s", lambda e: e.activation(scratch[:, :], xt, AF.Square,
                                     accum_out=ssq), dep=dep)
    sqm = p.cnt["s"]
    p.op("v", lambda e: e.reduce_sum(s1, xt, axis=AX.X), dep=dep)
    p.op("v", lambda e: e.tensor_scalar_mul(mu, s1, 1.0 / E),
         dep={"v": p.cnt["v"]})
    p.op("v", lambda e: e.tensor_scalar_mul(var, ssq, 1.0 / E),
         dep={"v": p.cnt["v"], "s": sqm})
    p.op("v", lambda e: e.tensor_mul(sd, mu, mu), dep={"v": p.cnt["v"]})
    p.op("v", lambda e: e.tensor_sub(var, var, sd), dep={"v": p.cnt["v"]})
    p.op("v", lambda e: e.tensor_scalar_add(var, var, float(EPS)),
         dep={"v": p.cnt["v"]})
    p.op("s", lambda e: e.activation(sd, var, AF.Ln),
         dep={"v": p.cnt["v"], "s": p.cnt["s"]})
    p.op("s", lambda e: e.activation(rstd, sd, AF.Exp, scale=-0.5),
         dep={"s": p.cnt["s"]})
    p.op("v", lambda e: e.scalar_tensor_tensor(nmr, mu, -1.0, rstd,
                                               ALU.mult, ALU.mult),
         dep={"v": p.cnt["v"], "s": p.cnt["s"]})
    fin = {"v": p.cnt["v"], "s": p.cnt["s"]}
    if dep_last:
        fin = M(dep_last, fin)
    p.op("v", lambda e: e.tensor_scalar(out_bf, xt, rstd, nmr,
                                        ALU.mult, ALU.add), dep=fin)
    return {"s": sqm, "v": p.cnt["v"]}


def ln_pair(p, xts, scrs, vecs, outs, deps, dep_lasts):
    st1 = [0, 0]
    st2 = [0, 0]
    ex = [None, None]
    for k in range(2):
        p.op("s", lambda e, k=k: e.activation(
            scrs[k][:, :], xts[k], AF.Square, accum_out=vecs[k][:, 0:1]),
            dep=deps[k])
        st1[k] = p.cnt["s"]
    for k in range(2):
        vec = vecs[k]
        ssq, s1, mu, var, sd, rstd, nmr = (vec[:, j:j + 1] for j in range(7))
        p.op("v", lambda e, k=k, s1=s1: e.reduce_sum(s1, xts[k], axis=AX.X),
             dep=M(deps[k], {"v": p.cnt["v"]}))
        p.op("v", lambda e, mu=mu, s1=s1: e.tensor_scalar_mul(
            mu, s1, 1.0 / E), dep={"v": p.cnt["v"]})
        p.op("v", lambda e, var=var, ssq=ssq: e.tensor_scalar_mul(
            var, ssq, 1.0 / E), dep={"v": p.cnt["v"], "s": st1[k]})
        p.op("v", lambda e, sd=sd, mu=mu: e.tensor_mul(sd, mu, mu),
             dep={"v": p.cnt["v"]})
        p.op("v", lambda e, var=var, sd=sd: e.tensor_sub(var, var, sd),
             dep={"v": p.cnt["v"]})
        p.op("v", lambda e, var=var: e.tensor_scalar_add(
            var, var, float(EPS)), dep={"v": p.cnt["v"]})
        vm = p.cnt["v"]
        p.op("s", lambda e, sd=sd, var=var: e.activation(sd, var, AF.Ln),
             dep={"v": vm, "s": p.cnt["s"]})
        p.op("s", lambda e, rstd=rstd, sd=sd: e.activation(
            rstd, sd, AF.Exp, scale=-0.5), dep={"s": p.cnt["s"]})
        st2[k] = p.cnt["s"]
    for k in range(2):
        vec = vecs[k]
        mu, rstd, nmr = vec[:, 2:3], vec[:, 5:6], vec[:, 6:7]
        p.op("v", lambda e, nmr=nmr, mu=mu, rstd=rstd: e.scalar_tensor_tensor(
            nmr, mu, -1.0, rstd, ALU.mult, ALU.mult),
            dep={"v": p.cnt["v"], "s": st2[k]})
        fin = M(dep_lasts[k], {"v": p.cnt["v"], "s": st2[k]})
        p.op("v", lambda e, k=k, rstd=rstd, nmr=nmr: e.tensor_scalar(
            outs[k], xts[k], rstd, nmr, ALU.mult, ALU.add), dep=fin)
        ex[k] = {"s": st1[k], "v": p.cnt["v"]}
    return ex


def build_L1():
    nc = bass.Bass("TRN2", target_bir_lowering=False, debug=False, num_swdge_queues=4)
    x_in = nc.dram_tensor("x", [T, E], F32, kind="ExternalInput").ap()
    wq_in = nc.dram_tensor("wq", [E, H * DH], F32, kind="ExternalInput").ap()
    wqr_in = nc.dram_tensor("wqr", [E, H * DR], F32, kind="ExternalInput").ap()
    wkr_in = nc.dram_tensor("wkr", [E, DR], F32, kind="ExternalInput").ap()
    wkvd_in = nc.dram_tensor("wkvd", [E, DKV], F32, kind="ExternalInput").ap()
    wku_in = nc.dram_tensor("wku", [DKV, H * DH], F32, kind="ExternalInput").ap()
    wvu_in = nc.dram_tensor("wvu", [DKV, H * DH], F32, kind="ExternalInput").ap()
    wo_in = nc.dram_tensor("wo", [E, E], F32, kind="ExternalInput").ap()
    lobo_in = nc.dram_tensor("lobo", [1, H], F32, kind="ExternalInput").ap()
    wg_in = nc.dram_tensor("wg", [E, NEXP], F32, kind="ExternalInput").ap()
    cosr_in = nc.dram_tensor("cosr", [T, 256], F32, kind="ExternalInput").ap()
    sinr_in = nc.dram_tensor("sinr", [T, 256], F32, kind="ExternalInput").ap()

    xp_out = nc.dram_tensor("xprime", [T, E], F32, kind="ExternalOutput").ap()
    x2_out = nc.dram_tensor("xln2", [T, E], BF16, kind="ExternalOutput").ap()
    idx_out = nc.dram_tensor("idxo", [T, 1], F32, kind="ExternalOutput").ap()
    gs_out = nc.dram_tensor("gso", [T, 1], F32, kind="ExternalOutput").ap()
    lloc_out = nc.dram_tensor("lloco", [NT, 128, NEXP], F32, kind="ExternalOutput").ap()
    cnt_out = nc.dram_tensor("cnto", [1, NEXP], F32, kind="ExternalOutput").ap()

    qhat = nc.dram_tensor("qhat", [H, 97, T], BF16, kind="Internal").ap()
    khat = nc.dram_tensor("khat", [H, 97, T], BF16, kind="Internal").ap()

    idbf = nc.inline_tensor(np.eye(128, dtype=ml_dtypes.bfloat16), "idbf").ap()
    idf = nc.inline_tensor(np.eye(128, dtype=np.float32), "idf").ap()
    m0np = np.where(np.arange(128)[None, :] > np.arange(128)[:, None],
                    np.float32(-1e30), np.float32(0.0)).astype(np.float32)
    m0 = nc.inline_tensor(m0np, "m0").ap()
    trilS = nc.inline_tensor(
        np.where(np.arange(128)[:, None] < np.arange(128)[None, :],
                 np.float32(1.0), np.float32(0.0)).astype(np.float32), "trilS").ap()
    onesf = nc.inline_tensor(np.ones((128, 128), np.float32), "onesf").ap()
    onescol = nc.inline_tensor(np.ones((128, 1), np.float32), "onescol").ap()
    iota16 = nc.inline_tensor(np.tile(np.arange(16, dtype=np.float32), (128, 1)),
                              "iota16").ap()
    iom99 = nc.inline_tensor(np.tile(np.arange(16, dtype=np.float32) - 99.0,
                                     (128, 1)), "iom99").ap()
    ones16 = nc.inline_tensor(np.ones((128, 16), np.float32), "ones16").ap()
    onesrow128 = nc.inline_tensor(np.ones((1, 128), np.float32), "or128i").ap()

    with ExitStack() as st:
        EE = st.enter_context
        p = P(nc, st)
        wkvd = EE(nc.sbuf_tensor("wkvd_s", [128, 8 * DKV], BF16))
        wku = EE(nc.sbuf_tensor("wku_s", [128, 2 * 1024], BF16))
        wvu = EE(nc.sbuf_tensor("wvu_s", [128, 2 * 1024], BF16))
        wkr = EE(nc.sbuf_tensor("wkr_s", [128, 8 * DR], BF16))
        wq = EE(nc.sbuf_tensor("wq_s", [128, 8 * 1024], BF16))
        wqr = EE(nc.sbuf_tensor("wqr_s", [128, 8 * 512], BF16))
        wg = EE(nc.sbuf_tensor("wg_s", [128, 8 * NEXP], BF16))
        vhat = [EE(nc.sbuf_tensor(f"vhat{t}", [128, H * 65], BF16))
                for t in range(NT)]
        idbf_sb = EE(nc.sbuf_tensor("idbf_sb", [128, 128], BF16))
        idf_sb = EE(nc.sbuf_tensor("idf_sb", [128, 128], F32))
        m0_sb = EE(nc.sbuf_tensor("m0_sb", [128, 128], F32))
        trilS_sb = EE(nc.sbuf_tensor("trilS_sb", [128, 128], F32))
        onesf_sb = EE(nc.sbuf_tensor("onesf_sb", [128, 128], F32))
        onescol_sb = EE(nc.sbuf_tensor("onescol_sb", [128, 1], F32))
        iota16_sb = EE(nc.sbuf_tensor("iota16_sb", [128, 16], F32))
        iom99_sb = EE(nc.sbuf_tensor("iom99_sb", [128, 16], F32))
        ones16_sb = EE(nc.sbuf_tensor("ones16_sb", [128, 16], F32))
        c_all = EE(nc.sbuf_tensor("c_all", [128, 16], F32))
        lobo_sb = EE(nc.sbuf_tensor("lobo_sb", [1, 16], F32))
        vec2 = [EE(nc.sbuf_tensor(f"vec{i}", [128, 16], F32)) for i in range(2)]
        scr2 = [EE(nc.sbuf_tensor(f"scr{i}", [128, 1024], F32)) for i in range(2)]

        for dst, src in ((idbf_sb, idbf), (idf_sb, idf), (m0_sb, m0),
                         (trilS_sb, trilS), (onesf_sb, onesf),
                         (onescol_sb, onescol), (iota16_sb, iota16),
                         (iom99_sb, iom99), (ones16_sb, ones16),
                         (lobo_sb, lobo_in)):
            p.dma("y", dst[:, :], src, ch="c0", dep=None)
        cm = p.mark("c0")

        p.dma("g", wkvd[:, :].rearrange("p (c n) -> p c n", n=DKV),
              wkvd_in.rearrange("(c p) n -> p c n", p=128), ch="w0", dep=None)
        p.dma("g", wku[:, :].rearrange("p (c n) -> p c n", n=1024),
              wku_in.rearrange("(c p) n -> p c n", p=128), ch="w1", dep=None)
        p.dma("g", wvu[:, :].rearrange("p (c n) -> p c n", n=1024),
              wvu_in.rearrange("(c p) n -> p c n", p=128), ch="w2", dep=None)
        p.dma("g", wkr[:, :].rearrange("p (c n) -> p c n", n=DR),
              wkr_in.rearrange("(c p) n -> p c n", p=128), ch="w0", dep=None)
        p.dma("g", wq[:, :].rearrange("p (c n) -> p c n", n=1024),
              wq_in.rearrange("(c p) n -> p c n", p=128), ch="w1", dep=None)
        p.dma("g", wqr[:, :].rearrange("p (c n) -> p c n", n=512),
              wqr_in.rearrange("(c p) n -> p c n", p=128), ch="w2", dep=None)
        p.dma("g", wg[:, :].rearrange("p (c n) -> p c n", n=NEXP),
              wg_in.rearrange("(c p) n -> p c n", p=128), ch="w0", dep=None)
        wm = p.mark("w0", "w1", "w2")
        p.op("s", lambda e: e.activation(wq[:, :], wq[:, :], AF.Copy,
                                         scale=SCALE), dep=wm)
        p.op("s", lambda e: e.activation(wqr[:, :], wqr[:, :], AF.Copy,
                                         scale=SCALE), dep=wm)
        wscale = p.mark("s")

        with nc.psum_tensor("psc", [128, 16], F32) as psc:
            p.op("s", lambda e: e.activation(lobo_sb[:, :], lobo_sb[:, :],
                                             AF.Exp), dep=cm)
            p.op("t", lambda e: e.matmul(psc[:, :], onesf_sb[0:1, :],
                                         lobo_sb[:, :], start=True, stop=True),
                 dep={"s": p.cnt["s"], **cm})
            p.op("v", lambda e: e.tensor_copy(c_all[:, :], psc[:, :]),
                 dep={"t": p.cnt["t"]})

        import os as _os3
        if _os3.environ.get("K_PHASE", "full") == "p0":
            with nc.Block() as block:
                p.build(block)
            return nc

        krfm = EE(nc.sbuf_tensor("krfm", [32, T], BF16))
        qrfm = [EE(nc.sbuf_tensor(f"qrfm{i}", [128, T], BF16))
                for i in range(4)]
        with ExitStack() as stA:
            EA = stA.enter_context
            xfm = [EA(nc.sbuf_tensor(f"xfm{c}", [128, T], BF16)) for c in range(8)]
            lat = [EA(nc.sbuf_tensor(f"lat{c}", [128, T], BF16)) for c in range(2)]
            cosr = EA(nc.sbuf_tensor("cosr_s", [128, NT * 256], F32))
            sinr = EA(nc.sbuf_tensor("sinr_s", [128, NT * 256], F32))
            xt2b = [EA(nc.sbuf_tensor(f"xt{i}", [128, 1024], F32))
                    for i in range(2)]
            x1t2 = [EA(nc.sbuf_tensor(f"x1t{i}", [128, 1024], BF16))
                    for i in range(2)]
            stg2b = [EA(nc.sbuf_tensor(f"stg{i}", [128, 512], BF16))
                     for i in range(2)]
            rop = EA(nc.sbuf_tensor("rop", [128, 512], F32))
            ropo = EA(nc.sbuf_tensor("ropo", [128, 512], BF16))
            krbf2 = [EA(nc.sbuf_tensor(f"krbf{i}", [128, 65], BF16))
                     for i in range(2)]

            pp2 = [EA(nc.psum_tensor(f"pp{i}", [128, 512], F32))
                   for i in range(4)]
            pb2 = [EA(nc.psum_tensor(f"pb{i}", [128, 128], BF16))
                   for i in range(2)]
            pk2 = [EA(nc.psum_tensor(f"pk{i}", [128, 32], F32))
                   for i in range(2)]

            p.dma("g", cosr[:, :].rearrange("p (t n) -> p t n", n=256),
                  cosr_in.rearrange("(t p) n -> p t n", p=128), ch="cs", dep=None)
            p.dma("g", sinr[:, :].rearrange("p (t n) -> p t n", n=256),
                  sinr_in.rearrange("(t p) n -> p t n", p=128), ch="cs", dep=None)
            csm = p.mark("cs")

            lnfree = [{}, {}]
            trmark = [0, 0]
            pbfree = [0, 0]
            for tp in range(0, NT, 2):
                ents = []
                for t in (tp, tp + 1):
                    i = t % 2
                    xch = f"x{i}"
                    p.dma("g", xt2b[i][:, :], x_in[t * 128:(t + 1) * 128, :],
                          ch=xch, dep=lnfree[i])
                    ents.append({xch: p.cnt[xch], **cm})
                exs = ln_pair(p, [xt2b[0][:, :], xt2b[1][:, :]],
                              scr2, vec2,
                              [x1t2[0][:, :], x1t2[1][:, :]], ents,
                              [{"t": trmark[0]}, {"t": trmark[1]}])
                for t in (tp, tp + 1):
                    i = t % 2
                    lnfree[i] = exs[i]
                    for c in range(8):
                        p.op("t", lambda e, c=c, i=i: e.transpose(
                            pb2[c % 2][:, :],
                            x1t2[i][:, c * 128:(c + 1) * 128],
                            idbf_sb[:, :]),
                            dep={"v": max(exs[i]["v"], pbfree[c % 2]), **cm})
                        tm = p.cnt["t"]
                        p.op("v", lambda e, c=c, t=t: e.tensor_copy(
                            xfm[c][:, t * 128:(t + 1) * 128],
                            pb2[c % 2][:, :]),
                            dep={"t": tm})
                        pbfree[c % 2] = p.cnt["v"]
                    trmark[i] = p.cnt["t"]
            xfm_done = {"v": p.cnt["v"]}

            if _os3.environ.get("K_PHASE", "full") == "p1":
                with nc.Block() as block:
                    p.build(block)
                return nc

            ppfree = [{}, {}, {}, {}]
            gj = [0]

            def proj(slices, evac_fn, extra=None, reader="s"):
                j = gj[0]
                gj[0] += 1
                bank = pp2[j % 4]
                for i, (l, r) in enumerate(slices):
                    dep = None
                    if i == 0:
                        dep = M(ppfree[j % 4], extra or {})
                    p.op("t", lambda e, l=l, r=r, i=i, n=len(slices), bank=bank:
                         e.matmul(bank[:, :], l, r, start=(i == 0),
                                  stop=(i == n - 1)), dep=dep)
                tm = p.cnt["t"]
                p.op(reader, lambda e, bank=bank: evac_fn(e, bank),
                     dep={"t": tm})
                ppfree[j % 4] = {reader: p.cnt[reader]}
                return p.cnt[reader]

            for B in range(4):
                for d in range(2):
                    sl = [(wkvd[:, c * DKV + d * 128: c * DKV + d * 128 + 128],
                           xfm[c][:, B * 512:(B + 1) * 512]) for c in range(8)]
                    proj(sl, lambda e, bank, B=B, d=d: e.activation(
                        lat[d][:, B * 512:(B + 1) * 512], bank[:, :], AF.Copy),
                        extra={**xfm_done, **wm})
            lat_done = {"s": p.cnt["s"]}

            if _os3.environ.get("K_PHASE", "full") == "p2a":
                with nc.Block() as block:
                    p.build(block)
                return nc

            stgfree = [{}, {}]
            kwi = [0]

            def head_rows(dst, B, o, src_is_q):
                j = gj[0]
                si = j % 2
                stg = stg2b[si]
                if src_is_q:
                    sl = [(wq[:, c * 1024 + o * 128: c * 1024 + o * 128 + 128],
                           xfm[c][:, B * 512:(B + 1) * 512]) for c in range(8)]
                    extra = {**xfm_done, **wscale}
                else:
                    sl = [(wku[:, c * 1024 + o * 128: c * 1024 + o * 128 + 128],
                           lat[c][:, B * 512:(B + 1) * 512]) for c in range(2)]
                    extra = {**lat_done, **wm}
                em = proj(sl, lambda e, bank, stg=stg: e.activation(
                    stg[:, :], bank[:, :], AF.Copy),
                    extra=M(extra, stgfree[si]))
                ch = f"kw{kwi[0] % 3}"
                kwi[0] += 1
                p.dma("g", dst[2 * o, 0:64, B * 512:(B + 1) * 512],
                      stg[0:64, :], ch=ch, dep={"s": em})
                p.dma("g", dst[2 * o + 1, 0:64, B * 512:(B + 1) * 512],
                      stg[64:128, :], ch=ch, dep=None)
                stgfree[si] = {ch: p.cnt[ch]}

            for B in range(4):
                for o in range(8):
                    head_rows(khat, B, o, False)
            for B in range(4):
                for o in range(8):
                    head_rows(qhat, B, o, True)

            if _os3.environ.get("K_PHASE", "full") == "p2c":
                with nc.Block() as block:
                    p.build(block)
                return nc

            _vdep = _os3.environ.get("K_P2D", "prec")
            for t in range(NT):
                vm = []
                for nh in range(2):
                    sl = [(lat[c][:, t * 128:(t + 1) * 128],
                           wvu[:, c * 1024 + nh * 512: c * 1024 + (nh + 1) * 512])
                          for c in range(2)]
                    if _vdep == "allall":
                        j = gj[0]; gj[0] += 1
                        bank = pp2[j % 2]
                        for i, (l, r) in enumerate(sl):
                            p.op("t", lambda e, l=l, r=r, i=i, bank=bank:
                                 e.matmul(bank[:, :], l, r, start=(i == 0),
                                          stop=(i == 1)), dep="all")
                        p.op("s", lambda e, bank=bank, t=t, nh=nh: e.activation(
                            vhat[t][:, :].rearrange("p (h q) -> p h q", q=65)
                            [:, nh * 8:(nh + 1) * 8, 0:64],
                            bank[:, :].rearrange("p (h d) -> p h d", d=64),
                            AF.Copy), dep="all")
                        ppfree[j % 2] = {"s": p.cnt["s"]}
                    else:
                        vm.append(proj(sl, lambda e, bank, t=t, nh=nh:
                                       e.activation(
                            vhat[t][:, :].rearrange("p (h q) -> p h q", q=65)
                            [:, nh * 8:(nh + 1) * 8, 0:64],
                            bank[:, :].rearrange("p (h d) -> p h d", d=64),
                            AF.Copy), extra={**lat_done, **wm}))
                if _os3.environ.get("K_NOMEMSET", "0") == "1":
                    pass
                elif _vdep == "allall":
                    p.op("v", lambda e, t=t: e.memset(
                        vhat[t][:, :].rearrange("p (h q) -> p h q", q=65)
                        [:, :, 64:65], 1.0), dep="all")
                else:
                    p.op("v", lambda e, t=t: e.memset(
                        vhat[t][:, :].rearrange("p (h q) -> p h q", q=65)
                        [:, :, 64:65], 1.0), dep=None)
            vhat_done = {"s": p.cnt["s"], "v": p.cnt["v"]}

            if _os3.environ.get("K_PHASE", "full") == "p2d":
                with nc.Block() as block:
                    p.build(block)
                return nc

            for i in range(2):
                p.op("v", lambda e, i=i: e.memset(krbf2[i][:, :], 0.0),
                     dep=None)
            pkfree = [0, 0]
            krtm = [0, 0]
            for t in range(NT):
                ti = t % 2
                pk = pk2[ti]
                krbf = krbf2[ti]
                for c in range(8):
                    p.op("t", lambda e, t=t, c=c, pk=pk: e.matmul(
                        pk[:, :],
                        xfm[c][:, t * 128:(t + 1) * 128],
                        wkr[:, c * 32:(c + 1) * 32],
                        start=(c == 0), stop=(c == 7)),
                        dep={**xfm_done, **wm,
                             "v": pkfree[ti]} if c == 0 else None)
                tm = p.cnt["t"]
                cs = cosr[:, t * 256: t * 256 + 16]
                sn = sinr[:, t * 256: t * 256 + 16]
                p.op("v", lambda e, cs=cs, pk=pk: e.tensor_mul(
                    rop[:, 0:16], pk[:, 0:16], cs),
                    dep={"t": tm, **csm, "v": p.cnt["v"]})
                p.op("v", lambda e, sn=sn, pk=pk: e.tensor_mul(
                    rop[:, 16:32], pk[:, 16:32], sn), dep={"v": p.cnt["v"]})
                p.op("v", lambda e: e.tensor_sub(rop[:, 32:48],
                                                 rop[:, 0:16], rop[:, 16:32]),
                     dep={"v": p.cnt["v"]})
                p.op("v", lambda e, cs=cs, pk=pk: e.tensor_mul(
                    rop[:, 0:16], pk[:, 16:32], cs), dep={"v": p.cnt["v"]})
                p.op("v", lambda e, sn=sn, pk=pk: e.tensor_mul(
                    rop[:, 16:32], pk[:, 0:16], sn), dep={"v": p.cnt["v"]})
                pkfree[ti] = p.cnt["v"]
                p.op("v", lambda e: e.tensor_add(rop[:, 48:64],
                                                 rop[:, 0:16], rop[:, 16:32]),
                     dep={"v": p.cnt["v"]})
                p.op("v", lambda e, krbf=krbf: e.tensor_copy(
                    krbf[:, 0:32], rop[:, 32:64]),
                    dep={"v": p.cnt["v"], "t": krtm[ti]})
                vm_ = p.cnt["v"]
                p.op("t", lambda e, ti=ti, krbf=krbf: e.transpose(
                    pb2[ti][0:65, :], krbf[:, :], idbf_sb[:, :]),
                     dep={"v": max(vm_, pbfree[ti])})
                krtm[ti] = p.cnt["t"]
                p.op("v", lambda e, t=t, ti=ti: e.tensor_copy(
                    krfm[0:32, t * 128:(t + 1) * 128], pb2[ti][0:32, 0:128]),
                    dep={"t": p.cnt["t"]})
                pbfree[ti] = p.cnt["v"]
            krm = p.cnt["v"]

            for t in range(NT):
                j = gj[0]
                gj[0] += 1
                bank = pp2[j % 2]
                for c in range(8):
                    p.op("t", lambda e, t=t, c=c, bank=bank: e.matmul(
                        bank[:, :],
                        xfm[c][:, t * 128:(t + 1) * 128],
                        wqr[:, c * 512:(c + 1) * 512],
                        start=(c == 0), stop=(c == 7)),
                        dep=M(xfm_done, wscale,
                              ppfree[j % 2]) if c == 0 else None)
                tm = p.cnt["t"]
                x1 = bank[:, :].rearrange("p (h d) -> p h d", d=32)[:, :, 0:16]
                x2 = bank[:, :].rearrange("p (h d) -> p h d", d=32)[:, :, 16:32]
                cs = cosr[:, t * 256:(t + 1) * 256].rearrange(
                    "p (h j) -> p h j", j=16)
                sn = sinr[:, t * 256:(t + 1) * 256].rearrange(
                    "p (h j) -> p h j", j=16)
                o1 = ropo[:, :].rearrange("p (h d) -> p h d", d=32)[:, :, 0:16]
                o2 = ropo[:, :].rearrange("p (h d) -> p h d", d=32)[:, :, 16:32]
                r1 = rop[:, 0:256].rearrange("p (h j) -> p h j", j=16)
                r2 = rop[:, 256:512].rearrange("p (h j) -> p h j", j=16)
                p.op("v", lambda e, x1=x1, cs=cs, r1=r1: e.tensor_mul(r1, x1, cs),
                     dep={"t": tm, **csm, "v": p.cnt["v"]})
                p.op("v", lambda e, x2=x2, sn=sn, r2=r2: e.tensor_mul(r2, x2, sn),
                     dep={"v": p.cnt["v"]})
                p.op("v", lambda e, o1=o1, r1=r1, r2=r2: e.tensor_sub(o1, r1, r2),
                     dep={"v": p.cnt["v"]})
                p.op("v", lambda e, x2=x2, cs=cs, r1=r1: e.tensor_mul(r1, x2, cs),
                     dep={"v": p.cnt["v"]})
                p.op("v", lambda e, x1=x1, sn=sn, r2=r2: e.tensor_mul(r2, x1, sn),
                     dep={"v": p.cnt["v"]})
                ppfree[j % 4] = {"v": p.cnt["v"]}
                p.op("v", lambda e, o2=o2, r1=r1, r2=r2: e.tensor_add(o2, r1, r2),
                     dep={"v": p.cnt["v"]})
                ropm = p.cnt["v"]
                for q in range(4):
                    si = q % 2
                    p.op("t", lambda e, q=q: e.transpose(
                        pb2[q % 2][:, :], ropo[:, q * 128:(q + 1) * 128],
                        idbf_sb[:, :]),
                        dep={"v": max(ropm, pbfree[q % 2])})
                    tm2 = p.cnt["t"]
                    p.op("v", lambda e, si=si, q=q, t=t: e.tensor_copy(
                        qrfm[q][:, t * 128:(t + 1) * 128], pb2[si][:, :]),
                        dep={"t": tm2})
                    pbfree[q % 2] = p.cnt["v"]

        kw_done = M(p.mark("kw0", "kw1", "kw2"), p.mark("t", "v", "s"))

        import os as _os2
        if _os2.environ.get("K_PHASE", "full") == "p2":
            with nc.Block() as block:
                p.build(block)
            return nc

        with ExitStack() as stB:
            EB = stB.enter_context
            y_all = [EB(nc.sbuf_tensor(f"yall{c}", [128, T], BF16))
                     for c in range(8)]
            wo = EB(nc.sbuf_tensor("wo_s", [128, 8 * 1024], BF16))
            stB1 = EB(ExitStack())
            EB1 = stB1.enter_context
            qsb2 = [EB1(nc.sbuf_tensor(f"qsb{i}", [97, T], BF16))
                    for i in range(3)]
            ksb2 = [EB1(nc.sbuf_tensor(f"ksb{i}", [97, T], BF16))
                    for i in range(3)]
            pbuf = EB1(nc.sbuf_tensor("pbuf", [128, 8192], BF16))
            m_all2 = [EB1(nc.sbuf_tensor(f"m_all{i}", [128, 65], F32))
                      for i in range(2)]
            mpart = EB1(nc.sbuf_tensor("mpart", [128, 4], F32))
            mT2 = [EB1(nc.sbuf_tensor(f"mT{i}", [16, 128], BF16))
                   for i in range(2)]
            dn64 = EB1(nc.sbuf_tensor("dn64", [128, 512], F32))
            ysb = EB1(nc.sbuf_tensor("ysb", [64, 512], F32))
            bcw = EB1(nc.sbuf_tensor("bcw", [128, 65], F32))
            psA2 = [EB1(nc.psum_tensor(f"psA{i}", [128, 512], F32))
                    for i in range(3)]
            psB = EB1(nc.psum_tensor("psB", [128, 2048], F32))
            psY = EB1(nc.psum_tensor("psY", [65, 512], F32))

            p.dma("g", wo[:, :].rearrange("p (c n) -> p c n", n=1024),
                  wo_in.rearrange("(c p) n -> p c n", p=128), ch="w1",
                  dep=kw_done)

            p.op("v", lambda e: e.memset(bcw[:, :], 0.0), dep=kw_done)
            p.op("v", lambda e: e.memset(bcw[64:65, :], 1.0),
                 dep={"v": p.cnt["v"]})
            p.op("v", lambda e: e.memset(dn64[:, :], 0.0), dep=kw_done)
            for i in range(2):
                p.op("v", lambda e, i=i: e.memset(m_all2[i][:, :], 0.0),
                     dep=kw_done)

            pieces = []
            for qt in range(NT):
                span = (qt + 1) * 128
                off = 0
                while off < span:
                    w = min(512, span - off)
                    pieces.append((qt, off, w, off == 0,
                                   off + w >= (qt + 1) * 128))
                    off += w

            state = {
                "qkfree": [{}, {}, {}],
                "ldm": [{}, {}, {}],
                "mfree": [0, 0],
                "redm": {},
                "pc": 0,
                "gpar": 0,
                "psB_guard": [{}, {}],
                "negm": [0, 0],
                "row96": [0, 0],
                "dnm": 0,
                "ysbm": 0,
                "ttm": 0,
                "bcm": 0,
                "rcpv": 0,
            }

            for par in range(3):
                p.dma("g", ksb2[par][64:96, :], krfm[0:32, :],
                      ch=f"hk{par}", dep=kw_done)
                p.op("v", lambda e, par=par: e.memset(
                    ksb2[par][96:97, :], 1.0), dep=kw_done)
            qkinit = {"v": p.cnt["v"]}

            def emit_loads(h):
                par = h % 3
                dep = dict(state["qkfree"][par])
                if h < 2:
                    dep = M(dep, kw_done)
                p.dma("g", qsb2[par][0:64, :], qhat[h, 0:64, :],
                      ch=f"hq{par}", dep=dep)
                p.dma("g", qsb2[par][64:96, :],
                      qrfm[h // 4][32 * (h % 4):32 * (h % 4) + 32, :],
                      ch=f"hq{par}", dep=None)
                p.dma("g", ksb2[par][0:64, :], khat[h, 0:64, :],
                      ch=f"hk{par}", dep=dep)
                state["ldm"][par] = M({f"hq{par}": p.cnt[f"hq{par}"],
                                       f"hk{par}": p.cnt[f"hk{par}"], **cm},
                                      qkinit)

            def gen_A(h):
                par = h % 3
                mpar = h % 2
                qsb, ksb = qsb2[par], ksb2[par]
                m_all = m_all2[mpar]
                ldm = state["ldm"][par]
                for qt, off, w, first, diag in pieces:
                    pc = state["pc"]
                    state["pc"] += 1
                    bank = psA2[pc % 3]
                    dep = dict(ldm)
                    if pc >= 3:
                        dep["v"] = max(dep.get("v", 0),
                                       state["redm"][pc - 3])
                    p.op("t", lambda e, qt=qt, off=off, w=w, bank=bank:
                         e.matmul(bank[:, 0:w],
                                  qsb[0:96, qt * 128:(qt + 1) * 128],
                                  ksb[0:96, off:off + w],
                                  start=True, stop=True), dep=dep)
                    tm = p.cnt["t"]
                    vdep = {"t": tm, "v": p.cnt["v"]}
                    if diag:
                        dcol = qt * 128 - off
                        p.op("v", lambda e, bank=bank, dcol=dcol: e.tensor_add(
                            bank[:, dcol:dcol + 128],
                            bank[:, dcol:dcol + 128], m0_sb[:, :]), dep=vdep)
                        vdep = {"v": p.cnt["v"]}
                    if first:
                        dd = dict(vdep)
                        dd["t"] = max(dd.get("t", 0), state["mfree"][mpar])
                        p.op("v", lambda e, qt=qt, w=w, bank=bank: e.reduce_max(
                            m_all[:, qt:qt + 1], bank[:, 0:w], axis=AX.X),
                            dep=dd)
                    else:
                        p.op("v", lambda e, w=w, bank=bank, pc=pc: e.reduce_max(
                            mpart[:, pc % 4:pc % 4 + 1], bank[:, 0:w],
                            axis=AX.X), dep=vdep)
                        p.op("v", lambda e, qt=qt, pc=pc: e.tensor_max(
                            m_all[:, qt:qt + 1], m_all[:, qt:qt + 1],
                            mpart[:, pc % 4:pc % 4 + 1]),
                            dep={"v": p.cnt["v"]})
                    state["redm"][pc] = p.cnt["v"]
                    yield
                p.op("t", lambda e: e.transpose(psB[0:65, 0:128], m_all[:, :],
                                                idf_sb[:, :]),
                     dep=M({"v": state["redm"][state["pc"] - 1], **cm},
                           state["psB_guard"][0]))
                state["mfree"][mpar] = p.cnt["t"]
                p.op("s", lambda e: e.activation(mT2[mpar][:, :],
                                                 psB[0:16, 0:128],
                                                 AF.Copy, scale=-1.0),
                     dep={"t": p.cnt["t"], "mw": p.cnt.get("mw", 0)})
                state["negm"][mpar] = p.cnt["s"]
                state["psB_guard"][0] = M(state["psB_guard"][0],
                                          {"s": p.cnt["s"]})
                p.dma("y", qhat[h, 96:97, :].rearrange("o (a b) -> (o a) b",
                                                       b=128),
                      mT2[mpar][:, :], ch="mw", dep={"s": p.cnt["s"]})
                p.dma("y", qsb[96:97, :], qhat[h, 96:97, :], ch="r96",
                      dep={"mw": p.cnt["mw"]})
                state["row96"][par % 2] = p.cnt["r96"]

            def gen_B(h):
                par = h % 3
                mpar = h % 2
                qsb, ksb = qsb2[par], ksb2[par]
                firstmm = True
                for b in range(4):
                    nch = 4 * (b + 1)
                    for g0 in range(0, nch, 2):
                        ng = min(2, nch - g0)
                        gi = state["gpar"]
                        state["gpar"] = 1 - gi
                        for ci in range(ng):
                            ct = g0 + ci
                            dep = None
                            if ci == 0:
                                dep = dict(state["psB_guard"][gi])
                                if firstmm:
                                    dep = M(dep, {"s": state["negm"][mpar]})
                                    dep["r96"] = state["row96"][par % 2]
                                    firstmm = False
                            cut = max(0, 128 * ct - 512 * b)
                            p.op("t", lambda e, ct=ct, ci=ci, b=b, gi=gi,
                                 cut=cut: e.matmul(
                                     psB[:, gi * 1024 + ci * 512 + cut:
                                         gi * 1024 + (ci + 1) * 512],
                                     ksb[:, ct * 128:(ct + 1) * 128],
                                     qsb[:, b * 512 + cut:(b + 1) * 512],
                                     start=True, stop=True), dep=dep,
                                 sem="tb")
                        p.op("s", lambda e, g0=g0, ng=ng, gi=gi: e.activation(
                            pbuf[:, g0 * 512:(g0 + ng) * 512],
                            psB[:, gi * 1024:gi * 1024 + ng * 512], AF.Exp),
                            dep={"tb": p.cnt["tb"], "g": p.cnt["g"]},
                            sem="sb")
                        state["psB_guard"][gi] = {"sb": p.cnt["sb"]}
                        em = p.cnt["sb"]
                        for ci in range(ng):
                            ct = g0 + ci
                            if ct >= 4 * b:
                                p.op("g", lambda e, ct=ct, b=b: e.affine_select(
                                    pbuf[:, ct * 512:(ct + 1) * 512],
                                    pbuf[:, ct * 512:(ct + 1) * 512],
                                    pattern=[[1, 512]], compare_op=ALU.is_ge,
                                    fill=0.0, base=512 * b - 128 * ct,
                                    channel_multiplier=-1), dep={"sb": em})
                        yield
                    for ct in range(nch):
                        cut = max(0, 128 * ct - 512 * b)
                        dep = None
                        if ct == 0:
                            dep = M({"sb": max(p.cnt["sb"], state["ysbm"]),
                                     "g": p.cnt["g"],
                                     "vb": state["ttm"],
                                     "tb": state["bcm"]}, vhat_done)
                        p.op("t", lambda e, ct=ct, h=h, nch=nch, cut=cut:
                             e.matmul(
                                 psY[:, cut:512],
                                 vhat[ct][:, h * 65:(h + 1) * 65],
                                 pbuf[:, ct * 512 + cut:(ct + 1) * 512],
                                 start=(ct == 0), stop=(ct == nch - 1)),
                             dep=dep, sem="tb")
                    vml = p.cnt["tb"]
                    p.op("v", lambda e, h=h: e.tensor_scalar_add(
                        dn64[64:65, :], psY[64:65, :],
                        c_all[64:65, h:h + 1]),
                        dep={"tb": max(vml, state["bcm"]),
                             "vb": state["rcpv"]}, sem="vb")
                    p.op("s", lambda e: e.activation(ysb[:, :], psY[0:64, :],
                                                     AF.Copy),
                         dep={"tb": vml, "vb": state["ttm"]}, sem="sb")
                    state["ysbm"] = p.cnt["sb"]
                    p.op("v", lambda e: e.reciprocal(dn64[64:65, :],
                                                     dn64[64:65, :]),
                         dep={"vb": p.cnt["vb"]},
                         sem="vb")
                    state["rcpv"] = p.cnt["vb"]
                    rcm2 = p.cnt["vb"]
                    p.op("t", lambda e: e.matmul(psY[0:65, :],
                                                 bcw[:, :],
                                                 dn64[:, :],
                                                 start=True, stop=True),
                         dep={"vb": rcm2, "sb": state["ysbm"]},
                         sem="tb")
                    state["bcm"] = p.cnt["tb"]
                    p.op("v", lambda e, h=h, b=b: e.tensor_mul(
                        y_all[h // 2][64 * (h % 2):64 * (h % 2) + 64,
                                      b * 512:(b + 1) * 512],
                        ysb[:, :], psY[0:64, :]),
                        dep={"sb": p.cnt["sb"], "tb": p.cnt["tb"],
                             "vb": p.cnt.get("vb", 0)}, sem="vb")
                    state["ttm"] = p.cnt["vb"]
                    yield
                state["qkfree"][par] = {"tb": p.cnt["tb"]}

            def drain(gen, n):
                for _ in range(n):
                    try:
                        next(gen)
                    except StopIteration:
                        return False
                return True

            import os as _os
            ILEAVE = _os.environ.get("K_ILEAVE", "1") == "1"
            emit_loads(0)
            emit_loads(1)
            ga = gen_A(0)
            drain(ga, 10 ** 9)
            for h in range(H):
                if h + 2 < H:
                    emit_loads(h + 2)
                if h + 1 < H:
                    ga = gen_A(h + 1)
                else:
                    ga = None
                gb = gen_B(h)
                if not ILEAVE:
                    if ga is not None:
                        drain(ga, 10 ** 9)
                    drain(gb, 10 ** 9)
                else:
                    while True:
                        alive_a = drain(ga, 3) if ga is not None else False
                        alive_b = drain(gb, 1)
                        if not alive_b and not alive_a:
                            break

            yall_done = {"vb": p.cnt.get("vb", 0), "v": p.cnt["v"]}

            if _os2.environ.get("K_PHASE", "full") == "attn":
                for c in range(8):
                    for hf in range(2):
                        p.dma("g", xp_out.rearrange("(c p) n -> c p n", p=128)
                              [2 * c + hf, :, :],
                              y_all[c][:, hf * 1024:(hf + 1) * 1024],
                              ch="dbg", dep="all")
                with nc.Block() as block:
                    p.build(block)
                return nc

            stB1.close()
            pend = p.mark("t", "v", "s", "g")
            with ExitStack() as stC:
                EC = stC.enter_context
                aofm = EC(nc.sbuf_tensor("aofm", [128, 8 * 512], BF16))
                aot = EC(nc.sbuf_tensor("aot", [128, 1024], F32))
                xt2 = [EC(nc.sbuf_tensor(f"xt2_{i}", [128, 1024], F32))
                       for i in range(2)]
                xp22 = [EC(nc.sbuf_tensor(f"xp2_{i}", [128, 1024], F32))
                        for i in range(2)]
                x2bf2 = [EC(nc.sbuf_tensor(f"x2bf{i}", [128, 1024], BF16))
                         for i in range(2)]
                x2fm2 = [EC(nc.sbuf_tensor(f"x2fm{i}", [128, 8 * 128], BF16))
                         for i in range(2)]
                gexp = EC(nc.sbuf_tensor("gexp", [128, 16], F32))
                grt2 = [EC(nc.sbuf_tensor(f"grt{i}", [128, 16], F32))
                        for i in range(2)]
                oneh2 = [EC(nc.sbuf_tensor(f"oneh{i}", [128, 16], F32))
                         for i in range(2)]
                xacc = EC(nc.sbuf_tensor("xacc", [128, 16], F32))
                lloct2 = [EC(nc.sbuf_tensor(f"lloct{i}", [128, 16], F32))
                          for i in range(2)]
                gvec2 = [EC(nc.sbuf_tensor(f"gvec{i}", [128, 8], F32))
                         for i in range(2)]
                cnt_sb = EC(nc.sbuf_tensor("cnt_sb", [1, 16], F32))
                pw2 = [EC(nc.psum_tensor(f"pw{i}", [128, 512], F32))
                       for i in range(2)]
                pao2 = [EC(nc.psum_tensor(f"pao{i}", [128, 128], BF16))
                        for i in range(2)]
                psG2 = [EC(nc.psum_tensor(f"psG{i}", [128, 16], F32))
                        for i in range(2)]
                psL = EC(nc.psum_tensor("psL", [128, 16], F32))
                psC = EC(nc.psum_tensor("psC", [1, 16], F32))

                p.op("v", lambda e: e.memset(xacc[:, :], 0.0), dep=pend)
                pwfree = [0, 0]
                paofree = [0, 0]
                xfree = [dict(pend), dict(pend)]
                xpfree = [{}, {}]
                x2bfree = [0, 0]
                x2fmfree = [0, 0]
                psGfree = [{}, {}]
                gvfree = [{}, {}]
                wom = p.mark("w1")
                for B in range(4):
                    for o in range(8):
                        for c in range(8):
                            dep = None
                            if c == 0:
                                dep = {**yall_done, **wom, **cm,
                                       "s": pwfree[o % 2]}
                            p.op("t", lambda e, B=B, o=o, c=c: e.matmul(
                                pw2[o % 2][:, :],
                                wo[:, c * 1024 + o * 128:
                                   c * 1024 + o * 128 + 128],
                                y_all[c][:, B * 512:(B + 1) * 512],
                                start=(c == 0), stop=(c == 7)), dep=dep)
                        p.op("s", lambda e, o=o: e.activation(
                            aofm[:, o * 512:(o + 1) * 512],
                            pw2[o % 2][:, :], AF.Copy),
                            dep={"t": p.cnt["t"]})
                        pwfree[o % 2] = p.cnt["s"]
                    aom = p.cnt["s"]
                    for u in range(4):
                        t = 4 * B + u
                        i = t % 2
                        psG = psG2[i]
                        gvec = gvec2[i]
                        grt = grt2[i]
                        oneh = oneh2[i]
                        for o in range(8):
                            p.op("t", lambda e, o=o, u=u: e.transpose(
                                pao2[o % 2][:, :],
                                aofm[:, o * 512 + u * 128:
                                     o * 512 + u * 128 + 128],
                                idbf_sb[:, :]),
                                dep={"s": aom, "v": paofree[o % 2]})
                            tm = p.cnt["t"]
                            p.op("v", lambda e, o=o: e.tensor_copy(
                                aot[:, o * 128:(o + 1) * 128],
                                pao2[o % 2][:, :]),
                                dep={"t": tm, "v": p.cnt["v"]})
                            paofree[o % 2] = p.cnt["v"]
                        p.dma("g", xt2[i][:, :],
                              x_in[t * 128:(t + 1) * 128, :],
                              ch=f"x{i}", dep=xfree[i])
                        p.op("v", lambda e, i=i: e.tensor_add(
                            xp22[i][:, :], xt2[i][:, :], aot[:, :]),
                            dep=M({f"x{i}": p.cnt[f"x{i}"],
                                   "v": p.cnt["v"]}, xpfree[i]))
                        xfree[i] = {"v": p.cnt["v"]}
                        p.dma("g", xp_out[t * 128:(t + 1) * 128, :],
                              xp22[i][:, :], ch="ow0", dep={"v": p.cnt["v"]})
                        lnr = ln_pass(p, xp22[i][:, :], scr2[i], vec2[i],
                                      x2bf2[i][:, :],
                                      dep={"v": p.cnt["v"], "s": p.cnt["s"]},
                                      dep_last={"t": x2bfree[i],
                                                "ow1": p.cnt.get("ow1", 0)})
                        xpfree[i] = M(lnr, {"ow0": p.cnt["ow0"]})
                        lnm = p.cnt["v"]
                        p.dma("g", x2_out[t * 128:(t + 1) * 128, :],
                              x2bf2[i][:, :], ch="ow1", dep={"v": lnm})
                        for c in range(8):
                            p.op("t", lambda e, c=c, i=i: e.transpose(
                                pao2[c % 2][:, :],
                                x2bf2[i][:, c * 128:(c + 1) * 128],
                                idbf_sb[:, :]),
                                dep={"v": max(lnm, paofree[c % 2])})
                            tm = p.cnt["t"]
                            p.op("v", lambda e, c=c, i=i: e.tensor_copy(
                                x2fm2[i][:, c * 128:(c + 1) * 128],
                                pao2[c % 2][:, :]),
                                dep={"t": max(tm, x2fmfree[i]),
                                     "v": p.cnt["v"]})
                            paofree[c % 2] = p.cnt["v"]
                        x2bfree[i] = p.cnt["t"]
                        x2m = p.cnt["v"]
                        for c in range(8):
                            p.op("t", lambda e, c=c, i=i, psG=psG: e.matmul(
                                psG[:, :],
                                x2fm2[i][:, c * 128:(c + 1) * 128],
                                wg[:, c * 16:(c + 1) * 16],
                                start=(c == 0), stop=(c == 7)),
                                dep=M({"v": x2m, **wm},
                                      psGfree[i]) if c == 0 else None)
                        gmm = p.cnt["t"]
                        x2fmfree[i] = gmm
                        gm = gvec[:, 0:1]
                        gmn = gvec[:, 1:2]
                        gsum = gvec[:, 2:3]
                        gsr = gvec[:, 3:4]
                        idxv = gvec[:, 5:6]
                        p.op("v", lambda e, gm=gm, psG=psG: e.reduce_max(
                            gm, psG[:, :], axis=AX.X),
                            dep=M({"t": gmm, "v": p.cnt["v"]}, gvfree[i]))
                        p.op("v", lambda e, gm=gm, gmn=gmn:
                             e.tensor_scalar_mul(gmn, gm, -1.0),
                             dep={"v": p.cnt["v"]})
                        gmnm = p.cnt["v"]
                        p.op("s", lambda e, gmn=gmn, gsum=gsum, psG=psG:
                             e.activation(
                                 gexp[:, :], psG[:, :], AF.Exp, bias=gmn,
                                 accum_out=gsum),
                             dep={"v": gmnm, "s": p.cnt["s"], "t": gmm})
                        sm = p.cnt["s"]
                        p.op("v", lambda e, grt=grt, gm=gm, psG=psG:
                             e.tensor_scalar_sub(grt[:, :], psG[:, :], gm),
                             dep={"v": p.cnt["v"]})
                        p.op("v", lambda e, grt=grt: e.tensor_scalar(
                            grt[:, :], grt[:, :], -1e30, 1.0,
                            ALU.mult, ALU.min), dep={"v": p.cnt["v"]})
                        p.op("v", lambda e, grt=grt: e.scalar_tensor_tensor(
                            grt[:, :], grt[:, :], -1.0, iom99_sb[:, :],
                            ALU.mult, ALU.mult), dep={"v": p.cnt["v"]})
                        p.op("v", lambda e, grt=grt: e.tensor_add(
                            grt[:, :], grt[:, :], iota16_sb[:, :]),
                            dep={"v": p.cnt["v"]})
                        p.op("v", lambda e, grt=grt, idxv=idxv:
                             e.tensor_reduce(
                                 idxv, grt[:, :], axis=AX.X, op=ALU.min),
                             dep={"v": p.cnt["v"]})
                        ivm = p.cnt["v"]
                        p.dma("y", idx_out[t * 128:(t + 1) * 128, :], idxv,
                              ch="ow3", dep={"v": ivm})
                        p.op("v", lambda e, oneh=oneh, idxv=idxv:
                             e.tensor_scalar_mul(
                                 oneh[:, :], ones16_sb[:, :], idxv),
                             dep={"v": ivm, "t": p.cnt["t"]})
                        p.op("v", lambda e, oneh=oneh: e.tensor_sub(
                            oneh[:, :], oneh[:, :], iota16_sb[:, :]),
                            dep={"v": p.cnt["v"]})
                        p.op("v", lambda e, oneh=oneh: e.tensor_mul(
                            oneh[:, :], oneh[:, :], oneh[:, :]),
                            dep={"v": p.cnt["v"]})
                        p.op("v", lambda e, oneh=oneh: e.tensor_scalar(
                            oneh[:, :], oneh[:, :], 1.0, -1.0,
                            ALU.min, ALU.mult), dep={"v": p.cnt["v"]})
                        p.op("v", lambda e, oneh=oneh: e.tensor_scalar_add(
                            oneh[:, :], oneh[:, :], 1.0),
                            dep={"v": p.cnt["v"]})
                        ohm = p.cnt["v"]
                        p.op("t", lambda e, oneh=oneh: e.matmul(
                            psL[:, :], trilS_sb[:, :], oneh[:, :],
                            start=True, stop=False),
                            dep={"v": ohm})
                        p.op("t", lambda e: e.matmul(
                            psL[:, :], onesf_sb[:, :], xacc[:, :],
                            start=False, stop=True), dep=None)
                        tm = p.cnt["t"]
                        p.op("v", lambda e, i=i: e.tensor_copy(
                            lloct2[i][:, :], psL[:, :]),
                             dep={"t": tm, "ow3": p.cnt.get("ow3", 0),
                                  "v": p.cnt["v"]})
                        p.dma("y", lloc_out[t, :, :], lloct2[i][:, :],
                              ch="ow3", dep={"v": p.cnt["v"]})
                        p.op("v", lambda e, oneh=oneh: e.tensor_add(
                            xacc[:, :], xacc[:, :], oneh[:, :]),
                            dep={"t": tm, "v": p.cnt["v"]})
                        p.op("v", lambda e, gsr=gsr, gsum=gsum: e.reciprocal(
                            gsr, gsum), dep={"s": sm, "v": p.cnt["v"]})
                        p.dma("y", gs_out[t * 128:(t + 1) * 128, :], gsr,
                              ch="ow2", dep={"v": p.cnt["v"]})
                        psGfree[i] = {"v": p.cnt["v"], "s": sm}
                        gvfree[i] = {"s": sm, "ow2": p.cnt["ow2"],
                                     "ow3": p.cnt["ow3"]}
                p.op("t", lambda e: e.matmul(psC[:, :], onescol_sb[:, :],
                                             xacc[:, :], start=True,
                                             stop=True),
                     dep={"v": p.cnt["v"], **cm})
                p.op("v", lambda e: e.tensor_copy(cnt_sb[:, :], psC[:, :]),
                     dep={"t": p.cnt["t"]})
                p.dma("y", cnt_out, cnt_sb[:, :], ch="ow0",
                      dep={"v": p.cnt["v"]})

        with nc.Block() as block:
            p.build(block)
    return nc

def build_L2():
    nc = bass.Bass("TRN2", target_bir_lowering=False, debug=False, num_swdge_queues=4)
    cnts_in = nc.dram_tensor("cnts", [NCORE, NEXP], F32, kind="ExternalInput").ap()
    lloc_in = nc.dram_tensor("lloc", [NT, 128, NEXP], F32, kind="ExternalInput").ap()
    idx_in = nc.dram_tensor("idxT", [128, NT], F32, kind="ExternalInput").ap()
    toki_in = nc.dram_tensor("tokiT", [128, NT], F32, kind="ExternalInput").ap()
    sel_in = nc.dram_tensor("sel", [NCORE, 1], F32, kind="ExternalInput").ap()

    gpart_out = nc.dram_tensor("gpart", [NCORE * T, 1], F32, kind="ExternalOutput").ap()
    slotp_out = nc.dram_tensor("slotp", [T, 1], F32, kind="ExternalOutput").ap()

    iota16 = nc.inline_tensor(np.tile(np.arange(16, dtype=np.float32), (128, 1)),
                              "iota16").ap()
    ones16 = nc.inline_tensor(np.ones((128, 16), np.float32), "ones16").ap()
    onesrow128 = nc.inline_tensor(np.ones((1, 128), np.float32), "or128").ap()
    idf2 = nc.inline_tensor(np.eye(128, dtype=np.float32), "idf2").ap()

    with ExitStack() as st:
        EE = st.enter_context
        p = P(nc, st)
        cnts = EE(nc.sbuf_tensor("cnts_s", [NCORE, NEXP], F32))
        or128_sb = EE(nc.sbuf_tensor("or128_sb", [1, 128], F32))
        sel = EE(nc.sbuf_tensor("sel_s", [NCORE, 1], F32))
        offr = EE(nc.sbuf_tensor("offr", [1, 16], F32))
        offb = EE(nc.sbuf_tensor("offb", [128, 16], F32))
        idxT = EE(nc.sbuf_tensor("idxT_s", [128, NT], F32))
        tokiT = EE(nc.sbuf_tensor("tokiT_s", [128, NT], F32))
        toki_u = EE(nc.sbuf_tensor("toki_u", [128, NT], F32))
        lloct = EE(nc.sbuf_tensor("lloct", [128, 16], F32))
        oneh = EE(nc.sbuf_tensor("oneh", [128, 16], F32))
        big = EE(nc.sbuf_tensor("big", [128, 128], F32))
        slots = EE(nc.sbuf_tensor("slots", [128, NT], F32))
        slots_u = EE(nc.sbuf_tensor("slots_u", [128, NT], U32))
        v1 = EE(nc.sbuf_tensor("v1", [128, 8], F32))
        i16 = EE(nc.sbuf_tensor("i16", [128, 16], F32))
        o16 = EE(nc.sbuf_tensor("o16", [128, 16], F32))
        idf_sb2 = EE(nc.sbuf_tensor("idf_sb2", [128, 128], F32))
        slotsT = EE(nc.sbuf_tensor("slotsT", [16, 128], F32))
        psO = EE(nc.psum_tensor("psO", [1, 16], F32))
        psB = EE(nc.psum_tensor("psB", [128, 16], F32))
        psT = EE(nc.psum_tensor("psT", [16, 128], F32))

        p.dma("y", cnts[:, :], cnts_in)
        p.dma("y", sel[:, :], sel_in)
        p.dma("y", idxT[:, :], idx_in)
        p.dma("y", tokiT[:, :], toki_in)
        p.dma("y", i16[:, :], iota16)
        p.dma("y", o16[:, :], ones16)
        p.dma("y", or128_sb[:, :], onesrow128)
        p.dma("y", idf_sb2[:, :], idf2)
        p.op("t", lambda e: e.matmul(psO[:, :], sel[:, :], cnts[:, :],
                                     start=True, stop=True))
        p.op("v", lambda e: e.tensor_copy(offr[:, :], psO[:, :]))
        p.op("t", lambda e: e.matmul(psB[:, :], or128_sb[:, :], offr[:, :],
                                     start=True, stop=True))
        p.op("v", lambda e: e.tensor_copy(offb[:, :], psB[:, :]))
        p.op("v", lambda e: e.memset(big[:, :], BIGSLOT))
        for r in range(NCORE):
            p.dma("y", gpart_out[r * T:(r + 1) * T, :]
                  .rearrange("(a b) o -> a (b o)", b=128), big[0:16, :])

        for t in range(NT):
            idxv = idxT[:, t:t + 1]
            gl = v1[:, 0:1]
            ofs = v1[:, 1:2]
            pen = v1[:, 2:3]
            p.dma("y", lloct[:, :], lloc_in[t, :, :])
            p.op("v", lambda e, idxv=idxv: e.tensor_scalar_mul(oneh[:, :],
                                                               o16[:, :], idxv))
            p.op("v", lambda e: e.tensor_sub(oneh[:, :], oneh[:, :], i16[:, :]))
            p.op("v", lambda e: e.tensor_mul(oneh[:, :], oneh[:, :], oneh[:, :]))
            p.op("v", lambda e: e.tensor_scalar_min(oneh[:, :], oneh[:, :], 1.0))
            p.op("s", lambda e: e.activation(oneh[:, :], oneh[:, :], AF.Copy,
                                             scale=-1.0, bias=1.0))
            p.op("v", lambda e: e.tensor_mul(lloct[:, :], lloct[:, :], oneh[:, :]))
            p.op("v", lambda e, gl=gl: e.reduce_sum(gl, lloct[:, :], axis=AX.X))
            p.op("v", lambda e: e.tensor_mul(oneh[:, :], offb[:, :], oneh[:, :]))
            p.op("v", lambda e, ofs=ofs: e.reduce_sum(ofs, oneh[:, :], axis=AX.X))
            p.op("v", lambda e, gl=gl, ofs=ofs: e.tensor_add(gl, gl, ofs))
            p.op("v", lambda e, pen=pen, gl=gl: e.tensor_scalar_add(
                pen, gl, -(CAP - 1.0)))
            p.op("v", lambda e, pen=pen: e.tensor_scalar_max(pen, pen, 0.0))
            p.op("v", lambda e, pen=pen: e.tensor_scalar_min(pen, pen, 1.0))
            p.op("s", lambda e, pen=pen: e.activation(pen, pen, AF.Copy,
                                                      scale=BIGSLOT))
            sl = slots[:, t:t + 1]
            p.op("v", lambda e, sl=sl, idxv=idxv: e.tensor_scalar_mul(
                sl, idxv, float(CAP)))
            p.op("v", lambda e, sl=sl, gl=gl: e.tensor_add(sl, sl, gl))
            p.op("v", lambda e, sl=sl, pen=pen: e.tensor_add(sl, sl, pen))
        p.op("v", lambda e: e.tensor_copy(slots_u[:, :], slots[:, :]))
        p.op("t", lambda e: e.transpose(psT[:, :], slots[:, :], idf_sb2[:, :]))
        p.op("v", lambda e: e.tensor_copy(slotsT[:, :], psT[0:16, :]))
        p.dma("y", slotp_out.rearrange("(a b) o -> a (b o)", b=128),
              slotsT[:, :])
        for t in range(NT):
            p.idma("g", lambda e, t=t: e.indirect_dma_start(
                out=gpart_out[0:128, :],
                out_offset=IOA(slots_u[:, t:t + 1], 0),
                in_=tokiT[:, t:t + 1],
                in_offset=None,
                bounds_check=NCORE * T - 1, oob_is_err=False))

        with nc.Block() as block:
            p.build(block)
    return nc


def build_L3():
    nc = bass.Bass("TRN2", target_bir_lowering=False, debug=False, num_swdge_queues=4)
    xall_in = nc.dram_tensor("xall", [NCORE * T, E], BF16, kind="ExternalInput").ap()
    gm_in = nc.dram_tensor("gmy", [128, NCORE * NT], F32, kind="ExternalInput").ap()
    w1_in = nc.dram_tensor("w1own", [2, E, FF], F32, kind="ExternalInput").ap()
    w2_in = nc.dram_tensor("w2own", [2, FF, E], F32, kind="ExternalInput").ap()

    co_out = nc.dram_tensor("contrib", [T, E], BF16, kind="ExternalOutput").ap()

    idbf = nc.inline_tensor(np.eye(128, dtype=ml_dtypes.bfloat16), "idbf").ap()

    with ExitStack() as st:
        EE = st.enter_context
        p = P(nc, st)
        idbf_sb = EE(nc.sbuf_tensor("idbf_sb", [128, 128], BF16))
        gm = EE(nc.sbuf_tensor("gm_s", [128, NCORE * NT], F32))
        gmin_u = EE(nc.sbuf_tensor("gmin_u", [128, NT], U32))
        w2sb = EE(nc.sbuf_tensor("w2sb", [128, 32 * 1024], BF16))
        w1sb2 = [EE(nc.sbuf_tensor(f"w1sb{i}", [128, 8 * 2048], BF16))
                 for i in range(2)]
        hsb = [EE(nc.sbuf_tensor(f"hsb{i}", [128, 16 * 512], BF16))
               for i in range(2)]
        xd_tm4 = [EE(nc.sbuf_tensor(f"xd_tm{i}", [128, 1024], BF16))
                  for i in range(4)]
        xd_fm2 = [[EE(nc.sbuf_tensor(f"xdfm{i}_{c}", [128, 512], BF16))
                   for c in range(8)] for i in range(2)]
        ost2 = [EE(nc.sbuf_tensor(f"ost{i}", [128, 512], BF16))
                for i in range(2)]
        po = [EE(nc.psum_tensor(f"po{i}", [128, 512], F32)) for i in range(6)]
        pb2 = [EE(nc.psum_tensor(f"pbx{i}", [128, 128], BF16))
               for i in range(2)]

        p.dma("y", idbf_sb[:, :], idbf, ch="c0", dep=None)
        p.dma("y", gm[:, :], gm_in, ch="c0", dep=None)
        cm = p.mark("c0")
        p.op("v", lambda e: e.tensor_tensor(
            gm[:, 0:NT], gm[:, 0:NT], gm[:, NT:2 * NT], ALU.min), dep=cm)
        for r in range(2, NCORE):
            p.op("v", lambda e, r=r: e.tensor_tensor(
                gm[:, 0:NT], gm[:, 0:NT],
                gm[:, r * NT:(r + 1) * NT], ALU.min), dep={"v": p.cnt["v"]})
        p.op("v", lambda e: e.tensor_copy(gmin_u[:, :], gm[:, 0:NT]),
             dep={"v": p.cnt["v"]})
        gum = {"v": p.cnt["v"]}

        p.dma("g", w1sb2[0][:, :].rearrange("p (c n) -> p c n", n=2048),
              w1_in[0, :, 0:2048].rearrange("(c p) n -> p c n", p=128),
              ch="wb", dep=None)
        wam = [None] * 8
        for cch in range(8):
            ch = f"wa{cch % 2}"
            p.dma("g", w2sb[:, cch * 4096:(cch + 1) * 4096]
                  .rearrange("p (c n) -> p c n", n=1024),
                  w2_in[0, cch * 512:(cch + 1) * 512, :]
                  .rearrange("(c p) n -> p c n", p=128),
                  ch=ch, dep=None)
            wam[cch] = (ch, p.cnt[ch])

        xdm = [0, 0, 0, 0]
        relum = [0, 0, 0, 0, 0, 0]
        outm_prev = 0
        ostm = [{}, {}]
        gimk = {}
        xdc2 = [0, 0]
        pbfree = [0, 0]

        def emit_gathers(blk):
            for u in range(4):
                j = blk * 4 + u
                p.op("v", lambda e, u=u: e.memset(xd_tm4[u][:, :], 0.0),
                     dep={"t": xdm[u], "v": p.cnt["v"]})
                p.idma("g", lambda e, j=j, u=u: e.indirect_dma_start(
                    out=xd_tm4[u][:, :], out_offset=None,
                    in_=xall_in[0:128, :],
                    in_offset=IOA(gmin_u[:, j:j + 1], 0),
                    bounds_check=NCORE * T - 1, oob_is_err=False),
                    ch="gi", dep=M(gum, {"v": p.cnt["v"]}))
                gimk[u] = p.cnt["gi"]

        def emit_gtrans(blk):
            par = blk % 2
            for u in range(4):
                for c in range(8):
                    p.op("t", lambda e, c=c, u=u: e.transpose(
                        pb2[c % 2][:, :],
                        xd_tm4[u][:, c * 128:(c + 1) * 128],
                        idbf_sb[:, :]),
                        dep={"gi": gimk[u], "v": max(p.cnt["v"],
                                                     pbfree[c % 2]), **cm})
                    tm = p.cnt["t"]
                    p.op("v", lambda e, c=c, u=u, par=par: e.tensor_copy(
                        xd_fm2[par][c][:, u * 128:(u + 1) * 128],
                        pb2[c % 2][:, :]),
                        dep={"t": tm})
                    pbfree[c % 2] = p.cnt["v"]
                xdm[u] = p.cnt["t"]
            xdc2[par] = p.cnt["v"]

        emit_gathers(0)
        emit_gtrans(0)
        p.dma("g", w1sb2[1][:, :].rearrange("p (c n) -> p c n", n=2048),
              w1_in[0, :, 2048:4096].rearrange("(c p) n -> p c n", p=128),
              ch="wb1", dep=None)
        w1ld = [("wb", p.cnt["wb"]), ("wb1", p.cnt["wb1"])]
        for ee in range(2):
            if ee == 1:
                for cch in range(8):
                    ch = f"wa{cch % 2}"
                    p.dma("g", w2sb[:, cch * 4096:(cch + 1) * 4096]
                          .rearrange("p (c n) -> p c n", n=1024),
                          w2_in[1, cch * 512:(cch + 1) * 512, :]
                          .rearrange("(c p) n -> p c n", p=128),
                          ch=ch, dep={"t": outm_prev} if cch < 2 else None)
                    wam[cch] = (ch, p.cnt[ch])
            for B in range(2):
                blk = ee * 2 + B
                par = blk % 2
                xdc = xdc2[par]
                if blk + 1 < 4:
                    emit_gathers(blk + 1)
                for fh in range(2):
                    lch, ldm = w1ld[fh]
                    for f in range(16):
                        pi = 4 + (f % 2)
                        for c in range(8):
                            dep = None
                            if c == 0:
                                dep = {"s": relum[pi], "v": xdc,
                                       lch: ldm}
                            p.op("t", lambda e, f=f, c=c, pi=pi, fh=fh,
                                 par=par: e.matmul(
                                     po[pi][:, :],
                                     w1sb2[fh][:, c * 2048 + f * 128:
                                               c * 2048 + f * 128 + 128],
                                     xd_fm2[par][c][:, :],
                                     start=(c == 0), stop=(c == 7)), dep=dep)
                        p.op("s", lambda e, f=f, fh=fh, pi=pi: e.activation(
                            hsb[fh][:, f * 512:(f + 1) * 512],
                            po[pi][:, :], AF.Relu), dep={"t": p.cnt["t"]})
                        relum[pi] = p.cnt["s"]
                    if B == 1 and ee == 0:
                        ch = "wb" if fh == 0 else "wb1"
                        p.dma("g", w1sb2[fh][:, :]
                              .rearrange("p (c n) -> p c n", n=2048),
                              w1_in[1, :, fh * 2048:(fh + 1) * 2048]
                              .rearrange("(c p) n -> p c n", p=128),
                              ch=ch, dep={"t": p.cnt["t"]})
                        w1ld[fh] = (ch, p.cnt[ch])
                hrel = p.cnt["s"]
                if blk + 1 < 4:
                    emit_gtrans(blk + 1)
                for half in range(2):
                    pairs = [(u, dh) for u in range(4) for dh in range(2)][
                        half * 4:(half + 1) * 4]
                    for pi, (u, dh) in enumerate(pairs):
                        for fh in range(2):
                            for f in range(16):
                                dep = None
                                if pi == 0:
                                    wch, wcm = wam[(fh * 16 + f) // 4]
                                    dep = {wch: wcm}
                                if fh == 0 and f == 0:
                                    dep = M(dep or {},
                                            {"s": max(relum[pi], hrel)})
                                p.op("t", lambda e, u=u, dh=dh, fh=fh, f=f,
                                     pi=pi: e.matmul(
                                         po[pi][:, :],
                                         hsb[fh][:, f * 512 + u * 128:
                                                 f * 512 + u * 128 + 128],
                                         w2sb[:, (fh * 16 + f) * 1024 +
                                              dh * 512: (fh * 16 + f) * 1024 +
                                              (dh + 1) * 512],
                                         start=(fh == 0 and f == 0),
                                         stop=(fh == 1 and f == 15)), dep=dep)
                        relum[pi] = 0
                    gend = p.cnt["t"]
                    for pi, (u, dh) in enumerate(pairs):
                        oi = pi % 2
                        p.op("s", lambda e, pi=pi, oi=oi: e.activation(
                            ost2[oi][:, :], po[pi][:, :], AF.Copy),
                            dep=M({"t": gend}, ostm[oi]))
                        relum[pi] = p.cnt["s"]
                        row0 = blk * 512 + u * 128
                        ch = f"co{oi}"
                        p.dma("g", co_out[row0:row0 + 128,
                                          dh * 512:(dh + 1) * 512],
                              ost2[oi][:, :], ch=ch, dep={"s": p.cnt["s"]})
                        ostm[oi] = {ch: p.cnt[ch]}
                outm_prev = p.cnt["t"]

        with nc.Block() as block:
            p.build(block)
    return nc


def build_L4():
    nc = bass.Bass("TRN2", target_bir_lowering=False, debug=False, num_swdge_queues=4)
    call_in = nc.dram_tensor("call", [NCORE * T, E], BF16, kind="ExternalInput").ap()
    slot_in = nc.dram_tensor("slotT", [128, NT], F32, kind="ExternalInput").ap()
    gs_in = nc.dram_tensor("gsT", [128, NT], F32, kind="ExternalInput").ap()
    xp_in = nc.dram_tensor("xp", [T, E], F32, kind="ExternalInput").ap()
    out = nc.dram_tensor("out", [T, E], F32, kind="ExternalOutput").ap()

    with ExitStack() as st:
        EE = st.enter_context
        p = P(nc, st)
        slotT = EE(nc.sbuf_tensor("slotT_s", [128, NT], F32))
        slot_u = EE(nc.sbuf_tensor("slot_u", [128, NT], U32))
        gsT = EE(nc.sbuf_tensor("gsT_s", [128, NT], F32))
        rt = EE(nc.sbuf_tensor("rt", [128, 1024], BF16))
        rtf = EE(nc.sbuf_tensor("rtf", [128, 1024], F32))
        xt = EE(nc.sbuf_tensor("xt", [128, 1024], F32))
        ot = EE(nc.sbuf_tensor("ot", [128, 1024], F32))

        p.dma("y", slotT[:, :], slot_in)
        p.dma("y", gsT[:, :], gs_in)
        p.op("v", lambda e: e.tensor_copy(slot_u[:, :], slotT[:, :]))
        for t in range(NT):
            p.op("v", lambda e: e.memset(rt[:, :], 0.0))
            p.idma("g", lambda e, t=t: e.indirect_dma_start(
                out=rt[:, :], out_offset=None,
                in_=call_in[0:128, :],
                in_offset=IOA(slot_u[:, t:t + 1], 0),
                bounds_check=NCORE * T - 1, oob_is_err=False))
            p.op("v", lambda e, t=t: e.tensor_scalar_mul(rtf[:, :], rt[:, :],
                                                         gsT[:, t:t + 1]))
            p.dma("g", xt[:, :], xp_in[t * 128:(t + 1) * 128, :], ch="xl")
            p.op("v", lambda e: e.tensor_add(ot[:, :], xt[:, :], rtf[:, :]))
            p.dma("g", out[t * 128:(t + 1) * 128, :], ot[:, :], ch="ol")

        with nc.Block() as block:
            p.build(block)
    return nc


_NC_CACHE = {}


def _get(name, builder):
    if name not in _NC_CACHE:
        _NC_CACHE[name] = builder()
    return _NC_CACHE[name]


def kernel(L0_attn_x, L0_attn_Wq, L0_attn_Wqr, L0_attn_Wkr, L0_attn_Wkvd,
           L0_attn_Wku, L0_attn_Wvu, L0_attn_Wo, L0_attn_lobo_log,
           L0_ffn_Wg, L0_ffn_w1, L0_ffn_w2):
    x = np.ascontiguousarray(np.asarray(L0_attn_x, dtype=np.float32))
    wq = np.ascontiguousarray(np.asarray(L0_attn_Wq, dtype=np.float32))
    wqr = np.ascontiguousarray(np.asarray(L0_attn_Wqr, dtype=np.float32))
    wkr = np.ascontiguousarray(np.asarray(L0_attn_Wkr, dtype=np.float32))
    wkvd = np.ascontiguousarray(np.asarray(L0_attn_Wkvd, dtype=np.float32))
    wku = np.ascontiguousarray(np.asarray(L0_attn_Wku, dtype=np.float32))
    wvu = np.ascontiguousarray(np.asarray(L0_attn_Wvu, dtype=np.float32))
    wo = np.ascontiguousarray(np.asarray(L0_attn_Wo, dtype=np.float32))
    lobo = np.ascontiguousarray(np.asarray(L0_attn_lobo_log,
                                           dtype=np.float32)).reshape(1, H)
    wg = np.ascontiguousarray(np.asarray(L0_ffn_Wg, dtype=np.float32))
    w1 = np.ascontiguousarray(np.asarray(L0_ffn_w1, dtype=np.float32))
    w2 = np.ascontiguousarray(np.asarray(L0_ffn_w2, dtype=np.float32))

    jj = np.arange(16, dtype=np.float64)
    inv = 1.0 / (10000.0 ** (jj / 16.0))
    ang = np.arange(T, dtype=np.float64)[:, None] * inv[None, :]
    cosr = np.tile(np.cos(ang).astype(np.float32), (1, 16))
    sinr = np.tile(np.sin(ang).astype(np.float32), (1, 16))

    nc1 = _get("L1", build_L1)
    maps1 = [{"x": x[k], "wq": wq, "wqr": wqr, "wkr": wkr, "wkvd": wkvd,
              "wku": wku, "wvu": wvu, "wo": wo, "lobo": lobo, "wg": wg,
              "cosr": cosr, "sinr": sinr} for k in range(NCORE)]
    r1 = run_bass_kernel_spmd(nc1, maps1, core_ids=list(range(NCORE))).results

    cnts = np.concatenate([r["cnto"] for r in r1], axis=0)
    nc2 = _get("L2", build_L2)
    toki = np.arange(NCORE * T, dtype=np.float32).reshape(NCORE, NT, 128)
    maps2 = []
    for k in range(NCORE):
        maps2.append({
            "cnts": cnts,
            "lloc": r1[k]["lloco"],
            "idxT": np.ascontiguousarray(
                r1[k]["idxo"].reshape(NT, 128).T),
            "tokiT": np.ascontiguousarray(toki[k].transpose(1, 0)),
            "sel": (np.arange(NCORE, dtype=np.float32) < k)
            .astype(np.float32).reshape(NCORE, 1),
        })
    r2 = run_bass_kernel_spmd(nc2, maps2, core_ids=list(range(NCORE))).results

    xall = np.concatenate([r["xln2"] for r in r1], axis=0)
    gparts = np.stack([r2[k]["gpart"][:, 0] for k in range(NCORE)])
    nc3 = _get("L3", build_L3)
    maps3 = []
    for k in range(NCORE):
        gmy = gparts[:, k * T:(k + 1) * T]
        gmy = gmy.reshape(NCORE, NT, 128).transpose(2, 0, 1).reshape(128,
                                                                     NCORE * NT)
        maps3.append({
            "xall": xall,
            "gmy": np.ascontiguousarray(gmy),
            "w1own": w1[2 * k:2 * k + 2],
            "w2own": w2[2 * k:2 * k + 2],
        })
    r3 = run_bass_kernel_spmd(nc3, maps3, core_ids=list(range(NCORE))).results

    call = np.concatenate([r["contrib"] for r in r3], axis=0)
    nc4 = _get("L4", build_L4)
    maps4 = []
    for k in range(NCORE):
        maps4.append({
            "call": call,
            "slotT": np.ascontiguousarray(
                r2[k]["slotp"].reshape(NT, 128).T),
            "gsT": np.ascontiguousarray(
                r1[k]["gso"].reshape(NT, 128).T),
            "xp": r1[k]["xprime"],
        })
    r4 = run_bass_kernel_spmd(nc4, maps4, core_ids=list(range(NCORE))).results

    out = np.stack([r4[k]["out"] for k in range(NCORE)], axis=0)
    return out.astype(np.float32)


# version 23
# speedup vs baseline: 1.0589x; 1.0010x over previous
import numpy as np
import ml_dtypes
import concourse.bass as bass
import concourse.mybir as mybir
from concourse.bass_utils import run_bass_kernel_spmd
from contextlib import ExitStack

F32 = mybir.dt.float32
BF16 = mybir.dt.bfloat16
U32 = mybir.dt.uint32
AF = mybir.ActivationFunctionType
ALU = mybir.AluOpType
AX = mybir.AxisListType
IOA = bass.IndirectOffsetOnAxis

NCORE = 8
T = 2048
E = 1024
H = 16
DH = 64
DR = 32
DQK = DH + DR
DKV = 256
NEXP = 16
CAP = 1024
FF = 4096
NT = T // 128
SCALE = 1.0 / float(np.sqrt(np.float64(DQK)))
EPS = 1e-5
BIGSLOT = 1.0e9


def M(*ds):
    out = {}
    for d in ds:
        for k, v in d.items():
            if v > out.get(k, 0):
                out[k] = v
    return out


class P:
    ENG = ("t", "v", "s", "g", "y")

    def __init__(self, nc, stack):
        self.nc = nc
        self.stack = stack
        self.sem = {e: stack.enter_context(nc.semaphore("sq_" + e))
                    for e in self.ENG}
        self.cnt = {e: 0 for e in self.ENG}
        self.ops = {e: [] for e in self.ENG}
        self.lastw = {e: {} for e in self.ENG}

    def chan(self, name):
        if name not in self.sem:
            self.sem[name] = self.stack.enter_context(
                self.nc.semaphore("ch_" + name))
            self.cnt[name] = 0
        return name

    def mark(self, *names):
        if names:
            return {n: self.cnt[n] for n in names}
        return dict(self.cnt)

    def _waits(self, eng, dep):
        if dep == "all":
            want = dict(self.cnt)
        elif dep is None:
            want = {}
        elif isinstance(dep, dict):
            want = {d: v for d, v in dep.items() if v > 0}
        else:
            want = {d: self.cnt[d] for d in dep}
        waits = []
        lw = self.lastw[eng]
        for d, v in want.items():
            if v > lw.get(d, 0):
                lw[d] = v
                waits.append((self.sem[d], v))
        return waits

    def op(self, eng, fn, dep="all", sem=None):
        name = eng if sem is None else self.chan_c(sem)
        waits = self._waits(eng, dep)
        self.cnt[name] += 1
        self.ops[eng].append((fn, waits, self.sem[name], 1))

    def chan_c(self, name):
        if name not in self.sem:
            self.sem[name] = self.stack.enter_context(
                self.nc.semaphore("cs_" + name))
            self.cnt[name] = 0
        return name

    def idma(self, eng, fn, dep="all", ch=None):
        ch = self.chan(ch if ch is not None else "qi_" + eng)
        waits = self._waits(eng, dep)
        v = self.cnt[ch]
        if v > self.lastw[eng].get(ch, 0):
            self.lastw[eng][ch] = v
            waits.append((self.sem[ch], v))
        self.cnt[ch] += 16
        self.ops[eng].append((fn, waits, self.sem[ch], 16))

    def dma(self, eng, out, in_, dep="all", ch=None, **kw):
        ch = self.chan(ch if ch is not None else "q_" + eng)
        waits = self._waits(eng, dep)
        v = self.cnt[ch]
        if v > self.lastw[eng].get(ch, 0):
            self.lastw[eng][ch] = v
            waits.append((self.sem[ch], v))
        self.cnt[ch] += 16
        self.ops[eng].append(
            (lambda e, o=out, i=in_, k=kw: e.dma_start(out=o, in_=i, **k),
             waits, self.sem[ch], 16))

    def build(self, block):
        deco = {"t": block.tensor, "v": block.vector, "s": block.scalar,
                "g": block.gpsimd, "y": block.sync}
        for eng in self.ENG:
            ops = self.ops[eng]
            if not ops:
                continue

            def body(e, ops=ops):
                for fn, waits, sem, inc in ops:
                    for h, v in waits:
                        e.wait_ge(h, v)
                    fn(e).then_inc(sem, inc)
            deco[eng](body)
        final = [(self.sem[e], c) for e, c in self.cnt.items() if c]

        def tail(e, fin=final):
            for h, v in fin:
                e.wait_ge(h, v)
        block.sync(tail)


def ln_pass(p, xt, scratch, vec, out_bf, dep, dep_last=None):
    ssq = vec[:, 0:1]
    s1 = vec[:, 1:2]
    mu = vec[:, 2:3]
    var = vec[:, 3:4]
    sd = vec[:, 4:5]
    rstd = vec[:, 5:6]
    nmr = vec[:, 6:7]
    p.op("s", lambda e: e.activation(scratch[:, :], xt, AF.Square,
                                     accum_out=ssq), dep=dep)
    sqm = p.cnt["s"]
    p.op("v", lambda e: e.reduce_sum(s1, xt, axis=AX.X), dep=dep)
    p.op("v", lambda e: e.tensor_scalar_mul(mu, s1, 1.0 / E),
         dep={"v": p.cnt["v"]})
    p.op("v", lambda e: e.tensor_scalar_mul(var, ssq, 1.0 / E),
         dep={"v": p.cnt["v"], "s": sqm})
    p.op("v", lambda e: e.tensor_mul(sd, mu, mu), dep={"v": p.cnt["v"]})
    p.op("v", lambda e: e.tensor_sub(var, var, sd), dep={"v": p.cnt["v"]})
    p.op("v", lambda e: e.tensor_scalar_add(var, var, float(EPS)),
         dep={"v": p.cnt["v"]})
    p.op("s", lambda e: e.activation(sd, var, AF.Ln),
         dep={"v": p.cnt["v"], "s": p.cnt["s"]})
    p.op("s", lambda e: e.activation(rstd, sd, AF.Exp, scale=-0.5),
         dep={"s": p.cnt["s"]})
    p.op("v", lambda e: e.scalar_tensor_tensor(nmr, mu, -1.0, rstd,
                                               ALU.mult, ALU.mult),
         dep={"v": p.cnt["v"], "s": p.cnt["s"]})
    fin = {"v": p.cnt["v"], "s": p.cnt["s"]}
    if dep_last:
        fin = M(dep_last, fin)
    p.op("v", lambda e: e.tensor_scalar(out_bf, xt, rstd, nmr,
                                        ALU.mult, ALU.add), dep=fin)
    return {"s": sqm, "v": p.cnt["v"]}


def ln_pair(p, xts, scrs, vecs, outs, deps, dep_lasts):
    st1 = [0, 0]
    st2 = [0, 0]
    ex = [None, None]
    for k in range(2):
        p.op("s", lambda e, k=k: e.activation(
            scrs[k][:, :], xts[k], AF.Square, accum_out=vecs[k][:, 0:1]),
            dep=deps[k])
        st1[k] = p.cnt["s"]
    for k in range(2):
        vec = vecs[k]
        ssq, s1, mu, var, sd, rstd, nmr = (vec[:, j:j + 1] for j in range(7))
        p.op("v", lambda e, k=k, s1=s1: e.reduce_sum(s1, xts[k], axis=AX.X),
             dep=M(deps[k], {"v": p.cnt["v"]}))
        p.op("v", lambda e, mu=mu, s1=s1: e.tensor_scalar_mul(
            mu, s1, 1.0 / E), dep={"v": p.cnt["v"]})
        p.op("v", lambda e, var=var, ssq=ssq: e.tensor_scalar_mul(
            var, ssq, 1.0 / E), dep={"v": p.cnt["v"], "s": st1[k]})
        p.op("v", lambda e, sd=sd, mu=mu: e.tensor_mul(sd, mu, mu),
             dep={"v": p.cnt["v"]})
        p.op("v", lambda e, var=var, sd=sd: e.tensor_sub(var, var, sd),
             dep={"v": p.cnt["v"]})
        p.op("v", lambda e, var=var: e.tensor_scalar_add(
            var, var, float(EPS)), dep={"v": p.cnt["v"]})
        vm = p.cnt["v"]
        p.op("s", lambda e, sd=sd, var=var: e.activation(sd, var, AF.Ln),
             dep={"v": vm, "s": p.cnt["s"]})
        p.op("s", lambda e, rstd=rstd, sd=sd: e.activation(
            rstd, sd, AF.Exp, scale=-0.5), dep={"s": p.cnt["s"]})
        st2[k] = p.cnt["s"]
    for k in range(2):
        vec = vecs[k]
        mu, rstd, nmr = vec[:, 2:3], vec[:, 5:6], vec[:, 6:7]
        p.op("v", lambda e, nmr=nmr, mu=mu, rstd=rstd: e.scalar_tensor_tensor(
            nmr, mu, -1.0, rstd, ALU.mult, ALU.mult),
            dep={"v": p.cnt["v"], "s": st2[k]})
        fin = M(dep_lasts[k], {"v": p.cnt["v"], "s": st2[k]})
        p.op("v", lambda e, k=k, rstd=rstd, nmr=nmr: e.tensor_scalar(
            outs[k], xts[k], rstd, nmr, ALU.mult, ALU.add), dep=fin)
        ex[k] = {"s": st1[k], "v": p.cnt["v"]}
    return ex


def build_L1():
    nc = bass.Bass("TRN2", target_bir_lowering=False, debug=False, num_swdge_queues=4)
    x_in = nc.dram_tensor("x", [T, E], F32, kind="ExternalInput").ap()
    wq_in = nc.dram_tensor("wq", [E, H * DH], F32, kind="ExternalInput").ap()
    wqr_in = nc.dram_tensor("wqr", [E, H * DR], F32, kind="ExternalInput").ap()
    wkr_in = nc.dram_tensor("wkr", [E, DR], F32, kind="ExternalInput").ap()
    wkvd_in = nc.dram_tensor("wkvd", [E, DKV], F32, kind="ExternalInput").ap()
    wku_in = nc.dram_tensor("wku", [DKV, H * DH], F32, kind="ExternalInput").ap()
    wvu_in = nc.dram_tensor("wvu", [DKV, H * DH], F32, kind="ExternalInput").ap()
    wo_in = nc.dram_tensor("wo", [E, E], F32, kind="ExternalInput").ap()
    lobo_in = nc.dram_tensor("lobo", [1, H], F32, kind="ExternalInput").ap()
    wg_in = nc.dram_tensor("wg", [E, NEXP], F32, kind="ExternalInput").ap()
    cosr_in = nc.dram_tensor("cosr", [T, 256], F32, kind="ExternalInput").ap()
    sinr_in = nc.dram_tensor("sinr", [T, 256], F32, kind="ExternalInput").ap()

    xp_out = nc.dram_tensor("xprime", [T, E], F32, kind="ExternalOutput").ap()
    x2_out = nc.dram_tensor("xln2", [T, E], BF16, kind="ExternalOutput").ap()
    idx_out = nc.dram_tensor("idxo", [T, 1], F32, kind="ExternalOutput").ap()
    gs_out = nc.dram_tensor("gso", [T, 1], F32, kind="ExternalOutput").ap()
    lloc_out = nc.dram_tensor("lloco", [NT, 128, NEXP], F32, kind="ExternalOutput").ap()
    cnt_out = nc.dram_tensor("cnto", [1, NEXP], F32, kind="ExternalOutput").ap()

    qhat = nc.dram_tensor("qhat", [H, 97, T], BF16, kind="Internal").ap()
    khat = nc.dram_tensor("khat", [H, 97, T], BF16, kind="Internal").ap()

    idbf = nc.inline_tensor(np.eye(128, dtype=ml_dtypes.bfloat16), "idbf").ap()
    idf = nc.inline_tensor(np.eye(128, dtype=np.float32), "idf").ap()
    m0np = np.where(np.arange(128)[None, :] > np.arange(128)[:, None],
                    np.float32(-1e30), np.float32(0.0)).astype(np.float32)
    m0 = nc.inline_tensor(m0np, "m0").ap()
    trilS = nc.inline_tensor(
        np.where(np.arange(128)[:, None] < np.arange(128)[None, :],
                 np.float32(1.0), np.float32(0.0)).astype(np.float32), "trilS").ap()
    onesf = nc.inline_tensor(np.ones((128, 128), np.float32), "onesf").ap()
    onescol = nc.inline_tensor(np.ones((128, 1), np.float32), "onescol").ap()
    iota16 = nc.inline_tensor(np.tile(np.arange(16, dtype=np.float32), (128, 1)),
                              "iota16").ap()
    iom99 = nc.inline_tensor(np.tile(np.arange(16, dtype=np.float32) - 99.0,
                                     (128, 1)), "iom99").ap()
    ones16 = nc.inline_tensor(np.ones((128, 16), np.float32), "ones16").ap()
    onesrow128 = nc.inline_tensor(np.ones((1, 128), np.float32), "or128i").ap()

    with ExitStack() as st:
        EE = st.enter_context
        p = P(nc, st)
        wkvd = EE(nc.sbuf_tensor("wkvd_s", [128, 8 * DKV], BF16))
        wku = EE(nc.sbuf_tensor("wku_s", [128, 2 * 1024], BF16))
        wvu = EE(nc.sbuf_tensor("wvu_s", [128, 2 * 1024], BF16))
        wkr = EE(nc.sbuf_tensor("wkr_s", [128, 8 * DR], BF16))
        wq = EE(nc.sbuf_tensor("wq_s", [128, 8 * 1024], BF16))
        wqr = EE(nc.sbuf_tensor("wqr_s", [128, 8 * 512], BF16))
        wg = EE(nc.sbuf_tensor("wg_s", [128, 8 * NEXP], BF16))
        vhat = [EE(nc.sbuf_tensor(f"vhat{t}", [128, H * 65], BF16))
                for t in range(NT)]
        idbf_sb = EE(nc.sbuf_tensor("idbf_sb", [128, 128], BF16))
        idf_sb = EE(nc.sbuf_tensor("idf_sb", [128, 128], F32))
        m0_sb = EE(nc.sbuf_tensor("m0_sb", [128, 128], F32))
        trilS_sb = EE(nc.sbuf_tensor("trilS_sb", [128, 128], F32))
        onesf_sb = EE(nc.sbuf_tensor("onesf_sb", [128, 128], F32))
        onescol_sb = EE(nc.sbuf_tensor("onescol_sb", [128, 1], F32))
        iota16_sb = EE(nc.sbuf_tensor("iota16_sb", [128, 16], F32))
        iom99_sb = EE(nc.sbuf_tensor("iom99_sb", [128, 16], F32))
        ones16_sb = EE(nc.sbuf_tensor("ones16_sb", [128, 16], F32))
        c_all = EE(nc.sbuf_tensor("c_all", [128, 16], F32))
        lobo_sb = EE(nc.sbuf_tensor("lobo_sb", [1, 16], F32))
        vec2 = [EE(nc.sbuf_tensor(f"vec{i}", [128, 16], F32)) for i in range(2)]
        scr2 = [EE(nc.sbuf_tensor(f"scr{i}", [128, 1024], F32)) for i in range(2)]

        for dst, src in ((idbf_sb, idbf), (idf_sb, idf), (m0_sb, m0),
                         (trilS_sb, trilS), (onesf_sb, onesf),
                         (onescol_sb, onescol), (iota16_sb, iota16),
                         (iom99_sb, iom99), (ones16_sb, ones16),
                         (lobo_sb, lobo_in)):
            p.dma("y", dst[:, :], src, ch="c0", dep=None)
        cm = p.mark("c0")

        p.dma("g", wkvd[:, :].rearrange("p (c n) -> p c n", n=DKV),
              wkvd_in.rearrange("(c p) n -> p c n", p=128), ch="w0", dep=None)
        p.dma("g", wku[:, :].rearrange("p (c n) -> p c n", n=1024),
              wku_in.rearrange("(c p) n -> p c n", p=128), ch="w1", dep=None)
        p.dma("g", wvu[:, :].rearrange("p (c n) -> p c n", n=1024),
              wvu_in.rearrange("(c p) n -> p c n", p=128), ch="w2", dep=None)
        p.dma("g", wkr[:, :].rearrange("p (c n) -> p c n", n=DR),
              wkr_in.rearrange("(c p) n -> p c n", p=128), ch="w0", dep=None)
        p.dma("g", wq[:, :].rearrange("p (c n) -> p c n", n=1024),
              wq_in.rearrange("(c p) n -> p c n", p=128), ch="w1", dep=None)
        p.dma("g", wqr[:, :].rearrange("p (c n) -> p c n", n=512),
              wqr_in.rearrange("(c p) n -> p c n", p=128), ch="w2", dep=None)
        p.dma("g", wg[:, :].rearrange("p (c n) -> p c n", n=NEXP),
              wg_in.rearrange("(c p) n -> p c n", p=128), ch="w0", dep=None)
        wm = p.mark("w0", "w1", "w2")
        p.op("s", lambda e: e.activation(wq[:, :], wq[:, :], AF.Copy,
                                         scale=SCALE), dep=wm)
        p.op("s", lambda e: e.activation(wqr[:, :], wqr[:, :], AF.Copy,
                                         scale=SCALE), dep=wm)
        wscale = p.mark("s")

        with nc.psum_tensor("psc", [128, 16], F32) as psc:
            p.op("s", lambda e: e.activation(lobo_sb[:, :], lobo_sb[:, :],
                                             AF.Exp), dep=cm)
            p.op("t", lambda e: e.matmul(psc[:, :], onesf_sb[0:1, :],
                                         lobo_sb[:, :], start=True, stop=True),
                 dep={"s": p.cnt["s"], **cm})
            p.op("v", lambda e: e.tensor_copy(c_all[:, :], psc[:, :]),
                 dep={"t": p.cnt["t"]})

        import os as _os3
        if _os3.environ.get("K_PHASE", "full") == "p0":
            with nc.Block() as block:
                p.build(block)
            return nc

        krfm = EE(nc.sbuf_tensor("krfm", [32, T], BF16))
        qrfm = [EE(nc.sbuf_tensor(f"qrfm{i}", [128, T], BF16))
                for i in range(4)]
        with ExitStack() as stA:
            EA = stA.enter_context
            xfm = [EA(nc.sbuf_tensor(f"xfm{c}", [128, T], BF16)) for c in range(8)]
            lat = [EA(nc.sbuf_tensor(f"lat{c}", [128, T], BF16)) for c in range(2)]
            cosr = EA(nc.sbuf_tensor("cosr_s", [128, NT * 256], F32))
            sinr = EA(nc.sbuf_tensor("sinr_s", [128, NT * 256], F32))
            xt2b = [EA(nc.sbuf_tensor(f"xt{i}", [128, 1024], F32))
                    for i in range(2)]
            x1t2 = [EA(nc.sbuf_tensor(f"x1t{i}", [128, 1024], BF16))
                    for i in range(2)]
            stg2b = [EA(nc.sbuf_tensor(f"stg{i}", [128, 512], BF16))
                     for i in range(2)]
            rop = EA(nc.sbuf_tensor("rop", [128, 512], F32))
            ropo = EA(nc.sbuf_tensor("ropo", [128, 512], BF16))
            krbf2 = [EA(nc.sbuf_tensor(f"krbf{i}", [128, 65], BF16))
                     for i in range(2)]

            pp2 = [EA(nc.psum_tensor(f"pp{i}", [128, 512], F32))
                   for i in range(4)]
            pb2 = [EA(nc.psum_tensor(f"pb{i}", [128, 128], BF16))
                   for i in range(2)]
            pk2 = [EA(nc.psum_tensor(f"pk{i}", [128, 32], F32))
                   for i in range(2)]

            p.dma("g", cosr[:, :].rearrange("p (t n) -> p t n", n=256),
                  cosr_in.rearrange("(t p) n -> p t n", p=128), ch="cs", dep=None)
            p.dma("g", sinr[:, :].rearrange("p (t n) -> p t n", n=256),
                  sinr_in.rearrange("(t p) n -> p t n", p=128), ch="cs", dep=None)
            csm = p.mark("cs")

            lnfree = [{}, {}]
            trmark = [0, 0]
            pbfree = [0, 0]
            for tp in range(0, NT, 2):
                ents = []
                for t in (tp, tp + 1):
                    i = t % 2
                    xch = f"x{i}"
                    p.dma("g", xt2b[i][:, :], x_in[t * 128:(t + 1) * 128, :],
                          ch=xch, dep=lnfree[i])
                    ents.append({xch: p.cnt[xch], **cm})
                exs = ln_pair(p, [xt2b[0][:, :], xt2b[1][:, :]],
                              scr2, vec2,
                              [x1t2[0][:, :], x1t2[1][:, :]], ents,
                              [{"t": trmark[0]}, {"t": trmark[1]}])
                for t in (tp, tp + 1):
                    i = t % 2
                    lnfree[i] = exs[i]
                    for c in range(8):
                        p.op("t", lambda e, c=c, i=i: e.transpose(
                            pb2[c % 2][:, :],
                            x1t2[i][:, c * 128:(c + 1) * 128],
                            idbf_sb[:, :]),
                            dep={"v": max(exs[i]["v"], pbfree[c % 2]), **cm})
                        tm = p.cnt["t"]
                        p.op("v", lambda e, c=c, t=t: e.tensor_copy(
                            xfm[c][:, t * 128:(t + 1) * 128],
                            pb2[c % 2][:, :]),
                            dep={"t": tm})
                        pbfree[c % 2] = p.cnt["v"]
                    trmark[i] = p.cnt["t"]
            xfm_done = {"v": p.cnt["v"]}

            if _os3.environ.get("K_PHASE", "full") == "p1":
                with nc.Block() as block:
                    p.build(block)
                return nc

            ppfree = [{}, {}, {}, {}]
            gj = [0]

            def proj(slices, evac_fn, extra=None, reader="s"):
                j = gj[0]
                gj[0] += 1
                bank = pp2[j % 4]
                for i, (l, r) in enumerate(slices):
                    dep = None
                    if i == 0:
                        dep = M(ppfree[j % 4], extra or {})
                    p.op("t", lambda e, l=l, r=r, i=i, n=len(slices), bank=bank:
                         e.matmul(bank[:, :], l, r, start=(i == 0),
                                  stop=(i == n - 1)), dep=dep)
                tm = p.cnt["t"]
                p.op(reader, lambda e, bank=bank: evac_fn(e, bank),
                     dep={"t": tm})
                ppfree[j % 4] = {reader: p.cnt[reader]}
                return p.cnt[reader]

            for B in range(4):
                for d in range(2):
                    sl = [(wkvd[:, c * DKV + d * 128: c * DKV + d * 128 + 128],
                           xfm[c][:, B * 512:(B + 1) * 512]) for c in range(8)]
                    proj(sl, lambda e, bank, B=B, d=d: e.activation(
                        lat[d][:, B * 512:(B + 1) * 512], bank[:, :], AF.Copy),
                        extra={**xfm_done, **wm})
            lat_done = {"s": p.cnt["s"]}

            if _os3.environ.get("K_PHASE", "full") == "p2a":
                with nc.Block() as block:
                    p.build(block)
                return nc

            stgfree = [{}, {}]
            kwi = [0]

            def head_rows(dst, B, o, src_is_q):
                j = gj[0]
                si = j % 2
                stg = stg2b[si]
                if src_is_q:
                    sl = [(wq[:, c * 1024 + o * 128: c * 1024 + o * 128 + 128],
                           xfm[c][:, B * 512:(B + 1) * 512]) for c in range(8)]
                    extra = {**xfm_done, **wscale}
                else:
                    sl = [(wku[:, c * 1024 + o * 128: c * 1024 + o * 128 + 128],
                           lat[c][:, B * 512:(B + 1) * 512]) for c in range(2)]
                    extra = {**lat_done, **wm}
                em = proj(sl, lambda e, bank, stg=stg: e.activation(
                    stg[:, :], bank[:, :], AF.Copy),
                    extra=M(extra, stgfree[si]))
                ch = f"kw{kwi[0] % 3}"
                kwi[0] += 1
                p.dma("g", dst[2 * o, 0:64, B * 512:(B + 1) * 512],
                      stg[0:64, :], ch=ch, dep={"s": em})
                p.dma("g", dst[2 * o + 1, 0:64, B * 512:(B + 1) * 512],
                      stg[64:128, :], ch=ch, dep=None)
                stgfree[si] = {ch: p.cnt[ch]}

            for B in range(4):
                for o in range(8):
                    head_rows(khat, B, o, False)
            for B in range(4):
                for o in range(8):
                    head_rows(qhat, B, o, True)

            if _os3.environ.get("K_PHASE", "full") == "p2c":
                with nc.Block() as block:
                    p.build(block)
                return nc

            _vdep = _os3.environ.get("K_P2D", "prec")
            for t in range(NT):
                vm = []
                for nh in range(2):
                    sl = [(lat[c][:, t * 128:(t + 1) * 128],
                           wvu[:, c * 1024 + nh * 512: c * 1024 + (nh + 1) * 512])
                          for c in range(2)]
                    if _vdep == "allall":
                        j = gj[0]; gj[0] += 1
                        bank = pp2[j % 2]
                        for i, (l, r) in enumerate(sl):
                            p.op("t", lambda e, l=l, r=r, i=i, bank=bank:
                                 e.matmul(bank[:, :], l, r, start=(i == 0),
                                          stop=(i == 1)), dep="all")
                        p.op("s", lambda e, bank=bank, t=t, nh=nh: e.activation(
                            vhat[t][:, :].rearrange("p (h q) -> p h q", q=65)
                            [:, nh * 8:(nh + 1) * 8, 0:64],
                            bank[:, :].rearrange("p (h d) -> p h d", d=64),
                            AF.Copy), dep="all")
                        ppfree[j % 2] = {"s": p.cnt["s"]}
                    else:
                        vm.append(proj(sl, lambda e, bank, t=t, nh=nh:
                                       e.activation(
                            vhat[t][:, :].rearrange("p (h q) -> p h q", q=65)
                            [:, nh * 8:(nh + 1) * 8, 0:64],
                            bank[:, :].rearrange("p (h d) -> p h d", d=64),
                            AF.Copy), extra={**lat_done, **wm}))
                if _os3.environ.get("K_NOMEMSET", "0") == "1":
                    pass
                elif _vdep == "allall":
                    p.op("v", lambda e, t=t: e.memset(
                        vhat[t][:, :].rearrange("p (h q) -> p h q", q=65)
                        [:, :, 64:65], 1.0), dep="all")
                else:
                    p.op("v", lambda e, t=t: e.memset(
                        vhat[t][:, :].rearrange("p (h q) -> p h q", q=65)
                        [:, :, 64:65], 1.0), dep=None)
            vhat_done = {"s": p.cnt["s"], "v": p.cnt["v"]}

            if _os3.environ.get("K_PHASE", "full") == "p2d":
                with nc.Block() as block:
                    p.build(block)
                return nc

            for i in range(2):
                p.op("v", lambda e, i=i: e.memset(krbf2[i][:, :], 0.0),
                     dep=None)
            pkfree = [0, 0]
            krtm = [0, 0]
            for t in range(NT):
                ti = t % 2
                pk = pk2[ti]
                krbf = krbf2[ti]
                for c in range(8):
                    p.op("t", lambda e, t=t, c=c, pk=pk: e.matmul(
                        pk[:, :],
                        xfm[c][:, t * 128:(t + 1) * 128],
                        wkr[:, c * 32:(c + 1) * 32],
                        start=(c == 0), stop=(c == 7)),
                        dep={**xfm_done, **wm,
                             "v": pkfree[ti]} if c == 0 else None)
                tm = p.cnt["t"]
                cs = cosr[:, t * 256: t * 256 + 16]
                sn = sinr[:, t * 256: t * 256 + 16]
                p.op("v", lambda e, cs=cs, pk=pk: e.tensor_mul(
                    rop[:, 0:16], pk[:, 0:16], cs),
                    dep={"t": tm, **csm, "v": p.cnt["v"]})
                p.op("v", lambda e, sn=sn, pk=pk: e.tensor_mul(
                    rop[:, 16:32], pk[:, 16:32], sn), dep={"v": p.cnt["v"]})
                p.op("v", lambda e: e.tensor_sub(rop[:, 32:48],
                                                 rop[:, 0:16], rop[:, 16:32]),
                     dep={"v": p.cnt["v"]})
                p.op("v", lambda e, cs=cs, pk=pk: e.tensor_mul(
                    rop[:, 0:16], pk[:, 16:32], cs), dep={"v": p.cnt["v"]})
                p.op("v", lambda e, sn=sn, pk=pk: e.tensor_mul(
                    rop[:, 16:32], pk[:, 0:16], sn), dep={"v": p.cnt["v"]})
                pkfree[ti] = p.cnt["v"]
                p.op("v", lambda e: e.tensor_add(rop[:, 48:64],
                                                 rop[:, 0:16], rop[:, 16:32]),
                     dep={"v": p.cnt["v"]})
                p.op("v", lambda e, krbf=krbf: e.tensor_copy(
                    krbf[:, 0:32], rop[:, 32:64]),
                    dep={"v": p.cnt["v"], "t": krtm[ti]})
                vm_ = p.cnt["v"]
                p.op("t", lambda e, ti=ti, krbf=krbf: e.transpose(
                    pb2[ti][0:65, :], krbf[:, :], idbf_sb[:, :]),
                     dep={"v": max(vm_, pbfree[ti])})
                krtm[ti] = p.cnt["t"]
                p.op("v", lambda e, t=t, ti=ti: e.tensor_copy(
                    krfm[0:32, t * 128:(t + 1) * 128], pb2[ti][0:32, 0:128]),
                    dep={"t": p.cnt["t"]})
                pbfree[ti] = p.cnt["v"]
            krm = p.cnt["v"]

            for t in range(NT):
                j = gj[0]
                gj[0] += 1
                bank = pp2[j % 2]
                for c in range(8):
                    p.op("t", lambda e, t=t, c=c, bank=bank: e.matmul(
                        bank[:, :],
                        xfm[c][:, t * 128:(t + 1) * 128],
                        wqr[:, c * 512:(c + 1) * 512],
                        start=(c == 0), stop=(c == 7)),
                        dep=M(xfm_done, wscale,
                              ppfree[j % 2]) if c == 0 else None)
                tm = p.cnt["t"]
                x1 = bank[:, :].rearrange("p (h d) -> p h d", d=32)[:, :, 0:16]
                x2 = bank[:, :].rearrange("p (h d) -> p h d", d=32)[:, :, 16:32]
                cs = cosr[:, t * 256:(t + 1) * 256].rearrange(
                    "p (h j) -> p h j", j=16)
                sn = sinr[:, t * 256:(t + 1) * 256].rearrange(
                    "p (h j) -> p h j", j=16)
                o1 = ropo[:, :].rearrange("p (h d) -> p h d", d=32)[:, :, 0:16]
                o2 = ropo[:, :].rearrange("p (h d) -> p h d", d=32)[:, :, 16:32]
                r1 = rop[:, 0:256].rearrange("p (h j) -> p h j", j=16)
                r2 = rop[:, 256:512].rearrange("p (h j) -> p h j", j=16)
                p.op("v", lambda e, x1=x1, cs=cs, r1=r1: e.tensor_mul(r1, x1, cs),
                     dep={"t": tm, **csm, "v": p.cnt["v"]})
                p.op("v", lambda e, x2=x2, sn=sn, r2=r2: e.tensor_mul(r2, x2, sn),
                     dep={"v": p.cnt["v"]})
                p.op("v", lambda e, o1=o1, r1=r1, r2=r2: e.tensor_sub(o1, r1, r2),
                     dep={"v": p.cnt["v"]})
                p.op("v", lambda e, x2=x2, cs=cs, r1=r1: e.tensor_mul(r1, x2, cs),
                     dep={"v": p.cnt["v"]})
                p.op("v", lambda e, x1=x1, sn=sn, r2=r2: e.tensor_mul(r2, x1, sn),
                     dep={"v": p.cnt["v"]})
                ppfree[j % 4] = {"v": p.cnt["v"]}
                p.op("v", lambda e, o2=o2, r1=r1, r2=r2: e.tensor_add(o2, r1, r2),
                     dep={"v": p.cnt["v"]})
                ropm = p.cnt["v"]
                for q in range(4):
                    si = q % 2
                    p.op("t", lambda e, q=q: e.transpose(
                        pb2[q % 2][:, :], ropo[:, q * 128:(q + 1) * 128],
                        idbf_sb[:, :]),
                        dep={"v": max(ropm, pbfree[q % 2])})
                    tm2 = p.cnt["t"]
                    p.op("v", lambda e, si=si, q=q, t=t: e.tensor_copy(
                        qrfm[q][:, t * 128:(t + 1) * 128], pb2[si][:, :]),
                        dep={"t": tm2})
                    pbfree[q % 2] = p.cnt["v"]

        kw_done = M(p.mark("kw0", "kw1", "kw2"), p.mark("t", "v", "s"))

        import os as _os2
        if _os2.environ.get("K_PHASE", "full") == "p2":
            with nc.Block() as block:
                p.build(block)
            return nc

        with ExitStack() as stB:
            EB = stB.enter_context
            y_all = [EB(nc.sbuf_tensor(f"yall{c}", [128, T], BF16))
                     for c in range(8)]
            wo = EB(nc.sbuf_tensor("wo_s", [128, 8 * 1024], BF16))
            stB1 = EB(ExitStack())
            EB1 = stB1.enter_context
            qsb2 = [EB1(nc.sbuf_tensor(f"qsb{i}", [97, T], BF16))
                    for i in range(3)]
            ksb2 = [EB1(nc.sbuf_tensor(f"ksb{i}", [97, T], BF16))
                    for i in range(3)]
            pbuf = EB1(nc.sbuf_tensor("pbuf", [128, 8192], BF16))
            m_all2 = [EB1(nc.sbuf_tensor(f"m_all{i}", [128, 65], F32))
                      for i in range(2)]
            mpart = EB1(nc.sbuf_tensor("mpart", [128, 4], F32))
            mT2 = [EB1(nc.sbuf_tensor(f"mT{i}", [16, 128], BF16))
                   for i in range(2)]
            dn64 = EB1(nc.sbuf_tensor("dn64", [128, 512], F32))
            ysb = EB1(nc.sbuf_tensor("ysb", [64, 512], F32))
            bcw = EB1(nc.sbuf_tensor("bcw", [128, 65], F32))
            psA2 = [EB1(nc.psum_tensor(f"psA{i}", [128, 512], F32))
                    for i in range(3)]
            psB = EB1(nc.psum_tensor("psB", [128, 2048], F32))
            psY = EB1(nc.psum_tensor("psY", [65, 512], F32))

            p.dma("g", wo[:, :].rearrange("p (c n) -> p c n", n=1024),
                  wo_in.rearrange("(c p) n -> p c n", p=128), ch="w1",
                  dep=kw_done)

            for bk in range(4):
                p.op("v", lambda e, bk=bk: e.memset(
                    psB[:, bk * 512:(bk + 1) * 512], 0.0), dep=kw_done)
            psb0 = {"v": p.cnt["v"]}
            p.op("v", lambda e: e.memset(bcw[:, :], 0.0), dep=kw_done)
            p.op("v", lambda e: e.memset(bcw[64:65, :], 1.0),
                 dep={"v": p.cnt["v"]})
            p.op("v", lambda e: e.memset(dn64[:, :], 0.0), dep=kw_done)
            for i in range(2):
                p.op("v", lambda e, i=i: e.memset(m_all2[i][:, :], 0.0),
                     dep=kw_done)

            pieces = []
            for qt in range(NT):
                span = (qt + 1) * 128
                off = 0
                while off < span:
                    w = min(512, span - off)
                    pieces.append((qt, off, w, off == 0,
                                   off + w >= (qt + 1) * 128))
                    off += w

            state = {
                "qkfree": [{}, {}, {}],
                "ldm": [{}, {}, {}],
                "mfree": [0, 0],
                "redm": {},
                "pc": 0,
                "gpar": 0,
                "psB_guard": [dict(psb0), dict(psb0)],
                "negm": [0, 0],
                "row96": [0, 0],
                "dnm": 0,
                "ysbm": 0,
                "ttm": 0,
                "bcm": 0,
                "rcpv": 0,
            }

            for par in range(3):
                p.dma("g", ksb2[par][64:96, :], krfm[0:32, :],
                      ch=f"hk{par}", dep=kw_done)
                p.op("v", lambda e, par=par: e.memset(
                    ksb2[par][96:97, :], 1.0), dep=kw_done)
            qkinit = {"v": p.cnt["v"]}

            def emit_loads(h):
                par = h % 3
                dep = dict(state["qkfree"][par])
                if h < 2:
                    dep = M(dep, kw_done)
                p.dma("g", qsb2[par][0:64, :], qhat[h, 0:64, :],
                      ch=f"hq{par}", dep=dep)
                p.dma("g", qsb2[par][64:96, :],
                      qrfm[h // 4][32 * (h % 4):32 * (h % 4) + 32, :],
                      ch=f"hq{par}", dep=None)
                p.dma("g", ksb2[par][0:64, :], khat[h, 0:64, :],
                      ch=f"hk{par}", dep=dep)
                state["ldm"][par] = M({f"hq{par}": p.cnt[f"hq{par}"],
                                       f"hk{par}": p.cnt[f"hk{par}"], **cm},
                                      qkinit)

            def gen_A(h):
                par = h % 3
                mpar = h % 2
                qsb, ksb = qsb2[par], ksb2[par]
                m_all = m_all2[mpar]
                ldm = state["ldm"][par]
                for qt, off, w, first, diag in pieces:
                    pc = state["pc"]
                    state["pc"] += 1
                    bank = psA2[pc % 3]
                    dep = dict(ldm)
                    if pc >= 3:
                        dep["v"] = max(dep.get("v", 0),
                                       state["redm"][pc - 3])
                    p.op("t", lambda e, qt=qt, off=off, w=w, bank=bank:
                         e.matmul(bank[:, 0:w],
                                  qsb[0:96, qt * 128:(qt + 1) * 128],
                                  ksb[0:96, off:off + w],
                                  start=True, stop=True), dep=dep)
                    tm = p.cnt["t"]
                    vdep = {"t": tm, "v": p.cnt["v"]}
                    if diag:
                        dcol = qt * 128 - off
                        p.op("v", lambda e, bank=bank, dcol=dcol: e.tensor_add(
                            bank[:, dcol:dcol + 128],
                            bank[:, dcol:dcol + 128], m0_sb[:, :]), dep=vdep)
                        vdep = {"v": p.cnt["v"]}
                    if first:
                        dd = dict(vdep)
                        dd["t"] = max(dd.get("t", 0), state["mfree"][mpar])
                        p.op("v", lambda e, qt=qt, w=w, bank=bank: e.reduce_max(
                            m_all[:, qt:qt + 1], bank[:, 0:w], axis=AX.X),
                            dep=dd)
                    else:
                        p.op("v", lambda e, w=w, bank=bank, pc=pc: e.reduce_max(
                            mpart[:, pc % 4:pc % 4 + 1], bank[:, 0:w],
                            axis=AX.X), dep=vdep)
                        p.op("v", lambda e, qt=qt, pc=pc: e.tensor_max(
                            m_all[:, qt:qt + 1], m_all[:, qt:qt + 1],
                            mpart[:, pc % 4:pc % 4 + 1]),
                            dep={"v": p.cnt["v"]})
                    state["redm"][pc] = p.cnt["v"]
                    yield
                p.op("t", lambda e: e.transpose(psB[0:65, 0:128], m_all[:, :],
                                                idf_sb[:, :]),
                     dep=M({"v": state["redm"][state["pc"] - 1], **cm},
                           state["psB_guard"][0]))
                state["mfree"][mpar] = p.cnt["t"]
                p.op("s", lambda e: e.activation(mT2[mpar][:, :],
                                                 psB[0:16, 0:128],
                                                 AF.Copy, scale=-1.0),
                     dep={"t": p.cnt["t"], "mw": p.cnt.get("mw", 0)})
                state["negm"][mpar] = p.cnt["s"]
                state["psB_guard"][0] = M(state["psB_guard"][0],
                                          {"s": p.cnt["s"]})
                p.dma("y", qhat[h, 96:97, :].rearrange("o (a b) -> (o a) b",
                                                       b=128),
                      mT2[mpar][:, :], ch="mw", dep={"s": p.cnt["s"]})
                p.dma("y", qsb[96:97, :], qhat[h, 96:97, :], ch="r96",
                      dep={"mw": p.cnt["mw"]})
                state["row96"][par % 2] = p.cnt["r96"]

            def gen_B(h):
                par = h % 3
                mpar = h % 2
                qsb, ksb = qsb2[par], ksb2[par]
                firstmm = True
                for b in range(4):
                    nch = 4 * (b + 1)
                    for g0 in range(0, nch, 2):
                        ng = min(2, nch - g0)
                        gi = state["gpar"]
                        state["gpar"] = 1 - gi
                        for ci in range(ng):
                            ct = g0 + ci
                            dep = None
                            if ci == 0:
                                dep = dict(state["psB_guard"][gi])
                                if firstmm:
                                    dep = M(dep, {"s": state["negm"][mpar]})
                                    dep["r96"] = state["row96"][par % 2]
                                    firstmm = False
                            cut = max(0, 128 * ct - 512 * b)
                            p.op("t", lambda e, ct=ct, ci=ci, b=b, gi=gi,
                                 cut=cut: e.matmul(
                                     psB[:, gi * 1024 + ci * 512 + cut:
                                         gi * 1024 + (ci + 1) * 512],
                                     ksb[:, ct * 128:(ct + 1) * 128],
                                     qsb[:, b * 512 + cut:(b + 1) * 512],
                                     start=True, stop=True), dep=dep,
                                 sem="tb")
                        p.op("s", lambda e, g0=g0, ng=ng, gi=gi: e.activation(
                            pbuf[:, g0 * 512:(g0 + ng) * 512],
                            psB[:, gi * 1024:gi * 1024 + ng * 512], AF.Exp),
                            dep={"tb": p.cnt["tb"], "g": p.cnt["g"]},
                            sem="sb")
                        state["psB_guard"][gi] = {"sb": p.cnt["sb"]}
                        em = p.cnt["sb"]
                        for ci in range(ng):
                            ct = g0 + ci
                            if ct >= 4 * b:
                                p.op("g", lambda e, ct=ct, b=b: e.affine_select(
                                    pbuf[:, ct * 512:(ct + 1) * 512],
                                    pbuf[:, ct * 512:(ct + 1) * 512],
                                    pattern=[[1, 512]], compare_op=ALU.is_ge,
                                    fill=0.0, base=512 * b - 128 * ct,
                                    channel_multiplier=-1), dep={"sb": em})
                        yield
                    for ct in range(nch):
                        cut = max(0, 128 * ct - 512 * b)
                        dep = None
                        if ct == 0:
                            dep = M({"sb": max(p.cnt["sb"], state["ysbm"]),
                                     "g": p.cnt["g"],
                                     "vb": state["ttm"],
                                     "tb": state["bcm"]}, vhat_done)
                        p.op("t", lambda e, ct=ct, h=h, nch=nch, cut=cut:
                             e.matmul(
                                 psY[:, cut:512],
                                 vhat[ct][:, h * 65:(h + 1) * 65],
                                 pbuf[:, ct * 512 + cut:(ct + 1) * 512],
                                 start=(ct == 0), stop=(ct == nch - 1)),
                             dep=dep, sem="tb")
                    vml = p.cnt["tb"]
                    p.op("v", lambda e, h=h: e.tensor_scalar_add(
                        dn64[64:65, :], psY[64:65, :],
                        c_all[64:65, h:h + 1]),
                        dep={"tb": max(vml, state["bcm"]),
                             "vb": state["rcpv"]}, sem="vb")
                    p.op("s", lambda e: e.activation(ysb[:, :], psY[0:64, :],
                                                     AF.Copy),
                         dep={"tb": vml, "vb": state["ttm"]}, sem="sb")
                    state["ysbm"] = p.cnt["sb"]
                    p.op("v", lambda e: e.reciprocal(dn64[64:65, :],
                                                     dn64[64:65, :]),
                         dep={"vb": p.cnt["vb"]},
                         sem="vb")
                    state["rcpv"] = p.cnt["vb"]
                    rcm2 = p.cnt["vb"]
                    p.op("t", lambda e: e.matmul(psY[0:65, :],
                                                 bcw[:, :],
                                                 dn64[:, :],
                                                 start=True, stop=True),
                         dep={"vb": rcm2, "sb": state["ysbm"]},
                         sem="tb")
                    state["bcm"] = p.cnt["tb"]
                    p.op("v", lambda e, h=h, b=b: e.tensor_mul(
                        y_all[h // 2][64 * (h % 2):64 * (h % 2) + 64,
                                      b * 512:(b + 1) * 512],
                        ysb[:, :], psY[0:64, :]),
                        dep={"sb": p.cnt["sb"], "tb": p.cnt["tb"],
                             "vb": p.cnt.get("vb", 0)}, sem="vb")
                    state["ttm"] = p.cnt["vb"]
                    yield
                state["qkfree"][par] = {"tb": p.cnt["tb"]}

            def drain(gen, n):
                for _ in range(n):
                    try:
                        next(gen)
                    except StopIteration:
                        return False
                return True

            import os as _os
            ILEAVE = _os.environ.get("K_ILEAVE", "1") == "1"
            emit_loads(0)
            emit_loads(1)
            ga = gen_A(0)
            drain(ga, 10 ** 9)
            for h in range(H):
                if h + 2 < H:
                    emit_loads(h + 2)
                if h + 1 < H:
                    ga = gen_A(h + 1)
                else:
                    ga = None
                gb = gen_B(h)
                if not ILEAVE:
                    if ga is not None:
                        drain(ga, 10 ** 9)
                    drain(gb, 10 ** 9)
                else:
                    while True:
                        alive_a = drain(ga, 3) if ga is not None else False
                        alive_b = drain(gb, 1)
                        if not alive_b and not alive_a:
                            break

            yall_done = {"vb": p.cnt.get("vb", 0), "v": p.cnt["v"]}

            if _os2.environ.get("K_PHASE", "full") == "attn":
                for c in range(8):
                    for hf in range(2):
                        p.dma("g", xp_out.rearrange("(c p) n -> c p n", p=128)
                              [2 * c + hf, :, :],
                              y_all[c][:, hf * 1024:(hf + 1) * 1024],
                              ch="dbg", dep="all")
                with nc.Block() as block:
                    p.build(block)
                return nc

            stB1.close()
            pend = p.mark("t", "v", "s", "g")
            with ExitStack() as stC:
                EC = stC.enter_context
                aofm = EC(nc.sbuf_tensor("aofm", [128, 8 * 512], BF16))
                aot = EC(nc.sbuf_tensor("aot", [128, 1024], F32))
                xt2 = [EC(nc.sbuf_tensor(f"xt2_{i}", [128, 1024], F32))
                       for i in range(2)]
                xp22 = [EC(nc.sbuf_tensor(f"xp2_{i}", [128, 1024], F32))
                        for i in range(2)]
                x2bf2 = [EC(nc.sbuf_tensor(f"x2bf{i}", [128, 1024], BF16))
                         for i in range(2)]
                x2fm2 = [EC(nc.sbuf_tensor(f"x2fm{i}", [128, 8 * 128], BF16))
                         for i in range(2)]
                gexp = EC(nc.sbuf_tensor("gexp", [128, 16], F32))
                grt2 = [EC(nc.sbuf_tensor(f"grt{i}", [128, 16], F32))
                        for i in range(2)]
                oneh2 = [EC(nc.sbuf_tensor(f"oneh{i}", [128, 16], F32))
                         for i in range(2)]
                xacc = EC(nc.sbuf_tensor("xacc", [128, 16], F32))
                lloct2 = [EC(nc.sbuf_tensor(f"lloct{i}", [128, 16], F32))
                          for i in range(2)]
                gvec2 = [EC(nc.sbuf_tensor(f"gvec{i}", [128, 8], F32))
                         for i in range(2)]
                cnt_sb = EC(nc.sbuf_tensor("cnt_sb", [1, 16], F32))
                pw2 = [EC(nc.psum_tensor(f"pw{i}", [128, 512], F32))
                       for i in range(2)]
                pao2 = [EC(nc.psum_tensor(f"pao{i}", [128, 128], BF16))
                        for i in range(2)]
                psG2 = [EC(nc.psum_tensor(f"psG{i}", [128, 16], F32))
                        for i in range(2)]
                psL = EC(nc.psum_tensor("psL", [128, 16], F32))
                psC = EC(nc.psum_tensor("psC", [1, 16], F32))

                p.op("v", lambda e: e.memset(xacc[:, :], 0.0), dep=pend)
                pwfree = [0, 0]
                paofree = [0, 0]
                xfree = [dict(pend), dict(pend)]
                xpfree = [{}, {}]
                x2bfree = [0, 0]
                x2fmfree = [0, 0]
                psGfree = [{}, {}]
                gvfree = [{}, {}]
                wom = p.mark("w1")
                for B in range(4):
                    for o in range(8):
                        for c in range(8):
                            dep = None
                            if c == 0:
                                dep = {**yall_done, **wom, **cm,
                                       "s": pwfree[o % 2]}
                            p.op("t", lambda e, B=B, o=o, c=c: e.matmul(
                                pw2[o % 2][:, :],
                                wo[:, c * 1024 + o * 128:
                                   c * 1024 + o * 128 + 128],
                                y_all[c][:, B * 512:(B + 1) * 512],
                                start=(c == 0), stop=(c == 7)), dep=dep)
                        p.op("s", lambda e, o=o: e.activation(
                            aofm[:, o * 512:(o + 1) * 512],
                            pw2[o % 2][:, :], AF.Copy),
                            dep={"t": p.cnt["t"]})
                        pwfree[o % 2] = p.cnt["s"]
                    aom = p.cnt["s"]
                    for u in range(4):
                        t = 4 * B + u
                        i = t % 2
                        psG = psG2[i]
                        gvec = gvec2[i]
                        grt = grt2[i]
                        oneh = oneh2[i]
                        for o in range(8):
                            p.op("t", lambda e, o=o, u=u: e.transpose(
                                pao2[o % 2][:, :],
                                aofm[:, o * 512 + u * 128:
                                     o * 512 + u * 128 + 128],
                                idbf_sb[:, :]),
                                dep={"s": aom, "v": paofree[o % 2]})
                            tm = p.cnt["t"]
                            p.op("v", lambda e, o=o: e.tensor_copy(
                                aot[:, o * 128:(o + 1) * 128],
                                pao2[o % 2][:, :]),
                                dep={"t": tm, "v": p.cnt["v"]})
                            paofree[o % 2] = p.cnt["v"]
                        p.dma("g", xt2[i][:, :],
                              x_in[t * 128:(t + 1) * 128, :],
                              ch=f"x{i}", dep=xfree[i])
                        p.op("v", lambda e, i=i: e.tensor_add(
                            xp22[i][:, :], xt2[i][:, :], aot[:, :]),
                            dep=M({f"x{i}": p.cnt[f"x{i}"],
                                   "v": p.cnt["v"]}, xpfree[i]))
                        xfree[i] = {"v": p.cnt["v"]}
                        p.dma("g", xp_out[t * 128:(t + 1) * 128, :],
                              xp22[i][:, :], ch="ow0", dep={"v": p.cnt["v"]})
                        lnr = ln_pass(p, xp22[i][:, :], scr2[i], vec2[i],
                                      x2bf2[i][:, :],
                                      dep={"v": p.cnt["v"], "s": p.cnt["s"]},
                                      dep_last={"t": x2bfree[i],
                                                "ow1": p.cnt.get("ow1", 0)})
                        xpfree[i] = M(lnr, {"ow0": p.cnt["ow0"]})
                        lnm = p.cnt["v"]
                        p.dma("g", x2_out[t * 128:(t + 1) * 128, :],
                              x2bf2[i][:, :], ch="ow1", dep={"v": lnm})
                        for c in range(8):
                            p.op("t", lambda e, c=c, i=i: e.transpose(
                                pao2[c % 2][:, :],
                                x2bf2[i][:, c * 128:(c + 1) * 128],
                                idbf_sb[:, :]),
                                dep={"v": max(lnm, paofree[c % 2])})
                            tm = p.cnt["t"]
                            p.op("v", lambda e, c=c, i=i: e.tensor_copy(
                                x2fm2[i][:, c * 128:(c + 1) * 128],
                                pao2[c % 2][:, :]),
                                dep={"t": max(tm, x2fmfree[i]),
                                     "v": p.cnt["v"]})
                            paofree[c % 2] = p.cnt["v"]
                        x2bfree[i] = p.cnt["t"]
                        x2m = p.cnt["v"]
                        for c in range(8):
                            p.op("t", lambda e, c=c, i=i, psG=psG: e.matmul(
                                psG[:, :],
                                x2fm2[i][:, c * 128:(c + 1) * 128],
                                wg[:, c * 16:(c + 1) * 16],
                                start=(c == 0), stop=(c == 7)),
                                dep=M({"v": x2m, **wm},
                                      psGfree[i]) if c == 0 else None)
                        gmm = p.cnt["t"]
                        x2fmfree[i] = gmm
                        gm = gvec[:, 0:1]
                        gmn = gvec[:, 1:2]
                        gsum = gvec[:, 2:3]
                        gsr = gvec[:, 3:4]
                        idxv = gvec[:, 5:6]
                        p.op("v", lambda e, gm=gm, psG=psG: e.reduce_max(
                            gm, psG[:, :], axis=AX.X),
                            dep=M({"t": gmm, "v": p.cnt["v"]}, gvfree[i]))
                        p.op("v", lambda e, gm=gm, gmn=gmn:
                             e.tensor_scalar_mul(gmn, gm, -1.0),
                             dep={"v": p.cnt["v"]})
                        gmnm = p.cnt["v"]
                        p.op("s", lambda e, gmn=gmn, gsum=gsum, psG=psG:
                             e.activation(
                                 gexp[:, :], psG[:, :], AF.Exp, bias=gmn,
                                 accum_out=gsum),
                             dep={"v": gmnm, "s": p.cnt["s"], "t": gmm})
                        sm = p.cnt["s"]
                        p.op("v", lambda e, grt=grt, gm=gm, psG=psG:
                             e.tensor_scalar_sub(grt[:, :], psG[:, :], gm),
                             dep={"v": p.cnt["v"]})
                        p.op("v", lambda e, grt=grt: e.tensor_scalar(
                            grt[:, :], grt[:, :], -1e30, 1.0,
                            ALU.mult, ALU.min), dep={"v": p.cnt["v"]})
                        p.op("v", lambda e, grt=grt: e.scalar_tensor_tensor(
                            grt[:, :], grt[:, :], -1.0, iom99_sb[:, :],
                            ALU.mult, ALU.mult), dep={"v": p.cnt["v"]})
                        p.op("v", lambda e, grt=grt: e.tensor_add(
                            grt[:, :], grt[:, :], iota16_sb[:, :]),
                            dep={"v": p.cnt["v"]})
                        p.op("v", lambda e, grt=grt, idxv=idxv:
                             e.tensor_reduce(
                                 idxv, grt[:, :], axis=AX.X, op=ALU.min),
                             dep={"v": p.cnt["v"]})
                        ivm = p.cnt["v"]
                        p.dma("y", idx_out[t * 128:(t + 1) * 128, :], idxv,
                              ch="ow3", dep={"v": ivm})
                        p.op("v", lambda e, oneh=oneh, idxv=idxv:
                             e.tensor_scalar_mul(
                                 oneh[:, :], ones16_sb[:, :], idxv),
                             dep={"v": ivm, "t": p.cnt["t"]})
                        p.op("v", lambda e, oneh=oneh: e.tensor_sub(
                            oneh[:, :], oneh[:, :], iota16_sb[:, :]),
                            dep={"v": p.cnt["v"]})
                        p.op("v", lambda e, oneh=oneh: e.tensor_mul(
                            oneh[:, :], oneh[:, :], oneh[:, :]),
                            dep={"v": p.cnt["v"]})
                        p.op("v", lambda e, oneh=oneh: e.tensor_scalar(
                            oneh[:, :], oneh[:, :], 1.0, -1.0,
                            ALU.min, ALU.mult), dep={"v": p.cnt["v"]})
                        p.op("v", lambda e, oneh=oneh: e.tensor_scalar_add(
                            oneh[:, :], oneh[:, :], 1.0),
                            dep={"v": p.cnt["v"]})
                        ohm = p.cnt["v"]
                        p.op("t", lambda e, oneh=oneh: e.matmul(
                            psL[:, :], trilS_sb[:, :], oneh[:, :],
                            start=True, stop=False),
                            dep={"v": ohm})
                        p.op("t", lambda e: e.matmul(
                            psL[:, :], onesf_sb[:, :], xacc[:, :],
                            start=False, stop=True), dep=None)
                        tm = p.cnt["t"]
                        p.op("v", lambda e, i=i: e.tensor_copy(
                            lloct2[i][:, :], psL[:, :]),
                             dep={"t": tm, "ow3": p.cnt.get("ow3", 0),
                                  "v": p.cnt["v"]})
                        p.dma("y", lloc_out[t, :, :], lloct2[i][:, :],
                              ch="ow3", dep={"v": p.cnt["v"]})
                        p.op("v", lambda e, oneh=oneh: e.tensor_add(
                            xacc[:, :], xacc[:, :], oneh[:, :]),
                            dep={"t": tm, "v": p.cnt["v"]})
                        p.op("v", lambda e, gsr=gsr, gsum=gsum: e.reciprocal(
                            gsr, gsum), dep={"s": sm, "v": p.cnt["v"]})
                        p.dma("y", gs_out[t * 128:(t + 1) * 128, :], gsr,
                              ch="ow2", dep={"v": p.cnt["v"]})
                        psGfree[i] = {"v": p.cnt["v"], "s": sm}
                        gvfree[i] = {"s": sm, "ow2": p.cnt["ow2"],
                                     "ow3": p.cnt["ow3"]}
                p.op("t", lambda e: e.matmul(psC[:, :], onescol_sb[:, :],
                                             xacc[:, :], start=True,
                                             stop=True),
                     dep={"v": p.cnt["v"], **cm})
                p.op("v", lambda e: e.tensor_copy(cnt_sb[:, :], psC[:, :]),
                     dep={"t": p.cnt["t"]})
                p.dma("y", cnt_out, cnt_sb[:, :], ch="ow2",
                      dep={"v": p.cnt["v"]})

        with nc.Block() as block:
            p.build(block)
    return nc

def build_L2():
    nc = bass.Bass("TRN2", target_bir_lowering=False, debug=False, num_swdge_queues=4)
    cnts_in = nc.dram_tensor("cnts", [NCORE, NEXP], F32, kind="ExternalInput").ap()
    lloc_in = nc.dram_tensor("lloc", [NT, 128, NEXP], F32, kind="ExternalInput").ap()
    idx_in = nc.dram_tensor("idxT", [128, NT], F32, kind="ExternalInput").ap()
    toki_in = nc.dram_tensor("tokiT", [128, NT], F32, kind="ExternalInput").ap()
    sel_in = nc.dram_tensor("sel", [NCORE, 1], F32, kind="ExternalInput").ap()

    gpart_out = nc.dram_tensor("gpart", [NCORE * T, 1], F32, kind="ExternalOutput").ap()
    slotp_out = nc.dram_tensor("slotp", [T, 1], F32, kind="ExternalOutput").ap()

    iota16 = nc.inline_tensor(np.tile(np.arange(16, dtype=np.float32), (128, 1)),
                              "iota16").ap()
    ones16 = nc.inline_tensor(np.ones((128, 16), np.float32), "ones16").ap()
    onesrow128 = nc.inline_tensor(np.ones((1, 128), np.float32), "or128").ap()
    idf2 = nc.inline_tensor(np.eye(128, dtype=np.float32), "idf2").ap()

    with ExitStack() as st:
        EE = st.enter_context
        p = P(nc, st)
        cnts = EE(nc.sbuf_tensor("cnts_s", [NCORE, NEXP], F32))
        or128_sb = EE(nc.sbuf_tensor("or128_sb", [1, 128], F32))
        sel = EE(nc.sbuf_tensor("sel_s", [NCORE, 1], F32))
        offr = EE(nc.sbuf_tensor("offr", [1, 16], F32))
        offb = EE(nc.sbuf_tensor("offb", [128, 16], F32))
        idxT = EE(nc.sbuf_tensor("idxT_s", [128, NT], F32))
        tokiT = EE(nc.sbuf_tensor("tokiT_s", [128, NT], F32))
        toki_u = EE(nc.sbuf_tensor("toki_u", [128, NT], F32))
        lloct = EE(nc.sbuf_tensor("lloct", [128, 16], F32))
        oneh = EE(nc.sbuf_tensor("oneh", [128, 16], F32))
        big = EE(nc.sbuf_tensor("big", [128, 128], F32))
        slots = EE(nc.sbuf_tensor("slots", [128, NT], F32))
        slots_u = EE(nc.sbuf_tensor("slots_u", [128, NT], U32))
        v1 = EE(nc.sbuf_tensor("v1", [128, 8], F32))
        i16 = EE(nc.sbuf_tensor("i16", [128, 16], F32))
        o16 = EE(nc.sbuf_tensor("o16", [128, 16], F32))
        idf_sb2 = EE(nc.sbuf_tensor("idf_sb2", [128, 128], F32))
        slotsT = EE(nc.sbuf_tensor("slotsT", [16, 128], F32))
        psO = EE(nc.psum_tensor("psO", [1, 16], F32))
        psB = EE(nc.psum_tensor("psB", [128, 16], F32))
        psT = EE(nc.psum_tensor("psT", [16, 128], F32))

        p.dma("y", cnts[:, :], cnts_in)
        p.dma("y", sel[:, :], sel_in)
        p.dma("y", idxT[:, :], idx_in)
        p.dma("y", tokiT[:, :], toki_in)
        p.dma("y", i16[:, :], iota16)
        p.dma("y", o16[:, :], ones16)
        p.dma("y", or128_sb[:, :], onesrow128)
        p.dma("y", idf_sb2[:, :], idf2)
        p.op("t", lambda e: e.matmul(psO[:, :], sel[:, :], cnts[:, :],
                                     start=True, stop=True))
        p.op("v", lambda e: e.tensor_copy(offr[:, :], psO[:, :]))
        p.op("t", lambda e: e.matmul(psB[:, :], or128_sb[:, :], offr[:, :],
                                     start=True, stop=True))
        p.op("v", lambda e: e.tensor_copy(offb[:, :], psB[:, :]))
        p.op("v", lambda e: e.memset(big[:, :], BIGSLOT))
        for r in range(NCORE):
            p.dma("y", gpart_out[r * T:(r + 1) * T, :]
                  .rearrange("(a b) o -> a (b o)", b=128), big[0:16, :])

        for t in range(NT):
            idxv = idxT[:, t:t + 1]
            gl = v1[:, 0:1]
            ofs = v1[:, 1:2]
            pen = v1[:, 2:3]
            p.dma("y", lloct[:, :], lloc_in[t, :, :])
            p.op("v", lambda e, idxv=idxv: e.tensor_scalar_mul(oneh[:, :],
                                                               o16[:, :], idxv))
            p.op("v", lambda e: e.tensor_sub(oneh[:, :], oneh[:, :], i16[:, :]))
            p.op("v", lambda e: e.tensor_mul(oneh[:, :], oneh[:, :], oneh[:, :]))
            p.op("v", lambda e: e.tensor_scalar_min(oneh[:, :], oneh[:, :], 1.0))
            p.op("s", lambda e: e.activation(oneh[:, :], oneh[:, :], AF.Copy,
                                             scale=-1.0, bias=1.0))
            p.op("v", lambda e: e.tensor_mul(lloct[:, :], lloct[:, :], oneh[:, :]))
            p.op("v", lambda e, gl=gl: e.reduce_sum(gl, lloct[:, :], axis=AX.X))
            p.op("v", lambda e: e.tensor_mul(oneh[:, :], offb[:, :], oneh[:, :]))
            p.op("v", lambda e, ofs=ofs: e.reduce_sum(ofs, oneh[:, :], axis=AX.X))
            p.op("v", lambda e, gl=gl, ofs=ofs: e.tensor_add(gl, gl, ofs))
            p.op("v", lambda e, pen=pen, gl=gl: e.tensor_scalar_add(
                pen, gl, -(CAP - 1.0)))
            p.op("v", lambda e, pen=pen: e.tensor_scalar_max(pen, pen, 0.0))
            p.op("v", lambda e, pen=pen: e.tensor_scalar_min(pen, pen, 1.0))
            p.op("s", lambda e, pen=pen: e.activation(pen, pen, AF.Copy,
                                                      scale=BIGSLOT))
            sl = slots[:, t:t + 1]
            p.op("v", lambda e, sl=sl, idxv=idxv: e.tensor_scalar_mul(
                sl, idxv, float(CAP)))
            p.op("v", lambda e, sl=sl, gl=gl: e.tensor_add(sl, sl, gl))
            p.op("v", lambda e, sl=sl, pen=pen: e.tensor_add(sl, sl, pen))
        p.op("v", lambda e: e.tensor_copy(slots_u[:, :], slots[:, :]))
        p.op("t", lambda e: e.transpose(psT[:, :], slots[:, :], idf_sb2[:, :]))
        p.op("v", lambda e: e.tensor_copy(slotsT[:, :], psT[0:16, :]))
        p.dma("y", slotp_out.rearrange("(a b) o -> a (b o)", b=128),
              slotsT[:, :])
        for t in range(NT):
            p.idma("g", lambda e, t=t: e.indirect_dma_start(
                out=gpart_out[:, :],
                out_offset=IOA(slots_u[:, t:t + 1], 0),
                in_=tokiT[:, t:t + 1],
                in_offset=None,
                bounds_check=NCORE * T - 1, oob_is_err=False))

        with nc.Block() as block:
            p.build(block)
    return nc


def build_L3():
    nc = bass.Bass("TRN2", target_bir_lowering=False, debug=False, num_swdge_queues=4)
    xall_in = nc.dram_tensor("xall", [NCORE * T, E], BF16, kind="ExternalInput").ap()
    gm_in = nc.dram_tensor("gmy", [128, NCORE * NT], F32, kind="ExternalInput").ap()
    w1_in = nc.dram_tensor("w1own", [2, E, FF], F32, kind="ExternalInput").ap()
    w2_in = nc.dram_tensor("w2own", [2, FF, E], F32, kind="ExternalInput").ap()

    co_out = nc.dram_tensor("contrib", [T, E], BF16, kind="ExternalOutput").ap()

    idbf = nc.inline_tensor(np.eye(128, dtype=ml_dtypes.bfloat16), "idbf").ap()

    with ExitStack() as st:
        EE = st.enter_context
        p = P(nc, st)
        idbf_sb = EE(nc.sbuf_tensor("idbf_sb", [128, 128], BF16))
        gm = EE(nc.sbuf_tensor("gm_s", [128, NCORE * NT], F32))
        gmin_u = EE(nc.sbuf_tensor("gmin_u", [128, NT], U32))
        w2sb = EE(nc.sbuf_tensor("w2sb", [128, 32 * 1024], BF16))
        w1sb2 = [EE(nc.sbuf_tensor(f"w1sb{i}", [128, 8 * 2048], BF16))
                 for i in range(2)]
        hsb = [EE(nc.sbuf_tensor(f"hsb{i}", [128, 16 * 512], BF16))
               for i in range(2)]
        xd_tm4 = [EE(nc.sbuf_tensor(f"xd_tm{i}", [128, 1024], BF16))
                  for i in range(4)]
        xd_fm2 = [[EE(nc.sbuf_tensor(f"xdfm{i}_{c}", [128, 512], BF16))
                   for c in range(8)] for i in range(2)]
        ost2 = [EE(nc.sbuf_tensor(f"ost{i}", [128, 512], BF16))
                for i in range(2)]
        po = [EE(nc.psum_tensor(f"po{i}", [128, 512], F32)) for i in range(6)]
        pb2 = [EE(nc.psum_tensor(f"pbx{i}", [128, 128], BF16))
               for i in range(2)]

        p.dma("y", idbf_sb[:, :], idbf, ch="c0", dep=None)
        p.dma("y", gm[:, :], gm_in, ch="c0", dep=None)
        cm = p.mark("c0")
        p.op("v", lambda e: e.tensor_tensor(
            gm[:, 0:NT], gm[:, 0:NT], gm[:, NT:2 * NT], ALU.min), dep=cm)
        for r in range(2, NCORE):
            p.op("v", lambda e, r=r: e.tensor_tensor(
                gm[:, 0:NT], gm[:, 0:NT],
                gm[:, r * NT:(r + 1) * NT], ALU.min), dep={"v": p.cnt["v"]})
        p.op("v", lambda e: e.tensor_copy(gmin_u[:, :], gm[:, 0:NT]),
             dep={"v": p.cnt["v"]})
        gum = {"v": p.cnt["v"]}

        p.dma("g", w1sb2[0][:, :].rearrange("p (c n) -> p c n", n=2048),
              w1_in[0, :, 0:2048].rearrange("(c p) n -> p c n", p=128),
              ch="wb", dep=None)
        wam = [None] * 8
        for cch in range(8):
            ch = f"wa{cch % 2}"
            p.dma("g", w2sb[:, cch * 4096:(cch + 1) * 4096]
                  .rearrange("p (c n) -> p c n", n=1024),
                  w2_in[0, cch * 512:(cch + 1) * 512, :]
                  .rearrange("(c p) n -> p c n", p=128),
                  ch=ch, dep=None)
            wam[cch] = (ch, p.cnt[ch])

        xdm = [0, 0, 0, 0]
        relum = [0, 0, 0, 0, 0, 0]
        outm_prev = 0
        ostm = [{}, {}]
        gimk = {}
        xdc2 = [0, 0]
        pbfree = [0, 0]

        def emit_gathers(blk):
            for u in range(4):
                j = blk * 4 + u
                p.op("v", lambda e, u=u: e.memset(xd_tm4[u][:, :], 0.0),
                     dep={"t": xdm[u], "v": p.cnt["v"]})
                p.idma("g", lambda e, j=j, u=u: e.indirect_dma_start(
                    out=xd_tm4[u][:, :], out_offset=None,
                    in_=xall_in[:, :],
                    in_offset=IOA(gmin_u[:, j:j + 1], 0),
                    bounds_check=NCORE * T - 1, oob_is_err=False),
                    ch="gi", dep=M(gum, {"v": p.cnt["v"]}))
                gimk[u] = p.cnt["gi"]

        def emit_gtrans(blk):
            par = blk % 2
            for u in range(4):
                for c in range(8):
                    p.op("t", lambda e, c=c, u=u: e.transpose(
                        pb2[c % 2][:, :],
                        xd_tm4[u][:, c * 128:(c + 1) * 128],
                        idbf_sb[:, :]),
                        dep={"gi": gimk[u], "v": max(p.cnt["v"],
                                                     pbfree[c % 2]), **cm})
                    tm = p.cnt["t"]
                    p.op("v", lambda e, c=c, u=u, par=par: e.tensor_copy(
                        xd_fm2[par][c][:, u * 128:(u + 1) * 128],
                        pb2[c % 2][:, :]),
                        dep={"t": tm})
                    pbfree[c % 2] = p.cnt["v"]
                xdm[u] = p.cnt["t"]
            xdc2[par] = p.cnt["v"]

        emit_gathers(0)
        emit_gtrans(0)
        p.dma("g", w1sb2[1][:, :].rearrange("p (c n) -> p c n", n=2048),
              w1_in[0, :, 2048:4096].rearrange("(c p) n -> p c n", p=128),
              ch="wb1", dep=None)
        w1ld = [("wb", p.cnt["wb"]), ("wb1", p.cnt["wb1"])]
        for ee in range(2):
            if ee == 1:
                for cch in range(8):
                    ch = f"wa{cch % 2}"
                    p.dma("g", w2sb[:, cch * 4096:(cch + 1) * 4096]
                          .rearrange("p (c n) -> p c n", n=1024),
                          w2_in[1, cch * 512:(cch + 1) * 512, :]
                          .rearrange("(c p) n -> p c n", p=128),
                          ch=ch, dep={"t": outm_prev} if cch < 2 else None)
                    wam[cch] = (ch, p.cnt[ch])
            for B in range(2):
                blk = ee * 2 + B
                par = blk % 2
                xdc = xdc2[par]
                if blk + 1 < 4:
                    emit_gathers(blk + 1)
                for fh in range(2):
                    lch, ldm = w1ld[fh]
                    for f in range(16):
                        pi = 4 + (f % 2)
                        for c in range(8):
                            dep = None
                            if c == 0:
                                dep = {"s": relum[pi], "v": xdc,
                                       lch: ldm}
                            p.op("t", lambda e, f=f, c=c, pi=pi, fh=fh,
                                 par=par: e.matmul(
                                     po[pi][:, :],
                                     w1sb2[fh][:, c * 2048 + f * 128:
                                               c * 2048 + f * 128 + 128],
                                     xd_fm2[par][c][:, :],
                                     start=(c == 0), stop=(c == 7)), dep=dep)
                        p.op("s", lambda e, f=f, fh=fh, pi=pi: e.activation(
                            hsb[fh][:, f * 512:(f + 1) * 512],
                            po[pi][:, :], AF.Relu), dep={"t": p.cnt["t"]})
                        relum[pi] = p.cnt["s"]
                    if B == 1 and ee == 0:
                        ch = "wb" if fh == 0 else "wb1"
                        p.dma("g", w1sb2[fh][:, :]
                              .rearrange("p (c n) -> p c n", n=2048),
                              w1_in[1, :, fh * 2048:(fh + 1) * 2048]
                              .rearrange("(c p) n -> p c n", p=128),
                              ch=ch, dep={"t": p.cnt["t"]})
                        w1ld[fh] = (ch, p.cnt[ch])
                hrel = p.cnt["s"]
                if blk + 1 < 4:
                    emit_gtrans(blk + 1)
                for half in range(2):
                    pairs = [(u, dh) for u in range(4) for dh in range(2)][
                        half * 4:(half + 1) * 4]
                    for pi, (u, dh) in enumerate(pairs):
                        for fh in range(2):
                            for f in range(16):
                                dep = None
                                if pi == 0:
                                    wch, wcm = wam[(fh * 16 + f) // 4]
                                    dep = {wch: wcm}
                                if fh == 0 and f == 0:
                                    dep = M(dep or {},
                                            {"s": max(relum[pi], hrel)})
                                p.op("t", lambda e, u=u, dh=dh, fh=fh, f=f,
                                     pi=pi: e.matmul(
                                         po[pi][:, :],
                                         hsb[fh][:, f * 512 + u * 128:
                                                 f * 512 + u * 128 + 128],
                                         w2sb[:, (fh * 16 + f) * 1024 +
                                              dh * 512: (fh * 16 + f) * 1024 +
                                              (dh + 1) * 512],
                                         start=(fh == 0 and f == 0),
                                         stop=(fh == 1 and f == 15)), dep=dep)
                        relum[pi] = 0
                    gend = p.cnt["t"]
                    for pi, (u, dh) in enumerate(pairs):
                        oi = pi % 2
                        p.op("s", lambda e, pi=pi, oi=oi: e.activation(
                            ost2[oi][:, :], po[pi][:, :], AF.Copy),
                            dep=M({"t": gend}, ostm[oi]))
                        relum[pi] = p.cnt["s"]
                        row0 = blk * 512 + u * 128
                        ch = f"co{oi}"
                        p.dma("g", co_out[row0:row0 + 128,
                                          dh * 512:(dh + 1) * 512],
                              ost2[oi][:, :], ch=ch, dep={"s": p.cnt["s"]})
                        ostm[oi] = {ch: p.cnt[ch]}
                outm_prev = p.cnt["t"]

        with nc.Block() as block:
            p.build(block)
    return nc


def build_L4():
    nc = bass.Bass("TRN2", target_bir_lowering=False, debug=False, num_swdge_queues=4)
    call_in = nc.dram_tensor("call", [NCORE * T, E], BF16, kind="ExternalInput").ap()
    slot_in = nc.dram_tensor("slotT", [128, NT], F32, kind="ExternalInput").ap()
    gs_in = nc.dram_tensor("gsT", [128, NT], F32, kind="ExternalInput").ap()
    xp_in = nc.dram_tensor("xp", [T, E], F32, kind="ExternalInput").ap()
    out = nc.dram_tensor("out", [T, E], F32, kind="ExternalOutput").ap()

    with ExitStack() as st:
        EE = st.enter_context
        p = P(nc, st)
        slotT = EE(nc.sbuf_tensor("slotT_s", [128, NT], F32))
        slot_u = EE(nc.sbuf_tensor("slot_u", [128, NT], U32))
        gsT = EE(nc.sbuf_tensor("gsT_s", [128, NT], F32))
        rt = EE(nc.sbuf_tensor("rt", [128, 1024], BF16))
        rtf = EE(nc.sbuf_tensor("rtf", [128, 1024], F32))
        xt = EE(nc.sbuf_tensor("xt", [128, 1024], F32))
        ot = EE(nc.sbuf_tensor("ot", [128, 1024], F32))

        p.dma("y", slotT[:, :], slot_in)
        p.dma("y", gsT[:, :], gs_in)
        p.op("v", lambda e: e.tensor_copy(slot_u[:, :], slotT[:, :]))
        for t in range(NT):
            p.op("v", lambda e: e.memset(rt[:, :], 0.0))
            p.idma("g", lambda e, t=t: e.indirect_dma_start(
                out=rt[:, :], out_offset=None,
                in_=call_in[:, :],
                in_offset=IOA(slot_u[:, t:t + 1], 0),
                bounds_check=NCORE * T - 1, oob_is_err=False))
            p.op("v", lambda e, t=t: e.tensor_scalar_mul(rtf[:, :], rt[:, :],
                                                         gsT[:, t:t + 1]))
            p.dma("g", xt[:, :], xp_in[t * 128:(t + 1) * 128, :], ch="xl")
            p.op("v", lambda e: e.tensor_add(ot[:, :], xt[:, :], rtf[:, :]))
            p.dma("g", out[t * 128:(t + 1) * 128, :], ot[:, :], ch="ol")

        with nc.Block() as block:
            p.build(block)
    return nc


_NC_CACHE = {}


def _get(name, builder):
    if name not in _NC_CACHE:
        _NC_CACHE[name] = builder()
    return _NC_CACHE[name]


def kernel(L0_attn_x, L0_attn_Wq, L0_attn_Wqr, L0_attn_Wkr, L0_attn_Wkvd,
           L0_attn_Wku, L0_attn_Wvu, L0_attn_Wo, L0_attn_lobo_log,
           L0_ffn_Wg, L0_ffn_w1, L0_ffn_w2):
    x = np.ascontiguousarray(np.asarray(L0_attn_x, dtype=np.float32))
    wq = np.ascontiguousarray(np.asarray(L0_attn_Wq, dtype=np.float32))
    wqr = np.ascontiguousarray(np.asarray(L0_attn_Wqr, dtype=np.float32))
    wkr = np.ascontiguousarray(np.asarray(L0_attn_Wkr, dtype=np.float32))
    wkvd = np.ascontiguousarray(np.asarray(L0_attn_Wkvd, dtype=np.float32))
    wku = np.ascontiguousarray(np.asarray(L0_attn_Wku, dtype=np.float32))
    wvu = np.ascontiguousarray(np.asarray(L0_attn_Wvu, dtype=np.float32))
    wo = np.ascontiguousarray(np.asarray(L0_attn_Wo, dtype=np.float32))
    lobo = np.ascontiguousarray(np.asarray(L0_attn_lobo_log,
                                           dtype=np.float32)).reshape(1, H)
    wg = np.ascontiguousarray(np.asarray(L0_ffn_Wg, dtype=np.float32))
    w1 = np.ascontiguousarray(np.asarray(L0_ffn_w1, dtype=np.float32))
    w2 = np.ascontiguousarray(np.asarray(L0_ffn_w2, dtype=np.float32))

    jj = np.arange(16, dtype=np.float64)
    inv = 1.0 / (10000.0 ** (jj / 16.0))
    ang = np.arange(T, dtype=np.float64)[:, None] * inv[None, :]
    cosr = np.tile(np.cos(ang).astype(np.float32), (1, 16))
    sinr = np.tile(np.sin(ang).astype(np.float32), (1, 16))

    nc1 = _get("L1", build_L1)
    maps1 = [{"x": x[k], "wq": wq, "wqr": wqr, "wkr": wkr, "wkvd": wkvd,
              "wku": wku, "wvu": wvu, "wo": wo, "lobo": lobo, "wg": wg,
              "cosr": cosr, "sinr": sinr} for k in range(NCORE)]
    r1 = run_bass_kernel_spmd(nc1, maps1, core_ids=list(range(NCORE))).results

    cnts = np.concatenate([r["cnto"] for r in r1], axis=0)
    nc2 = _get("L2", build_L2)
    toki = np.arange(NCORE * T, dtype=np.float32).reshape(NCORE, NT, 128)
    maps2 = []
    for k in range(NCORE):
        maps2.append({
            "cnts": cnts,
            "lloc": r1[k]["lloco"],
            "idxT": np.ascontiguousarray(
                r1[k]["idxo"].reshape(NT, 128).T),
            "tokiT": np.ascontiguousarray(toki[k].transpose(1, 0)),
            "sel": (np.arange(NCORE, dtype=np.float32) < k)
            .astype(np.float32).reshape(NCORE, 1),
        })
    r2 = run_bass_kernel_spmd(nc2, maps2, core_ids=list(range(NCORE))).results

    xall = np.concatenate([r["xln2"] for r in r1], axis=0)
    gparts = np.stack([r2[k]["gpart"][:, 0] for k in range(NCORE)])
    nc3 = _get("L3", build_L3)
    maps3 = []
    for k in range(NCORE):
        gmy = gparts[:, k * T:(k + 1) * T]
        gmy = gmy.reshape(NCORE, NT, 128).transpose(2, 0, 1).reshape(128,
                                                                     NCORE * NT)
        maps3.append({
            "xall": xall,
            "gmy": np.ascontiguousarray(gmy),
            "w1own": w1[2 * k:2 * k + 2],
            "w2own": w2[2 * k:2 * k + 2],
        })
    r3 = run_bass_kernel_spmd(nc3, maps3, core_ids=list(range(NCORE))).results

    call = np.concatenate([r["contrib"] for r in r3], axis=0)
    nc4 = _get("L4", build_L4)
    maps4 = []
    for k in range(NCORE):
        maps4.append({
            "call": call,
            "slotT": np.ascontiguousarray(
                r2[k]["slotp"].reshape(NT, 128).T),
            "gsT": np.ascontiguousarray(
                r1[k]["gso"].reshape(NT, 128).T),
            "xp": r1[k]["xprime"],
        })
    r4 = run_bass_kernel_spmd(nc4, maps4, core_ids=list(range(NCORE))).results

    out = np.stack([r4[k]["out"] for k in range(NCORE)], axis=0)
    return out.astype(np.float32)
